# version 02
# baseline (speedup 1.0000x reference)
import numpy as np
from contextlib import ExitStack

import concourse.bass as bass
import concourse.mybir as mybir
from concourse.bass import ds
from concourse.bass_utils import run_bass_kernel_spmd

F32 = mybir.dt.float32
BF16 = mybir.dt.bfloat16
U32 = mybir.dt.uint32
AF = mybir.ActivationFunctionType
AL = mybir.AluOpType
AX = mybir.AxisListType.X

NCORES = 8
RG = [list(range(NCORES))]

H = 2048
NCH = 16
R = 1024
NH = 16
SQ = 2048
HU = 5632
HUC = 44
SEC = 128 * R
NE = HU * H
QW1 = (H // 8) * HU
QWALL = 3 * QW1
PIECE = 512
NPIECE = (H // 8) * HU // 128 // PIECE
NP3 = 3 * NPIECE
QRING = 4

f32 = np.float32
SM_SCALE = f32(f32(1.0) / np.sqrt(f32(128.0)))
SM_BIAS = f32(-12.0)
INV_H = float(f32(1.0 / H))
LN_EPS = float(f32(1e-5))
NEG1 = -1.0
THR_SCL = float(f32(0.5 / NE))
THR_OFF = float(f32(0.5e-8))


class Ctr:
    def __init__(self, nc, name):
        self.sem = nc.alloc_semaphore(name)
        self.n = 0

    def inc(self, instr, k=1):
        self.n += k
        instr.then_inc(self.sem, k)
        return self.n


def build_program(debug=False):
    nc = bass.Bass("TRN2", target_bir_lowering=False)
    dbg = {}
    if debug:
        dbg["lnx"] = nc.dram_tensor("d_lnx", [128, NCH * R], BF16, kind="ExternalOutput").ap()
        dbg["kvloc"] = nc.dram_tensor("d_kvloc", [2 * NH, SEC], BF16, kind="ExternalOutput").ap()
        dbg["qloc"] = nc.dram_tensor("d_qloc", [NH, SEC], BF16, kind="ExternalOutput").ap()
        dbg["ctxn"] = nc.dram_tensor("d_ctxn", [128, NH * R], BF16, kind="ExternalOutput").ap()
        dbg["attn"] = nc.dram_tensor("d_attn", [128, NCH * R], BF16, kind="ExternalOutput").ap()
        dbg["thr"] = nc.dram_tensor("d_thr", [128, 8], F32, kind="ExternalOutput").ap()
        dbg["qwloc"] = nc.dram_tensor("d_qwloc", [1, QWALL], BF16, kind="ExternalOutput").ap()
        dbg["x1"] = nc.dram_tensor("d_x1", [128, NCH * R], F32, kind="ExternalOutput").ap()
        dbg["hid"] = nc.dram_tensor("d_hid", [128, HUC * 512], BF16, kind="ExternalOutput").ap()
        dbg["y"] = nc.dram_tensor("d_y", [128, NCH * 512], BF16, kind="ExternalOutput").ap()
        dbg["parts"] = nc.dram_tensor("d_parts", [128, NP3 + 2], F32, kind="ExternalOutput").ap()
        dbg["q3sl"] = nc.dram_tensor("d_q3sl", [128, 2 * HUC * 128], BF16, kind="ExternalOutput").ap()
        dbg["p3"] = nc.dram_tensor("d_p3", [128, 4], F32, kind="ExternalOutput").ap()
        dbg["ws"] = nc.dram_tensor("d_ws", [1, 16], F32, kind="ExternalOutput").ap()
        dbg["wsg"] = nc.dram_tensor("d_wsg", [1, 128], F32, kind="ExternalOutput").ap()

    xT = nc.dram_tensor("xT", [H, R], F32, kind="ExternalInput").ap()
    wkqT = nc.dram_tensor("wkqT", [H, 2 * H], F32, kind="ExternalInput").ap()
    wvT = nc.dram_tensor("wvT", [H, H], F32, kind="ExternalInput").ap()
    wpTt = nc.dram_tensor("wpT", [H, H], F32, kind="ExternalInput").ap()
    bpj = nc.dram_tensor("bpj", [1, H], F32, kind="ExternalInput").ap()
    w1s = nc.dram_tensor("w1s", [H // 8, HU], F32, kind="ExternalInput").ap()
    w2s = nc.dram_tensor("w2s", [H // 8, HU], F32, kind="ExternalInput").ap()
    w3s = nc.dram_tensor("w3s", [HU, H // 8], F32, kind="ExternalInput").ap()
    ctl = nc.dram_tensor("ctl", [1, 128], U32, kind="ExternalInput").ap()
    yT = nc.dram_tensor("yT", [H, R], F32, kind="ExternalOutput").ap()

    kv_loc = nc.dram_tensor("kv_loc", [32, 128, R], BF16)
    q_loc = nc.dram_tensor("q_loc", [16, 128, R], BF16)
    ws_loc = nc.dram_tensor("ws_loc", [1, 16], F32)
    qw_loc = nc.dram_tensor("qw_loc", [1, QWALL], BF16)
    kv_g = nc.dram_tensor("kv_g", [NCORES * 4, 8, 128, R], BF16)
    q_g = nc.dram_tensor("q_g", [NCORES * 4, 4, 128, R], BF16)
    kv_st = nc.dram_tensor("kv_st", [NCORES, 8, 128, R], BF16)
    q_st = nc.dram_tensor("q_st", [4, 4, 128, R], BF16)
    ws_g = nc.dram_tensor("ws_g", [NCORES, 16], F32)
    qw_g = nc.dram_tensor("qw_g", [NCORES, QWALL], BF16)

    xT_sb = nc.alloc_sbuf_tensor("xT_sb", [128, NCH * R], F32)
    rows = nc.alloc_sbuf_tensor("rows", [1, 5 * R], F32)
    ROW_M, ROW_RSQ, ROW_B, ROW_T, ROW_RSQ3 = 0, 1, 2, 3, 4
    ROW_B3 = ROW_B
    xb_sb = nc.alloc_sbuf_tensor("xb_sb", [128, 2 * R], BF16)
    sq_sb = nc.alloc_sbuf_tensor("sq_sb", [128, 2 * R], BF16)
    ln_tmp = nc.alloc_sbuf_tensor("ln_tmp", [128, R], F32)
    ones_cb = nc.alloc_sbuf_tensor("ones_cb", [128, 16], BF16)
    ones_cf = nc.alloc_sbuf_tensor("ones_cf", [128, 16], F32)
    ones_rf = nc.alloc_sbuf_tensor("ones_rf", [1, 128], F32)
    ones_rb = nc.alloc_sbuf_tensor("ones_rb", [1, R], BF16)
    bprojB = nc.alloc_sbuf_tensor("bprojB", [1, H], BF16)
    ctl_sb = nc.alloc_sbuf_tensor("ctl_sb", [1, 128], U32)
    thr_sb = nc.alloc_sbuf_tensor("thr_sb", [128, 8], F32)
    thrr_sb = nc.alloc_sbuf_tensor("thrr_sb", [1, 16], F32)
    wsg_sb = nc.alloc_sbuf_tensor("wsg_sb", [1, 128], F32)
    parts_sb = nc.alloc_sbuf_tensor("parts_sb", [128, NP3 + 2], F32)
    p3_sb = nc.alloc_sbuf_tensor("p3_sb", [128, 4], F32)
    ws_sb = nc.alloc_sbuf_tensor("ws_sb", [1, 16], F32)

    def row(i):
        return rows[0:1, i * R:(i + 1) * R]

    def rowh(i, h, w=512):
        return rows[0:1, i * R + h * w: i * R + (h + 1) * w]

    es_qu = ExitStack()
    wq_sb = es_qu.enter_context(nc.sbuf_tensor("wq_sb", [128, QRING * PIECE], F32))
    qa_sb = es_qu.enter_context(nc.sbuf_tensor("qa_sb", [128, 2 * PIECE], BF16))
    qb_sb = es_qu.enter_context(nc.sbuf_tensor("qb_sb", [128, 2 * PIECE], BF16))
    qp_sb = es_qu.enter_context(nc.sbuf_tensor("qp_sb", [128, 2 * PIECE], BF16))

    def wqsl(j):
        return wq_sb[:, (j % QRING) * PIECE:(j % QRING + 1) * PIECE]

    def ring2(buf, j, w):
        return buf[:, (j % 2) * w:(j % 2 + 1) * w]

    sX = Ctr(nc, "sX")
    sWS = Ctr(nc, "sWS")
    sQ = [Ctr(nc, f"sQ{i}") for i in range(4)]
    sP = [Ctr(nc, f"sP{i}") for i in range(2)]
    sW2 = [Ctr(nc, f"sW2{i}") for i in range(2)]
    sE = [Ctr(nc, f"sE{i}") for i in range(2)]
    sVW = [Ctr(nc, f"sVW{i}") for i in range(2)]
    sGL = [Ctr(nc, f"sGL{i}") for i in range(2)]
    sST = Ctr(nc, "sST")
    sF = [Ctr(nc, f"sF{i}") for i in range(2)]
    sQ3 = [Ctr(nc, f"sQ3{i}") for i in range(2)]
    sG = Ctr(nc, "sG")
    sC = Ctr(nc, "sC")
    sT = Ctr(nc, "sT")
    sV = Ctr(nc, "sV")
    sS = Ctr(nc, "sS")
    M = {}

    def slab_sem(s):
        return sW2[s % 2], 16 * (s // 2 + 1)

    w1v = w1s.rearrange("(c p) h -> c p h", p=128)
    w2v = w2s.rearrange("(c p) h -> c p h", p=128)
    w3v = w3s.rearrange("(c p) o -> c p o", p=128)
    WSRC = [w1v, w2v, w3v]

    def wpiece(j):
        m, pj = divmod(j, NPIECE)
        if m < 2:
            c, h0 = divmod(pj * PIECE, HU)
            return WSRC[m][c:c + 1, :, h0:h0 + PIECE].squeeze(0)
        cb = 2 * pj
        return w3v[cb:cb + 2, :, :].transpose([1, 0, 2])

    def wq_dst(j):
        if j // NPIECE < 2:
            return wqsl(j)
        return wqsl(j).rearrange("p (c o) -> p c o", c=2)

    kvv = kv_loc.ap()
    qvv = q_loc.ap()
    kvg4 = kv_g.ap()
    qg4 = q_g.ap()
    qwgv = qw_g.ap()

    es_p1 = ExitStack()
    lnx_sb = es_p1.enter_context(nc.sbuf_tensor("lnx_sb", [128, NCH * R], BF16))
    wsl_sb = es_p1.enter_context(nc.sbuf_tensor("wsl_sb", [128, 2 * NCH * 512], BF16))
    evq_sb = es_p1.enter_context(nc.sbuf_tensor("evq_sb", [128, 2 * R], BF16))
    evv_sb = es_p1.enter_context(nc.sbuf_tensor("evv_sb", [128, 2 * 512], BF16))

    def wsl(par):
        return wsl_sb[:, par * NCH * 512:(par + 1) * NCH * 512]

    es_ln1p = ExitStack()
    ss_ps = es_ln1p.enter_context(nc.psum_tensor("ss_ps", [1, R], F32))
    qq_ps = es_ln1p.enter_context(nc.psum_tensor("qq_ps", [1, R], F32))
    a_ps = es_ln1p.enter_context(nc.psum_tensor("a_ps", [128, 512], F32))
    b_ps = es_ln1p.enter_context(nc.psum_tensor("b_ps", [128, 512], F32))

    with nc.Block() as block:
        def s1(sync):
            sync.dma_start(out=ctl_sb[:, :], in_=ctl[:, :]).then_inc(sX.sem, 16)
            sX.n += 16
            sync.dma_start(
                out=xT_sb[:, :].rearrange("p (c r) -> p c r", c=NCH),
                in_=xT.rearrange("(c p) r -> p c r", p=128),
            ).then_inc(sX.sem, 16)
            sX.n += 16
            M["x_loaded"] = sX.n
        block.sync(s1)

        M["consts"] = 1
        for c in range(NCH):
            M[f"cast_{c}"] = 1 + 5 * c + 1
            for k in range(4):
                M[f"absred_{4 * c + k}"] = 1 + 5 * c + 2 + k
        M["rows1"] = 82
        M["rowsb"] = 83
        for h in range(2):
            for c in range(NCH):
                M[f"lnx_{h}_{c}"] = 83 + h * NCH + c + 1
        M["absred_64"] = 116
        M["absred_65"] = 117
        M["p3"] = 118

        def s2(sync):
            for j in range(NP3):
                if j >= QRING:
                    sync.wait_ge(sV.sem, M[f"absred_{j - QRING}"])
                sl = sQ[j % QRING]
                sync.dma_start(out=wq_dst(j), in_=wpiece(j)).then_inc(sl.sem, 16)
                sl.n += 16
                M[f"wq1_{j}"] = (j % QRING, sl.n)
        block.sync(s2)

        def v1(vector):
            vector.memset(ones_cb[:, :], 1.0)
            vector.memset(ones_cf[:, :], 1.0)
            vector.memset(ones_rf[:, :], 1.0)
            vector.memset(ones_rb[:, :], 1.0)
            vector.memset(p3_sb[:, :], 0.0)
            vector.memset(ws_sb[:, :], 0.0)
            sV.inc(vector.memset(parts_sb[:, :], 0.0))
            assert sV.n == M["consts"]
        block.vector(v1)

        def a1(scalar):
            scalar.wait_ge(sX.sem, M["x_loaded"])
            for c in range(NCH):
                if c >= 2:
                    scalar.wait_ge(sT.sem, c - 1)
                sS.inc(scalar.activation(
                    out=ring2(sq_sb, c, R), in_=xT_sb[:, c * R:(c + 1) * R],
                    func=AF.Square))
                M[f"sq1_{c}"] = sS.n
        block.scalar(a1)

        def v2(vector):
            vector.wait_ge(sX.sem, M["x_loaded"])
            for c in range(NCH):
                if c >= 2:
                    vector.wait_ge(sT.sem, c - 1)
                sV.inc(vector.tensor_copy(ring2(xb_sb, c, R),
                                          xT_sb[:, c * R:(c + 1) * R]))
                assert sV.n == M[f"cast_{c}"]
                for k in range(4):
                    j = 4 * c + k
                    slot, cnt = M[f"wq1_{j}"]
                    vector.wait_ge(sQ[slot].sem, cnt)
                    sV.inc(vector.tensor_reduce(
                        parts_sb[:, j:j + 1], wqsl(j), axis=AX, op=AL.add,
                        apply_absolute_value=True))
                    assert sV.n == M[f"absred_{j}"]
        block.vector(v2)

        def t1(tensor):
            assert sT.n == 0
            for c in range(NCH):
                tensor.wait_ge(sV.sem, M[f"cast_{c}"])
                tensor.wait_ge(sS.sem, M[f"sq1_{c}"])
                xb = ring2(xb_sb, c, R)
                sq = ring2(sq_sb, c, R)
                st, sp = (c == 0), (c == NCH - 1)
                tensor.matmul(ss_ps[0:1, 0:512], ones_cb[:, 0:1], xb[:, 0:512],
                              start=st, stop=sp)
                tensor.matmul(ss_ps[0:1, 512:1024], ones_cb[:, 0:1],
                              xb[:, 512:1024], start=st, stop=sp)
                tensor.matmul(qq_ps[0:1, 0:512], ones_cb[:, 0:1], sq[:, 0:512],
                              start=st, stop=sp)
                ins = tensor.matmul(qq_ps[0:1, 512:1024], ones_cb[:, 0:1],
                                    sq[:, 512:1024], start=st, stop=sp)
                sT.inc(ins)
            M["ln1_stats"] = sT.n
        block.tensor(t1)

        def rows_part1(vector, ssp, qqp, mark):
            vector.tensor_scalar(row(ROW_M), ssp[0:1, :], INV_H, None, AL.mult)
            vector.tensor_tensor(row(ROW_T), row(ROW_M), row(ROW_M), AL.mult)
            vector.scalar_tensor_tensor(row(ROW_T), qqp[0:1, :], INV_H,
                                        row(ROW_T), AL.mult, AL.subtract)
            vector.tensor_scalar(row(ROW_T), row(ROW_T), LN_EPS, None, AL.add)
            sV.inc(vector.reciprocal(row(ROW_RSQ), row(ROW_T)))
            M[mark] = sV.n

        def rows_part1_w(vector, ssp, qqp, mark, w):
            vector.tensor_scalar(rows[0:1, ROW_M * R:ROW_M * R + w],
                                 ssp[0:1, :], INV_H, None, AL.mult)
            rm = rows[0:1, ROW_M * R:ROW_M * R + w]
            rt = rows[0:1, ROW_T * R:ROW_T * R + w]
            vector.tensor_tensor(rt, rm, rm, AL.mult)
            vector.scalar_tensor_tensor(rt, qqp[0:1, :], INV_H, rt,
                                        AL.mult, AL.subtract)
            vector.tensor_scalar(rt, rt, LN_EPS, None, AL.add)
            sV.inc(vector.reciprocal(
                rows[0:1, ROW_RSQ * R:ROW_RSQ * R + w], rt))
            M[mark] = sV.n

        def v3(vector):
            vector.wait_ge(sT.sem, M["ln1_stats"])
            rows_part1(vector, ss_ps, qq_ps, "rows1")
            assert sV.n == M["rows1"]
        block.vector(v3)

        def a2(scalar):
            scalar.wait_ge(sV.sem, M["rows1"])
            sS.inc(scalar.sqrt(row(ROW_RSQ), row(ROW_RSQ)))
            M["sqrt1"] = sS.n
        block.scalar(a2)

        def v4(vector):
            vector.wait_ge(sS.sem, M["sqrt1"])
            sV.inc(vector.scalar_tensor_tensor(row(ROW_B), row(ROW_M), NEG1,
                                               row(ROW_RSQ), AL.mult, AL.mult))
            assert sV.n == M["rowsb"]
        block.vector(v4)

        for h in range(2):
            def t2(tensor, h=h):
                if h == 0:
                    tensor.wait_ge(sV.sem, M["rowsb"])
                else:
                    tensor.wait_ge(sV.sem, M[f"lnx_0_{NCH - 1}"])
                tensor.matmul(a_ps[:, :], ones_rf[0:1, :],
                              rowh(ROW_RSQ, h), start=True, stop=True)
                ins = tensor.matmul(b_ps[:, :], ones_rf[0:1, :],
                                    rowh(ROW_B, h), start=True, stop=True)
                sT.inc(ins)
                M[f"ln1_bc_{h}"] = sT.n
            block.tensor(t2)

            def v5a(vector, h=h):
                vector.wait_ge(sT.sem, M[f"ln1_bc_{h}"])
                for c in range(NCH):
                    xs = xT_sb[:, c * R + h * 512: c * R + (h + 1) * 512]
                    vector.tensor_tensor(ln_tmp[:, 0:512], xs, a_ps[:, :], AL.mult)
                    sV.inc(vector.tensor_tensor(
                        lnx_sb[:, c * R + h * 512: c * R + (h + 1) * 512],
                        ln_tmp[:, 0:512], b_ps[:, :], AL.add))
                    assert sV.n == M[f"lnx_{h}_{c}"]
            block.vector(v5a)

        def v5b(vector):
            for j in (64, 65):
                slot, cnt = M[f"wq1_{j}"]
                vector.wait_ge(sQ[slot].sem, cnt)
                sV.inc(vector.tensor_reduce(
                    parts_sb[:, j:j + 1], wqsl(j), axis=AX, op=AL.add,
                    apply_absolute_value=True))
                assert sV.n == M[f"absred_{j}"]
            for m in range(3):
                vector.tensor_reduce(p3_sb[:, m:m + 1],
                                     parts_sb[:, m * NPIECE:(m + 1) * NPIECE],
                                     axis=AX, op=AL.add)
            sV.inc(vector.engine_nop())
            assert sV.n == M["p3"]
        block.vector(v5b)

        es_ln1p.close()
        es_qkvp = ExitStack()
        pqk = [es_qkvp.enter_context(nc.psum_tensor(f"pqk{i}", [128, R], F32))
               for i in range(2)]
        pv = [es_qkvp.enter_context(nc.psum_tensor(f"pv{i}", [128, 512], F32))
              for i in range(2)]
        pr_ps = es_qkvp.enter_context(nc.psum_tensor("pr_ps", [1, 16], F32))

        EVB = M["p3"]
        M["ws_evac"] = EVB + 1

        def evK(oc):
            return EVB + 1 + oc + 1

        def evV(idx):
            return EVB + 17 + idx + 1

        def evQ(oc):
            return EVB + 49 + oc + 1

        _e = [sE[0].n, sE[1].n]
        _v = [sVW[0].n, sVW[1].n]
        WRK, WRV, WRQ = {}, {}, {}
        for _oc in range(NH):
            _e[_oc % 2] += 16
            WRK[_oc] = (_oc % 2, _e[_oc % 2])
        for _idx in range(32):
            _v[_idx % 2] += 16
            WRV[_idx] = (_idx % 2, _v[_idx % 2])
        for _oc in range(NH):
            _e[_oc % 2] += 16
            WRQ[_oc] = (_oc % 2, _e[_oc % 2])

        T_QKV = {}

        def t3(tensor):
            tensor.wait_ge(sV.sem, M["p3"])
            sT.inc(tensor.matmul(pr_ps[0:1, 0:4], ones_cf[:, 0:1], p3_sb[:, 0:4],
                                 start=True, stop=True))
            M["pmm"] = sT.n
            for oc in range(NH):
                s = oc // 4
                _sm, _c = slab_sem(s)
                tensor.wait_ge(_sm.sem, _c)
                if oc == 0:
                    tensor.wait_ge(sV.sem, M[f"lnx_1_{NCH - 1}"])
                if oc >= 2:
                    tensor.wait_ge(sV.sem, evK(oc - 2))
                tgt = pqk[oc % 2]
                for hc in range(NCH):
                    lhsT = wsl(s % 2)[:, hc * 512 + (oc % 4) * 128:
                                      hc * 512 + (oc % 4) * 128 + 128]
                    st, sp = (hc == 0), (hc == NCH - 1)
                    tensor.matmul(tgt[:, 0:512], lhsT,
                                  lnx_sb[:, hc * R: hc * R + 512],
                                  start=st, stop=sp)
                    ins = tensor.matmul(tgt[:, 512:1024], lhsT,
                                        lnx_sb[:, hc * R + 512: hc * R + 1024],
                                        start=st, stop=sp)
                sT.inc(ins)
                T_QKV[("K", oc)] = sT.n
            for vs in range(4):
                s = 4 + vs
                for rsub in range(8):
                    idx = vs * 8 + rsub
                    _sm, _c = slab_sem(s)
                    tensor.wait_ge(_sm.sem, _c)
                    if idx >= 2:
                        tensor.wait_ge(sV.sem, evV(idx - 2))
                    tgt = pv[idx % 2]
                    for hc in range(NCH):
                        lhsT = lnx_sb[:, hc * R + rsub * 128:
                                      hc * R + rsub * 128 + 128]
                        rhs = wsl(s % 2)[:, hc * 512:(hc + 1) * 512]
                        ins = tensor.matmul(tgt[:, :], lhsT, rhs,
                                            start=(hc == 0), stop=(hc == NCH - 1))
                    sT.inc(ins)
                    T_QKV[("V", idx)] = sT.n
            for oc in range(NH):
                s = 8 + oc // 4
                _sm, _c = slab_sem(s)
                tensor.wait_ge(_sm.sem, _c)
                if oc >= 2:
                    tensor.wait_ge(sV.sem, evQ(oc - 2))
                tgt = pqk[oc % 2]
                for hc in range(NCH):
                    lhsT = wsl(s % 2)[:, hc * 512 + (oc % 4) * 128:
                                      hc * 512 + (oc % 4) * 128 + 128]
                    st, sp = (hc == 0), (hc == NCH - 1)
                    tensor.matmul(tgt[:, 0:512], lhsT,
                                  lnx_sb[:, hc * R: hc * R + 512],
                                  start=st, stop=sp)
                    ins = tensor.matmul(tgt[:, 512:1024], lhsT,
                                        lnx_sb[:, hc * R + 512: hc * R + 1024],
                                        start=st, stop=sp)
                sT.inc(ins)
                T_QKV[("Q", oc)] = sT.n
        block.tensor(t3)

        def v6(vector):
            assert sV.n == EVB
            vector.wait_ge(sT.sem, M["pmm"])
            sV.inc(vector.tensor_copy(ws_sb[0:1, 0:4], pr_ps[0:1, 0:4]))
            assert sV.n == M["ws_evac"]
            for oc in range(NH):
                vector.wait_ge(sT.sem, T_QKV[("K", oc)])
                if oc >= 2:
                    _s, _c = WRK[oc - 2]
                    vector.wait_ge(sE[_s].sem, _c)
                sV.inc(vector.tensor_copy(ring2(evq_sb, oc, R), pqk[oc % 2][:, :]))
                assert sV.n == evK(oc)
            for idx in range(32):
                vector.wait_ge(sT.sem, T_QKV[("V", idx)])
                if idx >= 2:
                    _s, _c = WRV[idx - 2]
                    vector.wait_ge(sVW[_s].sem, _c)
                sV.inc(vector.tensor_copy(ring2(evv_sb, idx, 512),
                                          pv[idx % 2][:, :]))
                assert sV.n == evV(idx)
            for oc in range(NH):
                vector.wait_ge(sT.sem, T_QKV[("Q", oc)])
                if oc >= 2:
                    _s, _c = WRQ[oc - 2]
                    vector.wait_ge(sE[_s].sem, _c)
                sV.inc(vector.tensor_copy(ring2(evq_sb, oc, R), pqk[oc % 2][:, :]))
                assert sV.n == evQ(oc)
        block.vector(v6)

        def s3(sync):
            sync.wait_ge(sV.sem, M["ws_evac"])
            sync.dma_start(out=ws_loc.ap()[:, :],
                           in_=ws_sb[:, :]).then_inc(sWS.sem, 16)
            sWS.n += 16
            M["ws_store"] = sWS.n
            for oc in range(NH):
                sync.wait_ge(sV.sem, evK(oc))
                krow = (oc % 4) * 8 + (oc // 4) * 2
                _s, _c = WRK[oc]
                sync.dma_start(
                    out=kvv[krow, :, :],
                    in_=ring2(evq_sb, oc, R),
                ).then_inc(sE[_s].sem, 16)
                sE[_s].n += 16
                assert sE[_s].n == _c
            for idx in range(32):
                vs, rsub = divmod(idx, 8)
                sync.wait_ge(sV.sem, evV(idx))
                _s, _c = WRV[idx]
                sync.dma_start(
                    out=kvv.rearrange("(i c) p s -> i c p s", i=4)[
                        :, vs * 2 + 1, :, rsub * 128:(rsub + 1) * 128
                    ].transpose([1, 0, 2]),
                    in_=ring2(evv_sb, idx, 512).rearrange("p (n d) -> p n d", n=4),
                ).then_inc(sVW[_s].sem, 16)
                sVW[_s].n += 16
                assert sVW[_s].n == _c
            for oc in range(NH):
                sync.wait_ge(sV.sem, evQ(oc))
                qrow = (oc % 4) * 4 + (oc // 4)
                _s, _c = WRQ[oc]
                sync.dma_start(
                    out=qvv[qrow, :, :],
                    in_=ring2(evq_sb, oc, R),
                ).then_inc(sE[_s].sem, 16)
                sE[_s].n += 16
                assert sE[_s].n == _c
        block.sync(s3)

        def g1(gpsimd):
            slab_last = {}
            for s in range(4):
                slab_last[s] = T_QKV[("K", s * 4 + 3)]
                slab_last[4 + s] = T_QKV[("V", s * 8 + 7)]
                slab_last[8 + s] = T_QKV[("Q", s * 4 + 3)]
            for s in range(12):
                if s >= 2:
                    gpsimd.wait_ge(sT.sem, slab_last[s - 2])
                if s < 4:
                    src = wkqT[:, s * 512:(s + 1) * 512]
                elif s < 8:
                    src = wvT[:, (s - 4) * 512:(s - 3) * 512]
                else:
                    src = wkqT[:, 2048 + (s - 8) * 512: 2048 + (s - 7) * 512]
                _sm, _c = slab_sem(s)
                gpsimd.dma_start(
                    out=wsl(s % 2).rearrange("p (c o) -> p c o", c=NCH),
                    in_=src.rearrange("(c p) o -> p c o", p=128),
                ).then_inc(_sm.sem, 16)
                _sm.n += 16
                assert _sm.n == _c
            _sm, _c = slab_sem(12)
            gpsimd.dma_start(out=bprojB[:, :], in_=bpj[:, :]).then_inc(_sm.sem, 16)
            _sm.n += 16
            assert _sm.n == _c
        block.gpsimd(g1)

        def g2(gpsimd):
            gpsimd.wait_ge(sWS.sem, M["ws_store"])
            gpsimd.collective_compute(
                "AllGather", AL.bypass, replica_groups=RG,
                ins=[ws_loc.ap().opt()], outs=[ws_g.ap().opt()],
            ).then_inc(sC.sem, 1)
            sC.n += 1
            M["ag_ws"] = sC.n
            gpsimd.wait_ge(sE[0].sem, WRK[14][1])
            gpsimd.wait_ge(sE[1].sem, WRK[15][1])
            gpsimd.wait_ge(sVW[0].sem, WRV[30][1])
            gpsimd.wait_ge(sVW[1].sem, WRV[31][1])
            gpsimd.collective_compute(
                "AllGather", AL.bypass, replica_groups=RG,
                ins=[kv_loc.ap().opt()], outs=[kv_g.ap().opt()],
            ).then_inc(sC.sem, 1)
            sC.n += 1
            M["ag_kv"] = sC.n
            gpsimd.wait_ge(sE[0].sem, WRQ[14][1])
            gpsimd.wait_ge(sE[1].sem, WRQ[15][1])
            gpsimd.collective_compute(
                "AllGather", AL.bypass, replica_groups=RG,
                ins=[q_loc.ap().opt()], outs=[q_g.ap().opt()],
            ).then_inc(sC.sem, 1)
            sC.n += 1
            M["ag_q"] = sC.n
            if debug:
                gpsimd.dma_start(out=dbg["parts"][:, :], in_=parts_sb[:, :]).then_inc(sG.sem, 16)
                sG.n += 16
                gpsimd.dma_start(out=dbg["p3"][:, :], in_=p3_sb[:, :]).then_inc(sG.sem, 16)
                sG.n += 16
                gpsimd.dma_start(out=dbg["ws"][:, :], in_=ws_sb[:, :]).then_inc(sG.sem, 16)
                sG.n += 16
                gpsimd.dma_start(out=dbg["lnx"][:, :], in_=lnx_sb[:, :]).then_inc(sG.sem, 16)
                sG.n += 16
                gpsimd.dma_start(out=dbg["kvloc"][:, :], in_=kv_loc.ap()[:, :]).then_inc(sG.sem, 16)
                sG.n += 16
                gpsimd.dma_start(out=dbg["qloc"][:, :], in_=q_loc.ap()[:, :]).then_inc(sG.sem, 16)
                sG.n += 16
        block.gpsimd(g2)

        def s4(sync):
            sync.wait_ge(sC.sem, M["ag_ws"])
            sync.dma_start(out=wsg_sb[:, :],
                           in_=ws_g.ap().flatten().unsqueeze(0),
                           ).then_inc(sWS.sem, 16)
            sWS.n += 16
            M["wsg_loaded"] = sWS.n
        block.sync(s4)

        def v7(vector):
            vector.wait_ge(sWS.sem, M["wsg_loaded"])
            vector.tensor_reduce(
                ws_sb[0:1, :],
                wsg_sb[0:1, :].rearrange("a (r j) -> a j r", r=NCORES),
                axis=AX, op=AL.add)
            vector.tensor_copy(thrr_sb[0:1, 8:16], wsg_sb[0:1, 0:8])
            vector.tensor_copy(thrr_sb[0:1, 8:16], wsg_sb[0:1, 8:16])
            vector.tensor_scalar(thrr_sb[0:1, 0:4], ws_sb[0:1, 0:4],
                                 THR_SCL, THR_OFF, AL.mult, AL.add)
            sV.inc(vector.tensor_scalar(thrr_sb[0:1, 4:8], ws_sb[0:1, 0:4],
                                        -THR_SCL, -THR_OFF, AL.mult, AL.add))
            M["thr_rows"] = sV.n
        block.vector(v7)

        es_qkvp.close()
        es_attp = ExitStack()
        sc_ps = [es_attp.enter_context(nc.psum_tensor(f"sc{i}", [128, R], F32))
                 for i in range(2)]
        ctx_ps = es_attp.enter_context(nc.psum_tensor("ctx_ps", [128, R], F32))
        tail_ps = es_attp.enter_context(nc.psum_tensor("tail_ps", [128, R], F32))

        es_p1.close()
        es_p2 = ExitStack()
        ctxn_sb = es_p2.enter_context(nc.sbuf_tensor("ctxn_sb", [128, NH * R], BF16))
        wpsl_sb = es_p2.enter_context(nc.sbuf_tensor("wpsl_sb", [128, 2 * NCH * 256], BF16))
        es_att = ExitStack()
        kt_sb = es_att.enter_context(nc.sbuf_tensor("kt_sb", [128, 2 * SQ], BF16))
        vv_sb = es_att.enter_context(nc.sbuf_tensor("vv_sb", [128, 2 * SQ], BF16))
        qq_sb = es_att.enter_context(nc.sbuf_tensor("qq_sb", [128, 2 * R], BF16))
        pt_sb = es_att.enter_context(nc.sbuf_tensor("pt_sb", [128, 2 * R], BF16))
        run_sb = es_att.enter_context(nc.sbuf_tensor("run_sb", [128, R], BF16))
        rs_sb = es_att.enter_context(nc.sbuf_tensor("rs_sb", [128, R], F32))

        def ld_g(g):
            return sGL[g % 2], 80 * (g // 2 + 1)

        S0 = sS.n

        def ES(p):
            return S0 + p + 1

        V0 = sV.n
        M["thr_copy"] = V0 + 1

        def VS(p):
            g, t = divmod(p, 16)
            return V0 + 1 + 18 * g + t + 1

        def VR(g):
            return V0 + 1 + 18 * g + 17

        def VN(g):
            return V0 + 1 + 18 * g + 18

        TS, TC, TR, TB, TP = {}, {}, {}, {}, {}
        EV2B = V0 + 1 + 18 * 16

        def EV2(oc):
            return EV2B + oc + 1

        def t4(tensor):
            tensor.wait_ge(sV.sem, M["thr_rows"])
            tensor.wait_ge(sV.sem, evQ(15))
            sT.inc(tensor.matmul(tail_ps[:, 0:8], ones_rf[0:1, :],
                                 thrr_sb[0:1, 0:8], start=True, stop=True))
            M["thr_bc"] = sT.n
            for g in range(NH):
                par = g % 2
                kt = kt_sb[:, par * SQ:(par + 1) * SQ]
                vv = vv_sb[:, par * SQ:(par + 1) * SQ]
                qv = qq_sb[:, par * R:(par + 1) * R]
                for t in range(16):
                    p = 16 * g + t
                    if t == 0:
                        _sm, _c = ld_g(g)
                        tensor.wait_ge(_sm.sem, _c)
                    if p == 0:
                        tensor.wait_ge(sV.sem, M["thr_copy"])
                    if p >= 2:
                        tensor.wait_ge(sS.sem, ES(p - 2))
                    sc = sc_ps[t % 2]
                    lhsT = kt[:, t * 128:(t + 1) * 128]
                    tensor.matmul(sc[:, 0:512], lhsT, qv[:, 0:512],
                                  start=True, stop=True)
                    sT.inc(tensor.matmul(sc[:, 512:1024], lhsT, qv[:, 512:1024],
                                         start=True, stop=True))
                    TS[p] = sT.n
                    if t >= 1:
                        tp = t - 1
                        pp_ = 16 * g + tp
                        tensor.wait_ge(sS.sem, ES(pp_))
                        if tp == 0:
                            _sm, _c = ld_g(g)
                            tensor.wait_ge(_sm.sem, _c)
                            if g >= 1:
                                tensor.wait_ge(sV.sem, VN(g - 1))
                        pt = ring2(pt_sb, tp, R)
                        lv = vv[:, tp * 128:(tp + 1) * 128]
                        st = (tp == 0)
                        tensor.matmul(ctx_ps[:, 0:512], lv, pt[:, 0:512],
                                      start=st, stop=False)
                        sT.inc(tensor.matmul(ctx_ps[:, 512:1024], lv,
                                             pt[:, 512:1024], start=st, stop=False))
                        TC[pp_] = sT.n
                p15 = 16 * g + 15
                tensor.wait_ge(sS.sem, ES(p15))
                pt = ring2(pt_sb, 15, R)
                lv = vv[:, 15 * 128:16 * 128]
                tensor.matmul(ctx_ps[:, 0:512], lv, pt[:, 0:512],
                              start=False, stop=True)
                sT.inc(tensor.matmul(ctx_ps[:, 512:1024], lv, pt[:, 512:1024],
                                     start=False, stop=True))
                TC[p15] = sT.n
                tensor.wait_ge(sV.sem, VS(p15))
                tensor.matmul(tail_ps[0:1, 0:512], ones_cb[:, 0:1],
                              run_sb[:, 0:512], start=True, stop=True)
                sT.inc(tensor.matmul(tail_ps[0:1, 512:1024], ones_cb[:, 0:1],
                                     run_sb[:, 512:1024], start=True, stop=True))
                TR[g] = sT.n
                tensor.wait_ge(sV.sem, VR(g))
                tensor.matmul(tail_ps[:, 0:512], ones_rf[0:1, :],
                              row(ROW_T)[0:1, 0:512], start=True, stop=True)
                sT.inc(tensor.matmul(tail_ps[:, 512:1024], ones_rf[0:1, :],
                                     row(ROW_T)[0:1, 512:1024],
                                     start=True, stop=True))
                TB[g] = sT.n
            for oc in range(NCH):
                _sm, _c = slab_sem(13 + oc // 2)
                tensor.wait_ge(_sm.sem, _c)
                if oc == 0:
                    tensor.wait_ge(sV.sem, VN(NH - 1))
                    tensor.wait_ge(sS.sem, ES(255))
                if oc >= 2:
                    tensor.wait_ge(sV.sem, EV2(oc - 2))
                tgt = sc_ps[oc % 2]
                for np_ in range(NH):
                    lhsT = wpsl_sb[:, ((oc // 2) % 2) * NCH * 256 +
                                   np_ * 256 + (oc % 2) * 128:
                                   ((oc // 2) % 2) * NCH * 256 +
                                   np_ * 256 + (oc % 2) * 128 + 128]
                    st = (np_ == 0)
                    tensor.matmul(tgt[:, 0:512], lhsT,
                                  ctxn_sb[:, np_ * R: np_ * R + 512],
                                  start=st, stop=False)
                    tensor.matmul(tgt[:, 512:1024], lhsT,
                                  ctxn_sb[:, np_ * R + 512: np_ * R + 1024],
                                  start=st, stop=False)
                bl = bprojB[0:1, oc * 128:(oc + 1) * 128]
                tensor.matmul(tgt[:, 0:512], bl, ones_rb[0:1, 0:512],
                              start=False, stop=True)
                sT.inc(tensor.matmul(tgt[:, 512:1024], bl, ones_rb[0:1, 512:1024],
                                     start=False, stop=True))
                TP[oc] = sT.n
        block.tensor(t4)

        def a3(scalar):
            assert sS.n == S0
            for g in range(NH):
                for t in range(16):
                    p = 16 * g + t
                    scalar.wait_ge(sT.sem, TS[p])
                    if p >= 2:
                        scalar.wait_ge(sT.sem, TC[p - 2])
                        scalar.wait_ge(sV.sem, VS(p - 2))
                    sS.inc(scalar.activation(
                        out=ring2(pt_sb, t, R), in_=sc_ps[t % 2][:, :],
                        func=AF.Exp, bias=SM_BIAS, scale=SM_SCALE))
                    assert sS.n == ES(p)
        block.scalar(a3)

        def v8(vector):
            assert sV.n == V0
            vector.wait_ge(sT.sem, M["thr_bc"])
            sV.inc(vector.tensor_copy(thr_sb[:, :], tail_ps[:, 0:8]))
            assert sV.n == M["thr_copy"]
            for g in range(NH):
                for t in range(16):
                    p = 16 * g + t
                    vector.wait_ge(sS.sem, ES(p))
                    if t == 0:
                        sV.inc(vector.tensor_copy(run_sb[:, :], ring2(pt_sb, t, R)))
                    else:
                        sV.inc(vector.tensor_tensor(run_sb[:, :], run_sb[:, :],
                                                    ring2(pt_sb, t, R), AL.add))
                    assert sV.n == VS(p)
                vector.wait_ge(sT.sem, TR[g])
                sV.inc(vector.reciprocal(row(ROW_T), tail_ps[0:1, :]))
                assert sV.n == VR(g)
                vector.wait_ge(sT.sem, TB[g])
                vector.tensor_copy(rs_sb[:, :], tail_ps[:, :])
                sV.inc(vector.tensor_tensor(
                    ctxn_sb[:, g * R:(g + 1) * R], ctx_ps[:, :], rs_sb[:, :],
                    AL.mult))
                assert sV.n == VN(g)
        block.vector(v8)

        def s_att(sync):
            reg = sync.alloc_register("gidx")
            sync.wait_ge(sX.sem, M["x_loaded"])
            sync.wait_ge(sC.sem, M["ag_kv"])
            sync.reg_load(reg, ctl_sb[0:1, 0:1])
            wkv = sync.snap(reg, min_val=0, max_val=3)
            kv2d = kv_g.ap().rearrange("g h p s -> g (h p s)").rearrange(
                "(r c) m -> c r m", c=4)
            sync.dma_start(out=kv_st.ap().rearrange("r h p s -> r (h p s)"),
                           in_=kv2d[ds(wkv, 1), :, :].squeeze(0),
                           ).then_inc(sST.sem, 16)
            sST.n += 16
            sync.wait_ge(sC.sem, M["ag_q"])
            sync.reg_load(reg, ctl_sb[0:1, 1:2])
            qsl = sync.snap(reg, min_val=0, max_val=7)
            q2d = q_g.ap().rearrange("g h p s -> g (h p s)").rearrange(
                "(b z) m -> b z m", z=8)
            sync.dma_start(out=q_st.ap().rearrange("b h p s -> b (h p s)"),
                           in_=q2d[:, ds(qsl, 1), :].squeeze(1),
                           ).then_inc(sST.sem, 16)
            sST.n += 16
            sync.wait_ge(sST.sem, sST.n)
            kvst = kv_st.ap()
            qst = q_st.ap()
            for g in range(NH):
                par = g % 2
                a, b = g // 4, g % 4
                if g >= 2:
                    sync.wait_ge(sT.sem, TC[16 * (g - 2) + 15])
                sl = sGL[par]
                for t in range(2):
                    sync.dma_start(
                        out=kt_sb[:, par * SQ + t * R: par * SQ + (t + 1) * R],
                        in_=kvst[2 * a + t, 2 * b, :, :],
                    ).then_inc(sl.sem, 16)
                    sl.n += 16
                    sync.dma_start(
                        out=vv_sb[:, par * SQ + t * R: par * SQ + (t + 1) * R],
                        in_=kvst[2 * a + t, 2 * b + 1, :, :],
                    ).then_inc(sl.sem, 16)
                    sl.n += 16
                sync.dma_start(
                    out=qq_sb[:, par * R:(par + 1) * R],
                    in_=qst[a, b, :, :],
                ).then_inc(sl.sem, 16)
                sl.n += 16
                assert (sl.n) == ld_g(g)[1]
        block.sync(s_att)

        def g3(gpsimd):
            for k in range(8):
                s = 13 + k
                if k >= 2:
                    gpsimd.wait_ge(sT.sem, TP[2 * (k - 2) + 1])
                _sm, _c = slab_sem(s)
                gpsimd.dma_start(
                    out=wpsl_sb[:, (k % 2) * NCH * 256:(k % 2 + 1) * NCH * 256
                                ].rearrange("p (c o) -> p c o", c=NCH),
                    in_=wpTt[:, k * 256:(k + 1) * 256].rearrange(
                        "(c p) o -> p c o", p=128),
                ).then_inc(_sm.sem, 16)
                _sm.n += 16
                assert _sm.n == _c
        block.gpsimd(g3)

        es_att.close()
        es_p2b = ExitStack()
        attn_sb = es_p2b.enter_context(nc.sbuf_tensor("attn_sb", [128, NCH * R], BF16))

        def v9(vector):
            assert sV.n == EV2B
            for oc in range(NCH):
                vector.wait_ge(sT.sem, TP[oc])
                sV.inc(vector.tensor_copy(attn_sb[:, oc * R:(oc + 1) * R],
                                          sc_ps[oc % 2][:, :]))
                assert sV.n == EV2(oc)
        block.vector(v9)

        ss2 = ctx_ps
        qq2 = tail_ps

        def a5(scalar):
            for c in range(NCH):
                scalar.wait_ge(sV.sem, EV2(c))
                if c >= 2:
                    scalar.wait_ge(sT.sem, M.get(f"st2_{c - 2}", 0))
                sS.inc(scalar.activation(out=ring2(sq_sb, c, R),
                                         in_=attn_sb[:, c * R:(c + 1) * R],
                                         func=AF.Square))
                M[f"sq2_{c}"] = sS.n
        T5B = sT.n
        for c in range(NCH):
            M[f"st2_{c}"] = T5B + c + 1
        block.scalar(a5)

        def t5(tensor):
            assert sT.n == T5B
            for c in range(NCH):
                tensor.wait_ge(sV.sem, EV2(c))
                tensor.wait_ge(sS.sem, M[f"sq2_{c}"])
                if c == 0:
                    tensor.wait_ge(sV.sem, VN(NH - 1))
                ab = attn_sb[:, c * R:(c + 1) * R]
                sq = ring2(sq_sb, c, R)
                st, sp = (c == 0), (c == NCH - 1)
                tensor.matmul(ss2[0:1, 0:512], ones_cb[:, 0:1], ab[:, 0:512],
                              start=st, stop=sp)
                tensor.matmul(ss2[0:1, 512:1024], ones_cb[:, 0:1], ab[:, 512:1024],
                              start=st, stop=sp)
                tensor.matmul(qq2[0:1, 0:512], ones_cb[:, 0:1], sq[:, 0:512],
                              start=st, stop=sp)
                ins = tensor.matmul(qq2[0:1, 512:1024], ones_cb[:, 0:1],
                                    sq[:, 512:1024], start=st, stop=sp)
                sT.inc(ins)
                assert sT.n == M[f"st2_{c}"]
        block.tensor(t5)

        def v10(vector):
            vector.wait_ge(sT.sem, M[f"st2_{NCH - 1}"])
            rows_part1(vector, ss2, qq2, "rows2")
        block.vector(v10)

        def a6(scalar):
            scalar.wait_ge(sV.sem, M["rows2"])
            sS.inc(scalar.sqrt(row(ROW_RSQ), row(ROW_RSQ)))
            M["sqrt2"] = sS.n
        block.scalar(a6)

        m2b = ctx_ps
        a2b = tail_ps

        def t6(tensor):
            tensor.wait_ge(sS.sem, M["sqrt2"])
            tensor.matmul(m2b[:, 0:512], ones_rf[0:1, :], rowh(ROW_M, 0),
                          start=True, stop=True)
            tensor.matmul(m2b[:, 512:1024], ones_rf[0:1, :], rowh(ROW_M, 1),
                          start=True, stop=True)
            tensor.matmul(a2b[:, 0:512], ones_rf[0:1, :], rowh(ROW_RSQ, 0),
                          start=True, stop=True)
            sT.inc(tensor.matmul(a2b[:, 512:1024], ones_rf[0:1, :],
                                 rowh(ROW_RSQ, 1), start=True, stop=True))
            M["bc2"] = sT.n
        block.tensor(t6)

        def v12(vector):
            vector.wait_ge(sT.sem, M["bc2"])
            for c in range(NCH):
                ab = attn_sb[:, c * R:(c + 1) * R]
                xs = xT_sb[:, c * R:(c + 1) * R]
                vector.tensor_tensor(ln_tmp[:, :], ab, m2b[:, :], AL.subtract)
                vector.tensor_tensor(ln_tmp[:, :], ln_tmp[:, :], a2b[:, :], AL.mult)
                sV.inc(vector.tensor_tensor(xs, xs, ln_tmp[:, :], AL.add))
                M[f"xu2_{c}"] = sV.n
            M["x2done"] = sV.n
        block.vector(v12)

        VQ0 = sV.n

        def VQ2(j):
            return VQ0 + 2 * j + 1

        def VQ3(j):
            return VQ0 + 2 * j + 2

        SQ2, LDQ, STQ = {}, {}, {}

        def STQF(j):
            return j % 2, 16 * (j // 2 + 1) + (sP[j % 2].n)

        _sp_base = [sP[0].n, sP[1].n]

        def qdst(j):
            m, pj = divmod(j, NPIECE)
            if m < 2:
                c, h0 = divmod(pj * PIECE, HU)
                blk = qw_loc.ap()[0:1, m * QW1:(m + 1) * QW1].squeeze(0).rearrange(
                    "(c p h) -> c p h", c=2, p=128)
                return blk[c:c + 1, :, h0:h0 + PIECE].squeeze(0), None
            cb = pj * 2
            base = 2 * QW1 + cb * 128 * 256
            return (qw_loc.ap()[0:1, base:base + 2 * 128 * 256].squeeze(
                0).rearrange("(c p o) -> c p o", c=2,
                             p=128).transpose([1, 0, 2]), "w3")

        def aq(scalar):
            scalar.wait_ge(sV.sem, M["absred_65"])
            for j in range(min(QRING, NP3)):
                sl = sQ[j % QRING]
                scalar.dma_start(out=wq_dst(j), in_=wpiece(j)).then_inc(sl.sem, 16)
                sl.n += 16
                LDQ[j] = (j % QRING, sl.n)
            for j in range(NP3):
                if j + QRING < NP3:
                    scalar.wait_ge(sV.sem, VQ2(j))
                    sl = sQ[(j + QRING) % QRING]
                    scalar.dma_start(out=wq_dst(j + QRING),
                                     in_=wpiece(j + QRING)).then_inc(sl.sem, 16)
                    sl.n += 16
                    LDQ[j + QRING] = ((j + QRING) % QRING, sl.n)
                scalar.wait_ge(sV.sem, VQ3(j))
                dst, kind = qdst(j)
                srcp = (ring2(qp_sb, j, PIECE) if kind is None else
                        ring2(qp_sb, j, PIECE).rearrange("p (c o) -> p c o", c=2))
                pl = sP[j % 2]
                scalar.dma_start(out=dst, in_=srcp).then_inc(pl.sem, 16)
                pl.n += 16
                _sl, _cn = STQF(j)
                STQ[j] = (_sl, pl.n)
                assert pl.n == _sp_base[_sl] + 16 * (j // 2 + 1)
        block.scalar(aq)

        def vq(vector):
            assert sV.n == VQ0
            for j in range(NP3):
                m = j // NPIECE
                _s, _c = LDQ[j]
                vector.wait_ge(sQ[_s].sem, _c)
                vector.tensor_scalar(ring2(qa_sb, j, PIECE), wqsl(j),
                                     thr_sb[:, m:m + 1], None, AL.is_gt)
                sV.inc(vector.tensor_scalar(ring2(qb_sb, j, PIECE), wqsl(j),
                                            thr_sb[:, 4 + m:5 + m], None,
                                            AL.is_lt))
                assert sV.n == VQ2(j)
                if j >= 2:
                    _ps, _pc = STQ[j - 2]
                    vector.wait_ge(sP[_ps].sem, _pc)
                sV.inc(vector.tensor_tensor(ring2(qp_sb, j, PIECE),
                                            ring2(qa_sb, j, PIECE),
                                            ring2(qb_sb, j, PIECE), AL.subtract))
                assert sV.n == VQ3(j)
        block.vector(vq)

        if debug:
            def g_dbg1(gpsimd):
                gpsimd.wait_ge(sWS.sem, M["wsg_loaded"])
                gpsimd.dma_start(out=dbg["wsg"][:, :], in_=wsg_sb[:, :]).then_inc(sG.sem, 16)
                sG.n += 16
                gpsimd.wait_ge(sV.sem, M["thr_copy"])
                gpsimd.dma_start(out=dbg["thr"][:, :], in_=thr_sb[:, :]).then_inc(sG.sem, 16)
                sG.n += 16
                gpsimd.wait_ge(sV.sem, VN(NH - 1))
                gpsimd.dma_start(out=dbg["ctxn"][:, :], in_=ctxn_sb[:, :]).then_inc(sG.sem, 16)
                sG.n += 16
                gpsimd.wait_ge(sV.sem, EV2(NCH - 1))
                gpsimd.dma_start(out=dbg["attn"][:, :], in_=attn_sb[:, :]).then_inc(sG.sem, 16)
                sG.n += 16
            block.gpsimd(g_dbg1)

        def g4(gpsimd):
            gpsimd.wait_ge(sP[0].sem, sP[0].n)
            gpsimd.wait_ge(sP[1].sem, sP[1].n)
            gpsimd.collective_compute(
                "AllGather", AL.bypass, replica_groups=RG,
                ins=[qw_loc.ap().opt()], outs=[qw_g.ap().opt()],
            ).then_inc(sC.sem, 1)
            sC.n += 1
            M["ag_qw"] = sC.n
            if debug:
                gpsimd.dma_start(out=dbg["qwloc"][:, :], in_=qw_loc.ap()[:, :]).then_inc(sG.sem, 16)
                sG.n += 16
                gpsimd.wait_ge(sV.sem, M["x2done"])
                gpsimd.dma_start(out=dbg["x1"][:, :], in_=xT_sb[:, :]).then_inc(sG.sem, 16)
                sG.n += 16
        block.gpsimd(g4)

        es_attp.close()
        es_p2b.close()
        es_p2.close()
        es_ln3p = ExitStack()
        ss3 = es_ln3p.enter_context(nc.psum_tensor("ss3", [1, R], F32))
        qq3 = es_ln3p.enter_context(nc.psum_tensor("qq3", [1, R], F32))

        T7B = sT.n
        for c in range(NCH):
            M[f"st3_{c}"] = T7B + c + 1

        def a7(scalar):
            for c in range(NCH):
                scalar.wait_ge(sV.sem, M[f"xu2_{c}"])
                if c >= 2:
                    scalar.wait_ge(sT.sem, M[f"st3_{c - 2}"])
                sS.inc(scalar.activation(out=ring2(sq_sb, c, R),
                                         in_=xT_sb[:, c * R:(c + 1) * R],
                                         func=AF.Square))
                M[f"sq3_{c}"] = sS.n
        block.scalar(a7)

        def v13(vector):
            for c in range(NCH):
                if c >= 2:
                    vector.wait_ge(sT.sem, M[f"st3_{c - 2}"])
                sV.inc(vector.tensor_copy(ring2(xb_sb, c, R),
                                          xT_sb[:, c * R:(c + 1) * R]))
                M[f"cast3_{c}"] = sV.n
        block.vector(v13)

        def t7(tensor):
            assert sT.n == T7B
            for c in range(NCH):
                tensor.wait_ge(sV.sem, M[f"cast3_{c}"])
                tensor.wait_ge(sS.sem, M[f"sq3_{c}"])
                xb = ring2(xb_sb, c, R)
                sq = ring2(sq_sb, c, R)
                st, sp = (c == 0), (c == NCH - 1)
                tensor.matmul(ss3[0:1, 0:512], ones_cb[:, 0:1], xb[:, 0:512],
                              start=st, stop=sp)
                tensor.matmul(ss3[0:1, 512:1024], ones_cb[:, 0:1], xb[:, 512:1024],
                              start=st, stop=sp)
                tensor.matmul(qq3[0:1, 0:512], ones_cb[:, 0:1], sq[:, 0:512],
                              start=st, stop=sp)
                ins = tensor.matmul(qq3[0:1, 512:1024], ones_cb[:, 0:1],
                                    sq[:, 512:1024], start=st, stop=sp)
                sT.inc(ins)
                assert sT.n == M[f"st3_{c}"]
        block.tensor(t7)

        def v14(vector):
            vector.wait_ge(sT.sem, M[f"st3_{NCH - 1}"])
            rows_part1(vector, ss3, qq3, "rows3")
        block.vector(v14)

        def a8(scalar):
            scalar.wait_ge(sV.sem, M["rows3"])
            sS.inc(scalar.sqrt(row(ROW_RSQ3), row(ROW_RSQ)))
            M["sqrt3"] = sS.n
        block.scalar(a8)

        def v14b(vector):
            vector.wait_ge(sS.sem, M["sqrt3"])
            sV.inc(vector.scalar_tensor_tensor(row(ROW_B3), row(ROW_M), NEG1,
                                               row(ROW_RSQ3), AL.mult, AL.mult))
            M["rows3b"] = sV.n
        block.vector(v14b)

        es_ln3p.close()

        qw12 = qwgv[:, 0:QW1].rearrange("r (u p h) -> r u p h", u=2, p=128)
        qw22 = qwgv[:, QW1:2 * QW1].rearrange("r (u p h) -> r u p h", u=2, p=128)
        qw3v = qwgv[:, 2 * QW1:3 * QW1].rearrange("r (hc p o) -> r hc p o",
                                                  hc=HUC, p=128)
        qw12p = qw12.transpose([2, 0, 1, 3])
        qw22p = qw22.transpose([2, 0, 1, 3])
        es_qu.close()
        XUF = {}
        for rh in range(2):
            es_rh = ExitStack()
            hid_sb = es_rh.enter_context(
                nc.sbuf_tensor(f"hid_{rh}", [128, HUC * 512], BF16))
            q3sl = es_rh.enter_context(
                nc.sbuf_tensor(f"q3sl_{rh}", [128, 2 * HUC * 128], BF16))
            es_rha = ExitStack()
            lnx3_sb = es_rha.enter_context(
                nc.sbuf_tensor(f"lnx3_{rh}", [128, NCH * 512], BF16))
            sg_sb = es_rha.enter_context(
                nc.sbuf_tensor(f"sg_{rh}", [128, 2 * 512], BF16))
            xe_sb = es_rha.enter_context(
                nc.sbuf_tensor(f"xe_{rh}", [128, 2 * 512], BF16))
            q1sl = es_rha.enter_context(
                nc.sbuf_tensor(f"q1sl_{rh}", [128, 2 * NCH * 128], BF16))
            q2sl = es_rha.enter_context(
                nc.sbuf_tensor(f"q2sl_{rh}", [128, 2 * NCH * 128], BF16))
            es_rhp = ExitStack()
            a3b = es_rhp.enter_context(nc.psum_tensor(f"a3b_{rh}", [128, 512], F32))
            b3b = es_rhp.enter_context(nc.psum_tensor(f"b3b_{rh}", [128, 512], F32))
            px1 = [es_rhp.enter_context(nc.psum_tensor(f"px1{i}_{rh}", [128, 512], F32))
                   for i in range(2)]
            px2 = [es_rhp.enter_context(nc.psum_tensor(f"px2{i}_{rh}", [128, 512], F32))
                   for i in range(2)]

            def t8(tensor, rh=rh, a3b=a3b, b3b=b3b):
                tensor.wait_ge(sV.sem, M["rows3b"])
                if rh == 1:
                    tensor.wait_ge(sV.sem, XUF[0])
                tensor.matmul(a3b[:, :], ones_rf[0:1, :], rowh(ROW_RSQ3, rh),
                              start=True, stop=True)
                sT.inc(tensor.matmul(b3b[:, :], ones_rf[0:1, :], rowh(ROW_B3, rh),
                                     start=True, stop=True))
                M[f"bc3_{rh}"] = sT.n
            block.tensor(t8)

            def v15(vector, rh=rh, lnx3_sb=lnx3_sb, a3b=a3b, b3b=b3b):
                vector.wait_ge(sT.sem, M[f"bc3_{rh}"])
                for c in range(NCH):
                    xs = xT_sb[:, c * R + rh * 512: c * R + (rh + 1) * 512]
                    vector.tensor_tensor(ln_tmp[:, 0:512], xs, a3b[:, :], AL.mult)
                    sV.inc(vector.tensor_tensor(
                        lnx3_sb[:, c * 512:(c + 1) * 512], ln_tmp[:, 0:512],
                        b3b[:, :], AL.add))
                    M[f"lnx3_{rh}_{c}"] = sV.n
            block.vector(v15)

            T9B = sT.n
            S9B = sS.n
            V9B = sV.n

            def TX2(hc):
                return T9B + 2 * hc + 2

            def SIG(hc):
                return S9B + hc + 1

            def VH(hc):
                return V9B + hc + 1

            def s6(sync, rh=rh, q1sl=q1sl, q2sl=q2sl):
                for hc in range(HUC):
                    if rh == 0 and hc == 0:
                        sync.wait_ge(sC.sem, M["ag_qw"])
                        sync.wait_ge(sV.sem, M["x2done"])
                    if rh == 1 and hc == 0:
                        sync.wait_ge(sT.sem, M["w3_last_0"])
                    if hc >= 2:
                        sync.wait_ge(sT.sem, TX2(hc - 2))
                    fl = sF[hc % 2]
                    for (sl, qv2) in ((q1sl, qw12p), (q2sl, qw22p)):
                        for u in range(2):
                            sync.dma_start(
                                out=ring2(sl, hc, NCH * 128).rearrange(
                                    "p (r u h) -> p r u h", r=8, u=2
                                )[:, :, u, :],
                                in_=qv2[:, :, u, hc * 128:(hc + 1) * 128],
                            ).then_inc(fl.sem, 16)
                            fl.n += 16
                    M[f"q12_{rh}_{hc}"] = (hc % 2, fl.n)
            block.sync(s6)

            def t9(tensor, rh=rh, q1sl=q1sl, q2sl=q2sl, lnx3_sb=lnx3_sb,
                   px1=px1, px2=px2):
                assert sT.n == T9B
                for hc in range(HUC):
                    _s, _c = M[f"q12_{rh}_{hc}"]
                    tensor.wait_ge(sF[_s].sem, _c)
                    if hc == 0:
                        tensor.wait_ge(sV.sem, M[f"lnx3_{rh}_{NCH - 1}"])
                    if hc >= 2:
                        tensor.wait_ge(sS.sem, SIG(hc - 2))
                        tensor.wait_ge(sV.sem, VH(hc - 2))
                    for (sl, ps) in ((q1sl, px1[hc % 2]), (q2sl, px2[hc % 2])):
                        for dc in range(NCH):
                            lhsT = ring2(sl, hc, NCH * 128)[:, dc * 128:(dc + 1) * 128]
                            ins = tensor.matmul(
                                ps[:, :], lhsT,
                                lnx3_sb[:, dc * 512:(dc + 1) * 512],
                                start=(dc == 0), stop=(dc == NCH - 1))
                        sT.inc(ins)
                    assert sT.n == TX2(hc)
            block.tensor(t9)

            def a9(scalar, rh=rh, sg_sb=sg_sb, px1=px1):
                assert sS.n == S9B
                for hc in range(HUC):
                    scalar.wait_ge(sT.sem, TX2(hc) - 1)
                    sS.inc(scalar.activation(out=ring2(sg_sb, hc, 512),
                                             in_=px1[hc % 2][:, :], func=AF.Sigmoid))
                    assert sS.n == SIG(hc)
            block.scalar(a9)

            def v16(vector, rh=rh, sg_sb=sg_sb, xe_sb=xe_sb, hid_sb=hid_sb,
                    px2=px2):
                assert sV.n == V9B
                for hc in range(HUC):
                    vector.wait_ge(sT.sem, TX2(hc))
                    vector.wait_ge(sS.sem, SIG(hc))
                    vector.tensor_copy(ring2(xe_sb, hc, 512), px2[hc % 2][:, :])
                    sV.inc(vector.tensor_tensor(
                        hid_sb[:, hc * 512:(hc + 1) * 512], ring2(sg_sb, hc, 512),
                        ring2(xe_sb, hc, 512), AL.mult))
                    assert sV.n == VH(hc)
            block.vector(v16)

            es_rhp.close()
            es_rha.close()
            es_w3s = ExitStack()
            y_sb = es_w3s.enter_context(
                nc.sbuf_tensor(f"y_sb_{rh}", [128, NCH * 512], BF16))
            es_w3p = ExitStack()
            py = [es_w3p.enter_context(nc.psum_tensor(f"py{i}_{rh}", [128, 512], F32))
                  for i in range(2)]
            ss4 = es_w3p.enter_context(nc.psum_tensor(f"ss4_{rh}", [1, 512], F32))
            qq4 = es_w3p.enter_context(nc.psum_tensor(f"qq4_{rh}", [1, 512], F32))

            T10B = sT.n

            def TY(oc):
                return T10B + oc + 1

            V17B = sV.n

            def EV4(oc):
                return V17B + oc + 1

            def s7(sync, rh=rh, q3sl=q3sl):
                for oc in range(NCH):
                    if oc == 0 and rh == 1:
                        sync.wait_ge(sT.sem, M["w3_last_0"])
                    if oc >= 2:
                        sync.wait_ge(sT.sem, TY(oc - 2))
                    src = qw3v[oc // 2:oc // 2 + 1, :, :,
                               (oc % 2) * 128:(oc % 2) * 128 + 128]
                    q3l = sQ3[oc % 2]
                    for hh in range(2):
                        sync.dma_start(
                            out=ring2(q3sl, oc, HUC * 128)[
                                :, hh * 22 * 128:(hh + 1) * 22 * 128].rearrange(
                                "p (hc o) -> p hc o", hc=22),
                            in_=src[:, hh * 22:(hh + 1) * 22].squeeze(
                                0).transpose([1, 0, 2]),
                        ).then_inc(q3l.sem, 16)
                        q3l.n += 16
                    M[f"q3_{rh}_{oc}"] = (oc % 2, q3l.n)
            block.sync(s7)

            def t10(tensor, rh=rh, q3sl=q3sl, hid_sb=hid_sb, py=py):
                assert sT.n == T10B
                for oc in range(NCH):
                    _s, _c = M[f"q3_{rh}_{oc}"]
                    tensor.wait_ge(sQ3[_s].sem, _c)
                    if oc == 0:
                        tensor.wait_ge(sV.sem, VH(HUC - 1))
                    if oc >= 2:
                        tensor.wait_ge(sV.sem, EV4(oc - 2))
                    for hc in range(HUC):
                        lhsT = ring2(q3sl, oc, HUC * 128)[:, hc * 128:(hc + 1) * 128]
                        ins = tensor.matmul(py[oc % 2][:, :], lhsT,
                                            hid_sb[:, hc * 512:(hc + 1) * 512],
                                            start=(hc == 0), stop=(hc == HUC - 1))
                    sT.inc(ins)
                    assert sT.n == TY(oc)
                M[f"w3_last_{rh}"] = sT.n
            block.tensor(t10)

            if debug and rh == 0:
                def g_dbg2(gpsimd, hid_sb=hid_sb):
                    gpsimd.wait_ge(sV.sem, VH(HUC - 1))
                    gpsimd.dma_start(out=dbg["hid"][:, :], in_=hid_sb[:, :]).then_inc(sG.sem, 16)
                    sG.n += 16
                block.gpsimd(g_dbg2)

            def v17(vector, rh=rh, y_sb=y_sb, py=py):
                assert sV.n == V17B
                for oc in range(NCH):
                    vector.wait_ge(sT.sem, TY(oc))
                    sV.inc(vector.tensor_copy(y_sb[:, oc * 512:(oc + 1) * 512],
                                              py[oc % 2][:, :]))
                    assert sV.n == EV4(oc)
            block.vector(v17)

            if debug and rh == 0:
                def g_dbg2b(gpsimd, q3sl=q3sl, c0=32, c1=32):
                    gpsimd.wait_ge(sQ3[0].sem, c0)
                    gpsimd.wait_ge(sQ3[1].sem, c1)
                    gpsimd.dma_start(out=dbg["q3sl"][:, :], in_=q3sl[:, :]).then_inc(sG.sem, 16)
                    sG.n += 16
                block.gpsimd(g_dbg2b)

                def g_dbg3(gpsimd, y_sb=y_sb):
                    gpsimd.wait_ge(sV.sem, EV4(NCH - 1))
                    gpsimd.dma_start(out=dbg["y"][:, :], in_=y_sb[:, :]).then_inc(sG.sem, 16)
                    sG.n += 16
                block.gpsimd(g_dbg3)

            T11B = sT.n
            for c in range(NCH):
                M[f"st4_{rh}_{c}"] = T11B + c + 1

            def a10(scalar, rh=rh, y_sb=y_sb):
                for c in range(NCH):
                    scalar.wait_ge(sV.sem, EV4(c))
                    if c >= 2:
                        scalar.wait_ge(sT.sem, M[f"st4_{rh}_{c - 2}"])
                    sS.inc(scalar.activation(out=ring2(sq_sb, c, R)[:, 0:512],
                                             in_=y_sb[:, c * 512:(c + 1) * 512],
                                             func=AF.Square))
                    M[f"sq4_{rh}_{c}"] = sS.n
            block.scalar(a10)

            def t11(tensor, rh=rh, y_sb=y_sb, ss4=ss4, qq4=qq4):
                assert sT.n == T11B
                for c in range(NCH):
                    tensor.wait_ge(sV.sem, EV4(c))
                    tensor.wait_ge(sS.sem, M[f"sq4_{rh}_{c}"])
                    st, sp = (c == 0), (c == NCH - 1)
                    tensor.matmul(ss4[0:1, :], ones_cb[:, 0:1],
                                  y_sb[:, c * 512:(c + 1) * 512],
                                  start=st, stop=sp)
                    ins = tensor.matmul(qq4[0:1, :], ones_cb[:, 0:1],
                                        ring2(sq_sb, c, R)[:, 0:512],
                                        start=st, stop=sp)
                    sT.inc(ins)
                    assert sT.n == M[f"st4_{rh}_{c}"]
            block.tensor(t11)

            def v18(vector, rh=rh, ss4=ss4, qq4=qq4):
                vector.wait_ge(sT.sem, M[f"st4_{rh}_{NCH - 1}"])
                rows_part1_w(vector, ss4, qq4, f"rows4_{rh}", 512)
            block.vector(v18)

            def a11(scalar, rh=rh):
                scalar.wait_ge(sV.sem, M[f"rows4_{rh}"])
                sS.inc(scalar.sqrt(rows[0:1, ROW_RSQ * R:ROW_RSQ * R + 512],
                                   rows[0:1, ROW_RSQ * R:ROW_RSQ * R + 512]))
                M[f"sqrt4_{rh}"] = sS.n
            block.scalar(a11)

            es_bc4 = ExitStack()
            m4b = es_bc4.enter_context(nc.psum_tensor(f"m4b_{rh}", [128, 512], F32))
            a4b = es_bc4.enter_context(nc.psum_tensor(f"a4b_{rh}", [128, 512], F32))

            def t12(tensor, rh=rh, m4b=m4b, a4b=a4b):
                tensor.wait_ge(sS.sem, M[f"sqrt4_{rh}"])
                tensor.matmul(m4b[:, :], ones_rf[0:1, :],
                              rows[0:1, ROW_M * R:ROW_M * R + 512],
                              start=True, stop=True)
                sT.inc(tensor.matmul(a4b[:, :], ones_rf[0:1, :],
                                     rows[0:1, ROW_RSQ * R:ROW_RSQ * R + 512],
                                     start=True, stop=True))
                M[f"bc4_{rh}"] = sT.n
            block.tensor(t12)

            def v19(vector, rh=rh, y_sb=y_sb, m4b=m4b, a4b=a4b):
                vector.wait_ge(sT.sem, M[f"bc4_{rh}"])
                for c in range(NCH):
                    ys = y_sb[:, c * 512:(c + 1) * 512]
                    xs = xT_sb[:, c * R + rh * 512: c * R + (rh + 1) * 512]
                    vector.tensor_tensor(ln_tmp[:, 0:512], ys, m4b[:, :],
                                         AL.subtract)
                    vector.tensor_tensor(ln_tmp[:, 0:512], ln_tmp[:, 0:512],
                                         a4b[:, :], AL.mult)
                    sV.inc(vector.tensor_tensor(xs, xs, ln_tmp[:, 0:512], AL.add))
                XUF[rh] = sV.n
            block.vector(v19)

            es_bc4.close()
            es_w3p.close()
            es_w3s.close()
            es_rh.close()

        def g5(gpsimd):
            gpsimd.wait_ge(sV.sem, XUF[1])
            gpsimd.dma_start(
                out=yT.rearrange("(c p) r -> p c r", p=128),
                in_=xT_sb[:, :].rearrange("p (c r) -> p c r", c=NCH),
            ).then_inc(sG.sem, 16)
            sG.n += 16
            gpsimd.wait_ge(sG.sem, sG.n)
        block.gpsimd(g5)

        es_qu.close()

    return nc


_NC_CACHE = {}
_LAST_RES = None


def _get_nc():
    if "nc" not in _NC_CACHE:
        _NC_CACHE["nc"] = build_program()
    return _NC_CACHE["nc"]


def kernel(L0_attn_query, L0_attn_key, L0_attn_value, L0_attn_Wqkv,
           L0_attn_Wproj, L0_attn_bproj, L0_ffn_w1, L0_ffn_w2, L0_ffn_w3,
           **_unused):
    x = np.asarray(L0_attn_query)
    Wqkv = np.asarray(L0_attn_Wqkv)
    Wproj = np.asarray(L0_attn_Wproj)
    bproj = np.asarray(L0_attn_bproj)
    w1 = np.asarray(L0_ffn_w1)
    w2 = np.asarray(L0_ffn_w2)
    w3 = np.asarray(L0_ffn_w3)

    Wq3 = Wqkv.reshape(NH, 3, 128, H)
    WK = np.ascontiguousarray(Wq3[:, 1].reshape(NH * 128, H).T)
    WQ = np.ascontiguousarray(Wq3[:, 0].reshape(NH * 128, H).T)
    wkqT = np.ascontiguousarray(np.concatenate([WK, WQ], axis=1))
    wvT = np.ascontiguousarray(Wq3[:, 2].reshape(NH * 128, H).T)
    wpT = np.ascontiguousarray(Wproj.T)
    w1T = np.ascontiguousarray(w1.T)
    w2T = np.ascontiguousarray(w2.T)
    w3T = np.ascontiguousarray(w3.T)

    in_maps = []
    for c in range(NCORES):
        b, half = c // 2, c % 2
        s0 = half * R
        ctl = np.zeros((1, 128), dtype=np.uint32)
        ctl[0, 0] = b
        ctl[0, 1] = 4 * half + b
        in_maps.append({
            "xT": np.ascontiguousarray(x[b, s0:s0 + R, :].T),
            "wkqT": wkqT,
            "wvT": wvT,
            "wpT": wpT,
            "bpj": np.ascontiguousarray(bproj.reshape(1, H)),
            "w1s": np.ascontiguousarray(w1T[c * 256:(c + 1) * 256, :]),
            "w2s": np.ascontiguousarray(w2T[c * 256:(c + 1) * 256, :]),
            "w3s": np.ascontiguousarray(w3T[:, c * 256:(c + 1) * 256]),
            "ctl": ctl,
        })

    nc = _get_nc()
    res = run_bass_kernel_spmd(nc, in_maps, core_ids=list(range(NCORES)))
    global _LAST_RES
    _LAST_RES = res
    out = np.empty((4, SQ, H), dtype=np.float32)
    for c in range(NCORES):
        b, half = c // 2, c % 2
        out[b, half * R:(half + 1) * R, :] = res.results[c]["yT"].T
    return out


# version 04
# speedup vs baseline: 1.2385x; 1.2385x over previous
import os
os.environ.setdefault("TRN_ALLOW_ICI_ALL_TO_ALL", "1")

import numpy as np
from contextlib import ExitStack

import concourse.bass as bass
import concourse.mybir as mybir
from concourse.bass import ds
from concourse.bass_utils import run_bass_kernel_spmd

F32 = mybir.dt.float32
BF16 = mybir.dt.bfloat16
U32 = mybir.dt.uint32
AF = mybir.ActivationFunctionType
AL = mybir.AluOpType
AX = mybir.AxisListType.X

NCORES = 8
RG = [list(range(NCORES))]

H = 2048
NCH = 16
R = 1024
NH = 16
SQ = 2048
HU = 5632
HUC = 44
SEC = 128 * R
NE = HU * H
QW1 = (H // 8) * HU
QWALL = 3 * QW1
PIECE = 512
NPIECE = (H // 8) * HU // 128 // PIECE
NP3 = 3 * NPIECE
QRING = 4

f32 = np.float32
SM_SCALE = f32(f32(1.0) / np.sqrt(f32(128.0)))
SM_BIAS = f32(-12.0)
INV_H = float(f32(1.0 / H))
LN_EPS = float(f32(1e-5))
NEG1 = -1.0
THR_SCL = float(f32(0.5 / NE))
THR_OFF = float(f32(0.5e-8))


class Ctr:
    def __init__(self, nc, name):
        self.sem = nc.alloc_semaphore(name)
        self.n = 0

    def inc(self, instr, k=1):
        self.n += k
        instr.then_inc(self.sem, k)
        return self.n


def build_program(debug=False):
    nc = bass.Bass("TRN2", target_bir_lowering=False)
    dbg = {}
    if debug:
        dbg["lnx"] = nc.dram_tensor("d_lnx", [128, NCH * R], BF16, kind="ExternalOutput").ap()
        dbg["kvloc"] = nc.dram_tensor("d_kvloc", [2 * NH, SEC], BF16, kind="ExternalOutput").ap()
        dbg["qloc"] = nc.dram_tensor("d_qloc", [NH, SEC], BF16, kind="ExternalOutput").ap()
        dbg["ctxn"] = nc.dram_tensor("d_ctxn", [128, NH * R], BF16, kind="ExternalOutput").ap()
        dbg["attn"] = nc.dram_tensor("d_attn", [128, NCH * R], BF16, kind="ExternalOutput").ap()
        dbg["thr"] = nc.dram_tensor("d_thr", [128, 8], F32, kind="ExternalOutput").ap()
        dbg["qwloc"] = nc.dram_tensor("d_qwloc", [1, QWALL], BF16, kind="ExternalOutput").ap()
        dbg["x1"] = nc.dram_tensor("d_x1", [128, NCH * R], F32, kind="ExternalOutput").ap()
        dbg["hid"] = nc.dram_tensor("d_hid", [128, HUC * 512], BF16, kind="ExternalOutput").ap()
        dbg["y"] = nc.dram_tensor("d_y", [128, NCH * 512], BF16, kind="ExternalOutput").ap()
        dbg["parts"] = nc.dram_tensor("d_parts", [128, NP3 + 2], F32, kind="ExternalOutput").ap()
        dbg["q3sl"] = nc.dram_tensor("d_q3sl", [128, 2 * HUC * 128], BF16, kind="ExternalOutput").ap()
        dbg["p3"] = nc.dram_tensor("d_p3", [128, 4], F32, kind="ExternalOutput").ap()
        dbg["ws"] = nc.dram_tensor("d_ws", [1, 16], F32, kind="ExternalOutput").ap()
        dbg["wsg"] = nc.dram_tensor("d_wsg", [1, 128], F32, kind="ExternalOutput").ap()

    xT = nc.dram_tensor("xT", [H, R], F32, kind="ExternalInput").ap()
    wkqT = nc.dram_tensor("wkqT", [H, 2 * H], F32, kind="ExternalInput").ap()
    wvT = nc.dram_tensor("wvT", [H, H], F32, kind="ExternalInput").ap()
    wpTt = nc.dram_tensor("wpT", [H, H], F32, kind="ExternalInput").ap()
    bpj = nc.dram_tensor("bpj", [1, H], F32, kind="ExternalInput").ap()
    w1s = nc.dram_tensor("w1s", [H // 8, HU], F32, kind="ExternalInput").ap()
    w2s = nc.dram_tensor("w2s", [H // 8, HU], F32, kind="ExternalInput").ap()
    w3s = nc.dram_tensor("w3s", [HU, H // 8], F32, kind="ExternalInput").ap()
    ctl = nc.dram_tensor("ctl", [1, 128], U32, kind="ExternalInput").ap()
    yT = nc.dram_tensor("yT", [H, R], F32, kind="ExternalOutput").ap()

    kv_sendK = nc.dram_tensor("kv_sendK", [NCORES, 4, 128, R], BF16)
    kv_sendV = nc.dram_tensor("kv_sendV", [NCORES, 4, 128, R], BF16)
    q_send = nc.dram_tensor("q_send", [NCORES, 4, 128, R], BF16)
    ws_loc = nc.dram_tensor("ws_loc", [1, 16], F32)
    qw_loc = nc.dram_tensor("qw_loc", [1, QWALL], BF16)
    kv_stK = nc.dram_tensor("kv_stK", [NCORES, 4, 128, R], BF16)
    kv_stV = nc.dram_tensor("kv_stV", [NCORES, 4, 128, R], BF16)
    q_stA = nc.dram_tensor("q_stA", [NCORES, 4, 128, R], BF16)
    q_st = nc.dram_tensor("q_st", [4, 4, 128, R], BF16)
    ws_g = nc.dram_tensor("ws_g", [NCORES, 16], F32)
    qw_g = nc.dram_tensor("qw_g", [NCORES, QWALL], BF16)

    xT_sb = nc.alloc_sbuf_tensor("xT_sb", [128, NCH * R], F32)
    rows = nc.alloc_sbuf_tensor("rows", [1, 5 * R], F32)
    ROW_M, ROW_RSQ, ROW_B, ROW_T, ROW_RSQ3 = 0, 1, 2, 3, 4
    ROW_B3 = ROW_B
    xb_sb = nc.alloc_sbuf_tensor("xb_sb", [128, 2 * R], BF16)
    sq_sb = nc.alloc_sbuf_tensor("sq_sb", [128, 2 * R], BF16)
    ln_tmp = nc.alloc_sbuf_tensor("ln_tmp", [128, R], F32)
    ones_cb = nc.alloc_sbuf_tensor("ones_cb", [128, 16], BF16)
    ones_cf = nc.alloc_sbuf_tensor("ones_cf", [128, 16], F32)
    ones_rf = nc.alloc_sbuf_tensor("ones_rf", [1, 128], F32)
    ones_rb = nc.alloc_sbuf_tensor("ones_rb", [1, R], BF16)
    bprojB = nc.alloc_sbuf_tensor("bprojB", [1, H], BF16)
    ctl_sb = nc.alloc_sbuf_tensor("ctl_sb", [1, 128], U32)
    thr_sb = nc.alloc_sbuf_tensor("thr_sb", [128, 8], F32)
    thrr_sb = nc.alloc_sbuf_tensor("thrr_sb", [1, 16], F32)
    wsg_sb = nc.alloc_sbuf_tensor("wsg_sb", [1, 128], F32)
    parts_sb = nc.alloc_sbuf_tensor("parts_sb", [128, NP3 + 2], F32)
    p3_sb = nc.alloc_sbuf_tensor("p3_sb", [128, 4], F32)
    ws_sb = nc.alloc_sbuf_tensor("ws_sb", [1, 16], F32)

    def row(i):
        return rows[0:1, i * R:(i + 1) * R]

    def rowh(i, h, w=512):
        return rows[0:1, i * R + h * w: i * R + (h + 1) * w]

    es_qu = ExitStack()
    wq_sb = es_qu.enter_context(nc.sbuf_tensor("wq_sb", [128, QRING * PIECE], F32))
    qa_sb = es_qu.enter_context(nc.sbuf_tensor("qa_sb", [128, 2 * PIECE], BF16))
    qb_sb = es_qu.enter_context(nc.sbuf_tensor("qb_sb", [128, 2 * PIECE], BF16))
    qp_sb = es_qu.enter_context(nc.sbuf_tensor("qp_sb", [128, 2 * PIECE], BF16))

    def wqsl(j):
        return wq_sb[:, (j % QRING) * PIECE:(j % QRING + 1) * PIECE]

    def ring2(buf, j, w):
        return buf[:, (j % 2) * w:(j % 2 + 1) * w]

    sX = Ctr(nc, "sX")
    sWS = Ctr(nc, "sWS")
    sQ = [Ctr(nc, f"sQ{i}") for i in range(4)]
    sP = [Ctr(nc, f"sP{i}") for i in range(2)]
    sW2 = [Ctr(nc, f"sW2{i}") for i in range(2)]
    sE = [Ctr(nc, f"sE{i}") for i in range(2)]
    sVW = [Ctr(nc, f"sVW{i}") for i in range(2)]
    sGL = [Ctr(nc, f"sGL{i}") for i in range(2)]
    sST = Ctr(nc, "sST")
    sF = [Ctr(nc, f"sF{i}") for i in range(2)]
    sQ3 = [Ctr(nc, f"sQ3{i}") for i in range(2)]
    sG = Ctr(nc, "sG")
    sC = Ctr(nc, "sC")
    sT = Ctr(nc, "sT")
    sV = Ctr(nc, "sV")
    sS = Ctr(nc, "sS")
    M = {}

    def slab_sem(s):
        return sW2[s % 2], 16 * (s // 2 + 1)

    w1v = w1s.rearrange("(c p) h -> c p h", p=128)
    w2v = w2s.rearrange("(c p) h -> c p h", p=128)
    w3v = w3s.rearrange("(c p) o -> c p o", p=128)
    WSRC = [w1v, w2v, w3v]

    def wpiece(j):
        m, pj = divmod(j, NPIECE)
        if m < 2:
            c, h0 = divmod(pj * PIECE, HU)
            return WSRC[m][c:c + 1, :, h0:h0 + PIECE].squeeze(0)
        cb = 2 * pj
        return w3v[cb:cb + 2, :, :].transpose([1, 0, 2])

    def wq_dst(j):
        if j // NPIECE < 2:
            return wqsl(j)
        return wqsl(j).rearrange("p (c o) -> p c o", c=2)

    kvKv = kv_sendK.ap()
    kvVv = kv_sendV.ap().rearrange("(c j) h p s -> c j h p s", j=2)
    qsv = q_send.ap()
    qwgv = qw_g.ap()

    es_p1 = ExitStack()
    lnx_sb = es_p1.enter_context(nc.sbuf_tensor("lnx_sb", [128, NCH * R], BF16))
    wsl_sb = es_p1.enter_context(nc.sbuf_tensor("wsl_sb", [128, 2 * NCH * 512], BF16))
    evq_sb = es_p1.enter_context(nc.sbuf_tensor("evq_sb", [128, 2 * R], BF16))
    evv_sb = es_p1.enter_context(nc.sbuf_tensor("evv_sb", [128, 2 * 512], BF16))

    def wsl(par):
        return wsl_sb[:, par * NCH * 512:(par + 1) * NCH * 512]

    es_ln1p = ExitStack()
    ss_ps = es_ln1p.enter_context(nc.psum_tensor("ss_ps", [1, R], F32))
    qq_ps = es_ln1p.enter_context(nc.psum_tensor("qq_ps", [1, R], F32))
    a_ps = es_ln1p.enter_context(nc.psum_tensor("a_ps", [128, 512], F32))
    b_ps = es_ln1p.enter_context(nc.psum_tensor("b_ps", [128, 512], F32))

    with nc.Block() as block:
        def s1(sync):
            sync.dma_start(out=ctl_sb[:, :], in_=ctl[:, :]).then_inc(sX.sem, 16)
            sX.n += 16
            sync.dma_start(
                out=xT_sb[:, :].rearrange("p (c r) -> p c r", c=NCH),
                in_=xT.rearrange("(c p) r -> p c r", p=128),
            ).then_inc(sX.sem, 16)
            sX.n += 16
            M["x_loaded"] = sX.n
        block.sync(s1)

        M["consts"] = 1
        for c in range(NCH):
            M[f"cast_{c}"] = 1 + 5 * c + 1
            for k in range(4):
                M[f"absred_{4 * c + k}"] = 1 + 5 * c + 2 + k
        M["rows1"] = 82
        M["rowsb"] = 83
        for h in range(2):
            for c in range(NCH):
                M[f"lnx_{h}_{c}"] = 83 + h * NCH + c + 1
        M["absred_64"] = 116
        M["absred_65"] = 117
        M["p3"] = 118

        def s2(sync):
            for j in range(NP3):
                if j >= QRING:
                    sync.wait_ge(sV.sem, M[f"absred_{j - QRING}"])
                sl = sQ[j % QRING]
                sync.dma_start(out=wq_dst(j), in_=wpiece(j)).then_inc(sl.sem, 16)
                sl.n += 16
                M[f"wq1_{j}"] = (j % QRING, sl.n)
        block.sync(s2)

        def v1(vector):
            vector.memset(ones_cb[:, :], 1.0)
            vector.memset(ones_cf[:, :], 1.0)
            vector.memset(ones_rf[:, :], 1.0)
            vector.memset(ones_rb[:, :], 1.0)
            vector.memset(p3_sb[:, :], 0.0)
            vector.memset(ws_sb[:, :], 0.0)
            sV.inc(vector.memset(parts_sb[:, :], 0.0))
            assert sV.n == M["consts"]
        block.vector(v1)

        def a1(scalar):
            scalar.wait_ge(sX.sem, M["x_loaded"])
            for c in range(NCH):
                if c >= 2:
                    scalar.wait_ge(sT.sem, c - 1)
                sS.inc(scalar.activation(
                    out=ring2(sq_sb, c, R), in_=xT_sb[:, c * R:(c + 1) * R],
                    func=AF.Square))
                M[f"sq1_{c}"] = sS.n
        block.scalar(a1)

        def v2(vector):
            vector.wait_ge(sX.sem, M["x_loaded"])
            for c in range(NCH):
                if c >= 2:
                    vector.wait_ge(sT.sem, c - 1)
                sV.inc(vector.tensor_copy(ring2(xb_sb, c, R),
                                          xT_sb[:, c * R:(c + 1) * R]))
                assert sV.n == M[f"cast_{c}"]
                for k in range(4):
                    j = 4 * c + k
                    slot, cnt = M[f"wq1_{j}"]
                    vector.wait_ge(sQ[slot].sem, cnt)
                    sV.inc(vector.tensor_reduce(
                        parts_sb[:, j:j + 1], wqsl(j), axis=AX, op=AL.add,
                        apply_absolute_value=True))
                    assert sV.n == M[f"absred_{j}"]
        block.vector(v2)

        def t1(tensor):
            assert sT.n == 0
            for c in range(NCH):
                tensor.wait_ge(sV.sem, M[f"cast_{c}"])
                tensor.wait_ge(sS.sem, M[f"sq1_{c}"])
                xb = ring2(xb_sb, c, R)
                sq = ring2(sq_sb, c, R)
                st, sp = (c == 0), (c == NCH - 1)
                tensor.matmul(ss_ps[0:1, 0:512], ones_cb[:, 0:1], xb[:, 0:512],
                              start=st, stop=sp)
                tensor.matmul(ss_ps[0:1, 512:1024], ones_cb[:, 0:1],
                              xb[:, 512:1024], start=st, stop=sp)
                tensor.matmul(qq_ps[0:1, 0:512], ones_cb[:, 0:1], sq[:, 0:512],
                              start=st, stop=sp)
                ins = tensor.matmul(qq_ps[0:1, 512:1024], ones_cb[:, 0:1],
                                    sq[:, 512:1024], start=st, stop=sp)
                sT.inc(ins)
            M["ln1_stats"] = sT.n
        block.tensor(t1)

        def rows_part1(vector, ssp, qqp, mark):
            vector.tensor_scalar(row(ROW_M), ssp[0:1, :], INV_H, None, AL.mult)
            vector.tensor_tensor(row(ROW_T), row(ROW_M), row(ROW_M), AL.mult)
            vector.scalar_tensor_tensor(row(ROW_T), qqp[0:1, :], INV_H,
                                        row(ROW_T), AL.mult, AL.subtract)
            vector.tensor_scalar(row(ROW_T), row(ROW_T), LN_EPS, None, AL.add)
            sV.inc(vector.reciprocal(row(ROW_RSQ), row(ROW_T)))
            M[mark] = sV.n

        def rows_part1_w(vector, ssp, qqp, mark, w):
            vector.tensor_scalar(rows[0:1, ROW_M * R:ROW_M * R + w],
                                 ssp[0:1, :], INV_H, None, AL.mult)
            rm = rows[0:1, ROW_M * R:ROW_M * R + w]
            rt = rows[0:1, ROW_T * R:ROW_T * R + w]
            vector.tensor_tensor(rt, rm, rm, AL.mult)
            vector.scalar_tensor_tensor(rt, qqp[0:1, :], INV_H, rt,
                                        AL.mult, AL.subtract)
            vector.tensor_scalar(rt, rt, LN_EPS, None, AL.add)
            sV.inc(vector.reciprocal(
                rows[0:1, ROW_RSQ * R:ROW_RSQ * R + w], rt))
            M[mark] = sV.n

        def v3(vector):
            vector.wait_ge(sT.sem, M["ln1_stats"])
            rows_part1(vector, ss_ps, qq_ps, "rows1")
            assert sV.n == M["rows1"]
        block.vector(v3)

        def a2(scalar):
            scalar.wait_ge(sV.sem, M["rows1"])
            sS.inc(scalar.sqrt(row(ROW_RSQ), row(ROW_RSQ)))
            M["sqrt1"] = sS.n
        block.scalar(a2)

        def v4(vector):
            vector.wait_ge(sS.sem, M["sqrt1"])
            sV.inc(vector.scalar_tensor_tensor(row(ROW_B), row(ROW_M), NEG1,
                                               row(ROW_RSQ), AL.mult, AL.mult))
            assert sV.n == M["rowsb"]
        block.vector(v4)

        for h in range(2):
            def t2(tensor, h=h):
                if h == 0:
                    tensor.wait_ge(sV.sem, M["rowsb"])
                else:
                    tensor.wait_ge(sV.sem, M[f"lnx_0_{NCH - 1}"])
                tensor.matmul(a_ps[:, :], ones_rf[0:1, :],
                              rowh(ROW_RSQ, h), start=True, stop=True)
                ins = tensor.matmul(b_ps[:, :], ones_rf[0:1, :],
                                    rowh(ROW_B, h), start=True, stop=True)
                sT.inc(ins)
                M[f"ln1_bc_{h}"] = sT.n
            block.tensor(t2)

            def v5a(vector, h=h):
                vector.wait_ge(sT.sem, M[f"ln1_bc_{h}"])
                for c in range(NCH):
                    xs = xT_sb[:, c * R + h * 512: c * R + (h + 1) * 512]
                    vector.tensor_tensor(ln_tmp[:, 0:512], xs, a_ps[:, :], AL.mult)
                    sV.inc(vector.tensor_tensor(
                        lnx_sb[:, c * R + h * 512: c * R + (h + 1) * 512],
                        ln_tmp[:, 0:512], b_ps[:, :], AL.add))
                    assert sV.n == M[f"lnx_{h}_{c}"]
            block.vector(v5a)

        def v5b(vector):
            for j in (64, 65):
                slot, cnt = M[f"wq1_{j}"]
                vector.wait_ge(sQ[slot].sem, cnt)
                sV.inc(vector.tensor_reduce(
                    parts_sb[:, j:j + 1], wqsl(j), axis=AX, op=AL.add,
                    apply_absolute_value=True))
                assert sV.n == M[f"absred_{j}"]
            for m in range(3):
                vector.tensor_reduce(p3_sb[:, m:m + 1],
                                     parts_sb[:, m * NPIECE:(m + 1) * NPIECE],
                                     axis=AX, op=AL.add)
            sV.inc(vector.engine_nop())
            assert sV.n == M["p3"]
        block.vector(v5b)

        es_ln1p.close()
        es_qkvp = ExitStack()
        pqk = [es_qkvp.enter_context(nc.psum_tensor(f"pqk{i}", [128, R], F32))
               for i in range(2)]
        pv = [es_qkvp.enter_context(nc.psum_tensor(f"pv{i}", [128, 512], F32))
              for i in range(2)]
        pr_ps = es_qkvp.enter_context(nc.psum_tensor("pr_ps", [1, 16], F32))

        EVB = M["p3"]
        M["ws_evac"] = EVB + 1

        def evK(oc):
            return EVB + 1 + oc + 1

        def evV(idx):
            return EVB + 17 + idx + 1

        def evQ(oc):
            return EVB + 49 + oc + 1

        _e = [sE[0].n, sE[1].n]
        _v = [sVW[0].n, sVW[1].n]
        WRK, WRV, WRQ = {}, {}, {}
        for _oc in range(NH):
            _e[_oc % 2] += 32
            WRK[_oc] = (_oc % 2, _e[_oc % 2])
        for _idx in range(32):
            _v[_idx % 2] += 32
            WRV[_idx] = (_idx % 2, _v[_idx % 2])
        for _oc in range(NH):
            _e[_oc % 2] += 32
            WRQ[_oc] = (_oc % 2, _e[_oc % 2])

        T_QKV = {}

        def t3(tensor):
            tensor.wait_ge(sV.sem, M["p3"])
            sT.inc(tensor.matmul(pr_ps[0:1, 0:4], ones_cf[:, 0:1], p3_sb[:, 0:4],
                                 start=True, stop=True))
            M["pmm"] = sT.n
            for oc in range(NH):
                s = oc // 4
                _sm, _c = slab_sem(s)
                tensor.wait_ge(_sm.sem, _c)
                if oc == 0:
                    tensor.wait_ge(sV.sem, M[f"lnx_1_{NCH - 1}"])
                if oc >= 2:
                    tensor.wait_ge(sV.sem, evK(oc - 2))
                tgt = pqk[oc % 2]
                for hc in range(NCH):
                    lhsT = wsl(s % 2)[:, hc * 512 + (oc % 4) * 128:
                                      hc * 512 + (oc % 4) * 128 + 128]
                    st, sp = (hc == 0), (hc == NCH - 1)
                    tensor.matmul(tgt[:, 0:512], lhsT,
                                  lnx_sb[:, hc * R: hc * R + 512],
                                  start=st, stop=sp)
                    ins = tensor.matmul(tgt[:, 512:1024], lhsT,
                                        lnx_sb[:, hc * R + 512: hc * R + 1024],
                                        start=st, stop=sp)
                sT.inc(ins)
                T_QKV[("K", oc)] = sT.n
            for vs in range(4):
                s = 4 + vs
                for rsub in range(8):
                    idx = vs * 8 + rsub
                    _sm, _c = slab_sem(s)
                    tensor.wait_ge(_sm.sem, _c)
                    if idx >= 2:
                        tensor.wait_ge(sV.sem, evV(idx - 2))
                    tgt = pv[idx % 2]
                    for hc in range(NCH):
                        lhsT = lnx_sb[:, hc * R + rsub * 128:
                                      hc * R + rsub * 128 + 128]
                        rhs = wsl(s % 2)[:, hc * 512:(hc + 1) * 512]
                        ins = tensor.matmul(tgt[:, :], lhsT, rhs,
                                            start=(hc == 0), stop=(hc == NCH - 1))
                    sT.inc(ins)
                    T_QKV[("V", idx)] = sT.n
            for oc in range(NH):
                s = 8 + oc // 4
                _sm, _c = slab_sem(s)
                tensor.wait_ge(_sm.sem, _c)
                if oc >= 2:
                    tensor.wait_ge(sV.sem, evQ(oc - 2))
                tgt = pqk[oc % 2]
                for hc in range(NCH):
                    lhsT = wsl(s % 2)[:, hc * 512 + (oc % 4) * 128:
                                      hc * 512 + (oc % 4) * 128 + 128]
                    st, sp = (hc == 0), (hc == NCH - 1)
                    tensor.matmul(tgt[:, 0:512], lhsT,
                                  lnx_sb[:, hc * R: hc * R + 512],
                                  start=st, stop=sp)
                    ins = tensor.matmul(tgt[:, 512:1024], lhsT,
                                        lnx_sb[:, hc * R + 512: hc * R + 1024],
                                        start=st, stop=sp)
                sT.inc(ins)
                T_QKV[("Q", oc)] = sT.n
        block.tensor(t3)

        def v6(vector):
            assert sV.n == EVB
            vector.wait_ge(sT.sem, M["pmm"])
            sV.inc(vector.tensor_copy(ws_sb[0:1, 0:4], pr_ps[0:1, 0:4]))
            assert sV.n == M["ws_evac"]
            for oc in range(NH):
                vector.wait_ge(sT.sem, T_QKV[("K", oc)])
                if oc >= 2:
                    _s, _c = WRK[oc - 2]
                    vector.wait_ge(sE[_s].sem, _c)
                sV.inc(vector.tensor_copy(ring2(evq_sb, oc, R), pqk[oc % 2][:, :]))
                assert sV.n == evK(oc)
            for idx in range(32):
                vector.wait_ge(sT.sem, T_QKV[("V", idx)])
                if idx >= 2:
                    _s, _c = WRV[idx - 2]
                    vector.wait_ge(sVW[_s].sem, _c)
                sV.inc(vector.tensor_copy(ring2(evv_sb, idx, 512),
                                          pv[idx % 2][:, :]))
                assert sV.n == evV(idx)
            for oc in range(NH):
                vector.wait_ge(sT.sem, T_QKV[("Q", oc)])
                if oc >= 2:
                    _s, _c = WRQ[oc - 2]
                    vector.wait_ge(sE[_s].sem, _c)
                sV.inc(vector.tensor_copy(ring2(evq_sb, oc, R), pqk[oc % 2][:, :]))
                assert sV.n == evQ(oc)
        block.vector(v6)

        def s3(sync):
            sync.wait_ge(sV.sem, M["ws_evac"])
            sync.dma_start(out=ws_loc.ap()[:, :],
                           in_=ws_sb[:, :]).then_inc(sWS.sem, 16)
            sWS.n += 16
            M["ws_store"] = sWS.n
            for oc in range(NH):
                sync.wait_ge(sV.sem, evK(oc))
                cl, hic = oc % 4, oc // 4
                _s, _c = WRK[oc]
                for j in range(2):
                    sync.dma_start(
                        out=kvKv[2 * cl + j, hic, :, :],
                        in_=ring2(evq_sb, oc, R),
                    ).then_inc(sE[_s].sem, 16)
                    sE[_s].n += 16
                assert sE[_s].n == _c
            for idx in range(32):
                vs, rsub = divmod(idx, 8)
                sync.wait_ge(sV.sem, evV(idx))
                _s, _c = WRV[idx]
                for j in range(2):
                    sync.dma_start(
                        out=kvVv[:, j, vs, :, rsub * 128:(rsub + 1) * 128
                                 ].transpose([1, 0, 2]),
                        in_=ring2(evv_sb, idx, 512).rearrange(
                            "p (n d) -> p n d", n=4),
                    ).then_inc(sVW[_s].sem, 16)
                    sVW[_s].n += 16
                assert sVW[_s].n == _c
            for oc in range(NH):
                sync.wait_ge(sV.sem, evQ(oc))
                cl, hic = oc % 4, oc // 4
                _s, _c = WRQ[oc]
                for j in range(2):
                    sync.dma_start(
                        out=qsv[2 * cl + j, hic, :, :],
                        in_=ring2(evq_sb, oc, R),
                    ).then_inc(sE[_s].sem, 16)
                    sE[_s].n += 16
                assert sE[_s].n == _c
        block.sync(s3)

        def g1(gpsimd):
            slab_last = {}
            for s in range(4):
                slab_last[s] = T_QKV[("K", s * 4 + 3)]
                slab_last[4 + s] = T_QKV[("V", s * 8 + 7)]
                slab_last[8 + s] = T_QKV[("Q", s * 4 + 3)]
            for s in range(12):
                if s >= 2:
                    gpsimd.wait_ge(sT.sem, slab_last[s - 2])
                if s < 4:
                    src = wkqT[:, s * 512:(s + 1) * 512]
                elif s < 8:
                    src = wvT[:, (s - 4) * 512:(s - 3) * 512]
                else:
                    src = wkqT[:, 2048 + (s - 8) * 512: 2048 + (s - 7) * 512]
                _sm, _c = slab_sem(s)
                gpsimd.dma_start(
                    out=wsl(s % 2).rearrange("p (c o) -> p c o", c=NCH),
                    in_=src.rearrange("(c p) o -> p c o", p=128),
                ).then_inc(_sm.sem, 16)
                _sm.n += 16
                assert _sm.n == _c
            _sm, _c = slab_sem(12)
            gpsimd.dma_start(out=bprojB[:, :], in_=bpj[:, :]).then_inc(_sm.sem, 16)
            _sm.n += 16
            assert _sm.n == _c
        block.gpsimd(g1)

        def g2(gpsimd):
            gpsimd.wait_ge(sWS.sem, M["ws_store"])
            gpsimd.collective_compute(
                "AllGather", AL.bypass, replica_groups=RG,
                ins=[ws_loc.ap().opt()], outs=[ws_g.ap().opt()],
            ).then_inc(sC.sem, 1)
            sC.n += 1
            M["ag_ws"] = sC.n
            gpsimd.wait_ge(sE[0].sem, WRK[14][1])
            gpsimd.wait_ge(sE[1].sem, WRK[15][1])
            gpsimd.collective_compute(
                "AllToAll", AL.bypass, replica_groups=RG,
                ins=[kv_sendK.ap().opt()], outs=[kv_stK.ap().opt()],
            ).then_inc(sC.sem, 1)
            sC.n += 1
            M["a2a_k"] = sC.n
            gpsimd.wait_ge(sVW[0].sem, WRV[30][1])
            gpsimd.wait_ge(sVW[1].sem, WRV[31][1])
            gpsimd.collective_compute(
                "AllToAll", AL.bypass, replica_groups=RG,
                ins=[kv_sendV.ap().opt()], outs=[kv_stV.ap().opt()],
            ).then_inc(sC.sem, 1)
            sC.n += 1
            M["a2a_v"] = sC.n
            gpsimd.wait_ge(sE[0].sem, WRQ[14][1])
            gpsimd.wait_ge(sE[1].sem, WRQ[15][1])
            gpsimd.collective_compute(
                "AllToAll", AL.bypass, replica_groups=RG,
                ins=[q_send.ap().opt()], outs=[q_stA.ap().opt()],
            ).then_inc(sC.sem, 1)
            sC.n += 1
            M["a2a_q"] = sC.n
            if debug:
                gpsimd.dma_start(out=dbg["parts"][:, :], in_=parts_sb[:, :]).then_inc(sG.sem, 16)
                sG.n += 16
                gpsimd.dma_start(out=dbg["p3"][:, :], in_=p3_sb[:, :]).then_inc(sG.sem, 16)
                sG.n += 16
                gpsimd.dma_start(out=dbg["ws"][:, :], in_=ws_sb[:, :]).then_inc(sG.sem, 16)
                sG.n += 16
                gpsimd.dma_start(out=dbg["lnx"][:, :], in_=lnx_sb[:, :]).then_inc(sG.sem, 16)
                sG.n += 16
                kvdbg = dbg["kvloc"].rearrange("(c h k) m -> k c h m", c=4, h=4)
                gpsimd.dma_start(
                    out=kvdbg[0, :, :, :],
                    in_=kv_sendK.ap().rearrange(
                        "(c j) h p s -> c j h (p s)", j=2)[:, 0],
                ).then_inc(sG.sem, 16)
                sG.n += 16
                gpsimd.dma_start(
                    out=kvdbg[1, :, :, :],
                    in_=kv_sendV.ap().rearrange(
                        "(c j) h p s -> c j h (p s)", j=2)[:, 0],
                ).then_inc(sG.sem, 16)
                sG.n += 16
                gpsimd.dma_start(
                    out=dbg["qloc"].rearrange("(c h) m -> c h m", c=4),
                    in_=q_send.ap().rearrange(
                        "(c j) h p s -> c j h (p s)", j=2)[:, 0],
                ).then_inc(sG.sem, 16)
                sG.n += 16
        block.gpsimd(g2)

        def s4(sync):
            sync.wait_ge(sC.sem, M["ag_ws"])
            sync.dma_start(out=wsg_sb[:, :],
                           in_=ws_g.ap().flatten().unsqueeze(0),
                           ).then_inc(sWS.sem, 16)
            sWS.n += 16
            M["wsg_loaded"] = sWS.n
        block.sync(s4)

        def v7(vector):
            vector.wait_ge(sWS.sem, M["wsg_loaded"])
            vector.tensor_reduce(
                ws_sb[0:1, :],
                wsg_sb[0:1, :].rearrange("a (r j) -> a j r", r=NCORES),
                axis=AX, op=AL.add)
            vector.tensor_copy(thrr_sb[0:1, 8:16], wsg_sb[0:1, 0:8])
            vector.tensor_copy(thrr_sb[0:1, 8:16], wsg_sb[0:1, 8:16])
            vector.tensor_scalar(thrr_sb[0:1, 0:4], ws_sb[0:1, 0:4],
                                 THR_SCL, THR_OFF, AL.mult, AL.add)
            sV.inc(vector.tensor_scalar(thrr_sb[0:1, 4:8], ws_sb[0:1, 0:4],
                                        -THR_SCL, -THR_OFF, AL.mult, AL.add))
            M["thr_rows"] = sV.n
        block.vector(v7)

        es_qkvp.close()
        es_attp = ExitStack()
        sc_ps = [es_attp.enter_context(nc.psum_tensor(f"sc{i}", [128, R], F32))
                 for i in range(2)]
        ctx_ps = es_attp.enter_context(nc.psum_tensor("ctx_ps", [128, R], F32))
        tail_ps = es_attp.enter_context(nc.psum_tensor("tail_ps", [128, R], F32))

        es_p1.close()
        es_p2 = ExitStack()
        ctxn_sb = es_p2.enter_context(nc.sbuf_tensor("ctxn_sb", [128, NH * R], BF16))
        wpsl_sb = es_p2.enter_context(nc.sbuf_tensor("wpsl_sb", [128, 2 * NCH * 256], BF16))
        es_att = ExitStack()
        kt_sb = es_att.enter_context(nc.sbuf_tensor("kt_sb", [128, 2 * SQ], BF16))
        vv_sb = es_att.enter_context(nc.sbuf_tensor("vv_sb", [128, 2 * SQ], BF16))
        qq_sb = es_att.enter_context(nc.sbuf_tensor("qq_sb", [128, 2 * R], BF16))
        pt_sb = es_att.enter_context(nc.sbuf_tensor("pt_sb", [128, 2 * R], BF16))
        run_sb = es_att.enter_context(nc.sbuf_tensor("run_sb", [128, R], BF16))
        rs_sb = es_att.enter_context(nc.sbuf_tensor("rs_sb", [128, R], F32))

        M["thr_bc"] = sT.n + 1

        def v8a(vector):
            vector.wait_ge(sT.sem, M["thr_bc"])
            sV.inc(vector.tensor_copy(thr_sb[:, :], tail_ps[:, 0:8]))
            M["thr_copy"] = sV.n
            vector.tensor_copy(thrr_sb[0:1, 8:16], wsg_sb[0:1, 0:8])
            vector.tensor_copy(thrr_sb[0:1, 8:16], wsg_sb[0:1, 8:16])
        block.vector(v8a)

        VQ0 = sV.n

        def VQ2(j):
            return VQ0 + 2 * j + 1

        def VQ3(j):
            return VQ0 + 2 * j + 2

        SQ2, LDQ, STQ = {}, {}, {}

        def STQF(j):
            return j % 2, 16 * (j // 2 + 1) + (sP[j % 2].n)

        _sp_base = [sP[0].n, sP[1].n]

        def qdst(j):
            m, pj = divmod(j, NPIECE)
            if m < 2:
                c, h0 = divmod(pj * PIECE, HU)
                blk = qw_loc.ap()[0:1, m * QW1:(m + 1) * QW1].squeeze(0).rearrange(
                    "(c p h) -> c p h", c=2, p=128)
                return blk[c:c + 1, :, h0:h0 + PIECE].squeeze(0), None
            cb = pj * 2
            base = 2 * QW1 + cb * 128 * 256
            return (qw_loc.ap()[0:1, base:base + 2 * 128 * 256].squeeze(
                0).rearrange("(c p o) -> c p o", c=2,
                             p=128).transpose([1, 0, 2]), "w3")

        def aq(scalar):
            scalar.wait_ge(sV.sem, M["absred_65"])
            for j in range(min(QRING, NP3)):
                sl = sQ[j % QRING]
                scalar.dma_start(out=wq_dst(j), in_=wpiece(j)).then_inc(sl.sem, 16)
                sl.n += 16
                LDQ[j] = (j % QRING, sl.n)
            for j in range(NP3):
                if j + QRING < NP3:
                    scalar.wait_ge(sV.sem, VQ2(j))
                    sl = sQ[(j + QRING) % QRING]
                    scalar.dma_start(out=wq_dst(j + QRING),
                                     in_=wpiece(j + QRING)).then_inc(sl.sem, 16)
                    sl.n += 16
                    LDQ[j + QRING] = ((j + QRING) % QRING, sl.n)
                scalar.wait_ge(sV.sem, VQ3(j))
                dst, kind = qdst(j)
                srcp = (ring2(qp_sb, j, PIECE) if kind is None else
                        ring2(qp_sb, j, PIECE).rearrange("p (c o) -> p c o", c=2))
                pl = sP[j % 2]
                scalar.dma_start(out=dst, in_=srcp).then_inc(pl.sem, 16)
                pl.n += 16
                _sl, _cn = STQF(j)
                STQ[j] = (_sl, pl.n)
                assert pl.n == _sp_base[_sl] + 16 * (j // 2 + 1)
        block.scalar(aq)

        def vq(vector):
            assert sV.n == VQ0
            for j in range(NP3):
                m = j // NPIECE
                _s, _c = LDQ[j]
                vector.wait_ge(sQ[_s].sem, _c)
                vector.tensor_scalar(ring2(qa_sb, j, PIECE), wqsl(j),
                                     thr_sb[:, m:m + 1], None, AL.is_gt)
                sV.inc(vector.tensor_scalar(ring2(qb_sb, j, PIECE), wqsl(j),
                                            thr_sb[:, 4 + m:5 + m], None,
                                            AL.is_lt))
                assert sV.n == VQ2(j)
                if j >= 2:
                    _ps, _pc = STQ[j - 2]
                    vector.wait_ge(sP[_ps].sem, _pc)
                sV.inc(vector.tensor_tensor(ring2(qp_sb, j, PIECE),
                                            ring2(qa_sb, j, PIECE),
                                            ring2(qb_sb, j, PIECE), AL.subtract))
                assert sV.n == VQ3(j)
        block.vector(vq)

        def g2b(gpsimd):
            gpsimd.wait_ge(sP[0].sem, sP[0].n)
            gpsimd.wait_ge(sP[1].sem, sP[1].n)
            gpsimd.collective_compute(
                "AllGather", AL.bypass, replica_groups=RG,
                ins=[qw_loc.ap().opt()], outs=[qw_g.ap().opt()],
            ).then_inc(sC.sem, 1)
            sC.n += 1
            M["ag_qw"] = sC.n
        block.gpsimd(g2b)

        def ld_g(g):
            return sGL[g % 2], 80 * (g // 2 + 1)

        S0 = sS.n

        def ES(p):
            return S0 + p + 1

        V0 = sV.n

        def VS(p):
            g, t = divmod(p, 16)
            return V0 + 18 * g + t + 1

        def VR(g):
            return V0 + 18 * g + 17

        def VN(g):
            return V0 + 18 * g + 18

        TS, TC, TR, TB, TP = {}, {}, {}, {}, {}
        EV2B = V0 + 18 * 16

        def EV2(oc):
            return EV2B + oc + 1

        def t4(tensor):
            tensor.wait_ge(sV.sem, M["thr_rows"])
            tensor.wait_ge(sV.sem, evQ(15))
            sT.inc(tensor.matmul(tail_ps[:, 0:8], ones_rf[0:1, :],
                                 thrr_sb[0:1, 0:8], start=True, stop=True))
            assert M["thr_bc"] == sT.n
            for g in range(NH):
                par = g % 2
                kt = kt_sb[:, par * SQ:(par + 1) * SQ]
                vv = vv_sb[:, par * SQ:(par + 1) * SQ]
                qv = qq_sb[:, par * R:(par + 1) * R]
                for t in range(16):
                    p = 16 * g + t
                    if t == 0:
                        _sm, _c = ld_g(g)
                        tensor.wait_ge(_sm.sem, _c)
                    if p == 0:
                        tensor.wait_ge(sV.sem, M["thr_copy"])
                    if p >= 2:
                        tensor.wait_ge(sS.sem, ES(p - 2))
                    sc = sc_ps[t % 2]
                    lhsT = kt[:, t * 128:(t + 1) * 128]
                    tensor.matmul(sc[:, 0:512], lhsT, qv[:, 0:512],
                                  start=True, stop=True)
                    sT.inc(tensor.matmul(sc[:, 512:1024], lhsT, qv[:, 512:1024],
                                         start=True, stop=True))
                    TS[p] = sT.n
                    if t >= 1:
                        tp = t - 1
                        pp_ = 16 * g + tp
                        tensor.wait_ge(sS.sem, ES(pp_))
                        if tp == 0:
                            _sm, _c = ld_g(g)
                            tensor.wait_ge(_sm.sem, _c)
                            if g >= 1:
                                tensor.wait_ge(sV.sem, VN(g - 1))
                        pt = ring2(pt_sb, tp, R)
                        lv = vv[:, tp * 128:(tp + 1) * 128]
                        st = (tp == 0)
                        tensor.matmul(ctx_ps[:, 0:512], lv, pt[:, 0:512],
                                      start=st, stop=False)
                        sT.inc(tensor.matmul(ctx_ps[:, 512:1024], lv,
                                             pt[:, 512:1024], start=st, stop=False))
                        TC[pp_] = sT.n
                p15 = 16 * g + 15
                tensor.wait_ge(sS.sem, ES(p15))
                pt = ring2(pt_sb, 15, R)
                lv = vv[:, 15 * 128:16 * 128]
                tensor.matmul(ctx_ps[:, 0:512], lv, pt[:, 0:512],
                              start=False, stop=True)
                sT.inc(tensor.matmul(ctx_ps[:, 512:1024], lv, pt[:, 512:1024],
                                     start=False, stop=True))
                TC[p15] = sT.n
                tensor.wait_ge(sV.sem, VS(p15))
                tensor.matmul(tail_ps[0:1, 0:512], ones_cb[:, 0:1],
                              run_sb[:, 0:512], start=True, stop=True)
                sT.inc(tensor.matmul(tail_ps[0:1, 512:1024], ones_cb[:, 0:1],
                                     run_sb[:, 512:1024], start=True, stop=True))
                TR[g] = sT.n
                tensor.wait_ge(sV.sem, VR(g))
                tensor.matmul(tail_ps[:, 0:512], ones_rf[0:1, :],
                              row(ROW_T)[0:1, 0:512], start=True, stop=True)
                sT.inc(tensor.matmul(tail_ps[:, 512:1024], ones_rf[0:1, :],
                                     row(ROW_T)[0:1, 512:1024],
                                     start=True, stop=True))
                TB[g] = sT.n
            for oc in range(NCH):
                _sm, _c = slab_sem(13 + oc // 2)
                tensor.wait_ge(_sm.sem, _c)
                if oc == 0:
                    tensor.wait_ge(sV.sem, VN(NH - 1))
                    tensor.wait_ge(sS.sem, ES(255))
                if oc >= 2:
                    tensor.wait_ge(sV.sem, EV2(oc - 2))
                tgt = sc_ps[oc % 2]
                for np_ in range(NH):
                    lhsT = wpsl_sb[:, ((oc // 2) % 2) * NCH * 256 +
                                   np_ * 256 + (oc % 2) * 128:
                                   ((oc // 2) % 2) * NCH * 256 +
                                   np_ * 256 + (oc % 2) * 128 + 128]
                    st = (np_ == 0)
                    tensor.matmul(tgt[:, 0:512], lhsT,
                                  ctxn_sb[:, np_ * R: np_ * R + 512],
                                  start=st, stop=False)
                    tensor.matmul(tgt[:, 512:1024], lhsT,
                                  ctxn_sb[:, np_ * R + 512: np_ * R + 1024],
                                  start=st, stop=False)
                bl = bprojB[0:1, oc * 128:(oc + 1) * 128]
                tensor.matmul(tgt[:, 0:512], bl, ones_rb[0:1, 0:512],
                              start=False, stop=True)
                sT.inc(tensor.matmul(tgt[:, 512:1024], bl, ones_rb[0:1, 512:1024],
                                     start=False, stop=True))
                TP[oc] = sT.n
        block.tensor(t4)

        def a3(scalar):
            assert sS.n == S0
            for g in range(NH):
                for t in range(16):
                    p = 16 * g + t
                    scalar.wait_ge(sT.sem, TS[p])
                    if p >= 2:
                        scalar.wait_ge(sT.sem, TC[p - 2])
                        scalar.wait_ge(sV.sem, VS(p - 2))
                    sS.inc(scalar.activation(
                        out=ring2(pt_sb, t, R), in_=sc_ps[t % 2][:, :],
                        func=AF.Exp, bias=SM_BIAS, scale=SM_SCALE))
                    assert sS.n == ES(p)
        block.scalar(a3)

        def v8(vector):
            assert sV.n == V0
            for g in range(NH):
                for t in range(16):
                    p = 16 * g + t
                    vector.wait_ge(sS.sem, ES(p))
                    if t == 0:
                        sV.inc(vector.tensor_copy(run_sb[:, :], ring2(pt_sb, t, R)))
                    else:
                        sV.inc(vector.tensor_tensor(run_sb[:, :], run_sb[:, :],
                                                    ring2(pt_sb, t, R), AL.add))
                    assert sV.n == VS(p)
                vector.wait_ge(sT.sem, TR[g])
                sV.inc(vector.reciprocal(row(ROW_T), tail_ps[0:1, :]))
                assert sV.n == VR(g)
                vector.wait_ge(sT.sem, TB[g])
                vector.tensor_copy(rs_sb[:, :], tail_ps[:, :])
                sV.inc(vector.tensor_tensor(
                    ctxn_sb[:, g * R:(g + 1) * R], ctx_ps[:, :], rs_sb[:, :],
                    AL.mult))
                assert sV.n == VN(g)
        block.vector(v8)

        def s_att(sync):
            reg = sync.alloc_register("gidx")
            sync.wait_ge(sX.sem, M["x_loaded"])
            sync.wait_ge(sC.sem, M["a2a_q"])
            sync.reg_load(reg, ctl_sb[0:1, 0:1])
            hsel = sync.snap(reg, min_val=0, max_val=1)
            q2A = q_stA.ap().rearrange("g h p s -> g (h p s)").rearrange(
                "(a j) m -> j a m", j=2)
            sync.dma_start(out=q_st.ap().rearrange("b h p s -> b (h p s)"),
                           in_=q2A[ds(hsel, 1), :, :].squeeze(0),
                           ).then_inc(sST.sem, 16)
            sST.n += 16
            sync.wait_ge(sST.sem, sST.n)
            kstK = kv_stK.ap()
            kstV = kv_stV.ap()
            qst = q_st.ap()
            for g in range(NH):
                par = g % 2
                a, b = g // 4, g % 4
                if g >= 2:
                    sync.wait_ge(sT.sem, TC[16 * (g - 2) + 15])
                sl = sGL[par]
                for t in range(2):
                    sync.dma_start(
                        out=kt_sb[:, par * SQ + t * R: par * SQ + (t + 1) * R],
                        in_=kstK[2 * a + t, b, :, :],
                    ).then_inc(sl.sem, 16)
                    sl.n += 16
                    sync.dma_start(
                        out=vv_sb[:, par * SQ + t * R: par * SQ + (t + 1) * R],
                        in_=kstV[2 * a + t, b, :, :],
                    ).then_inc(sl.sem, 16)
                    sl.n += 16
                sync.dma_start(
                    out=qq_sb[:, par * R:(par + 1) * R],
                    in_=qst[a, b, :, :],
                ).then_inc(sl.sem, 16)
                sl.n += 16
                assert (sl.n) == ld_g(g)[1]
        block.sync(s_att)

        def g3(gpsimd):
            for k in range(8):
                s = 13 + k
                if k >= 2:
                    gpsimd.wait_ge(sT.sem, TP[2 * (k - 2) + 1])
                _sm, _c = slab_sem(s)
                gpsimd.dma_start(
                    out=wpsl_sb[:, (k % 2) * NCH * 256:(k % 2 + 1) * NCH * 256
                                ].rearrange("p (c o) -> p c o", c=NCH),
                    in_=wpTt[:, k * 256:(k + 1) * 256].rearrange(
                        "(c p) o -> p c o", p=128),
                ).then_inc(_sm.sem, 16)
                _sm.n += 16
                assert _sm.n == _c
        block.gpsimd(g3)

        es_att.close()
        es_p2b = ExitStack()
        attn_sb = es_p2b.enter_context(nc.sbuf_tensor("attn_sb", [128, NCH * R], BF16))

        def v9(vector):
            assert sV.n == EV2B
            for oc in range(NCH):
                vector.wait_ge(sT.sem, TP[oc])
                sV.inc(vector.tensor_copy(attn_sb[:, oc * R:(oc + 1) * R],
                                          sc_ps[oc % 2][:, :]))
                assert sV.n == EV2(oc)
        block.vector(v9)

        ss2 = ctx_ps
        qq2 = tail_ps

        def a5(scalar):
            for c in range(NCH):
                scalar.wait_ge(sV.sem, EV2(c))
                if c >= 2:
                    scalar.wait_ge(sT.sem, M.get(f"st2_{c - 2}", 0))
                sS.inc(scalar.activation(out=ring2(sq_sb, c, R),
                                         in_=attn_sb[:, c * R:(c + 1) * R],
                                         func=AF.Square))
                M[f"sq2_{c}"] = sS.n
        T5B = sT.n
        for c in range(NCH):
            M[f"st2_{c}"] = T5B + c + 1
        block.scalar(a5)

        def t5(tensor):
            assert sT.n == T5B
            for c in range(NCH):
                tensor.wait_ge(sV.sem, EV2(c))
                tensor.wait_ge(sS.sem, M[f"sq2_{c}"])
                if c == 0:
                    tensor.wait_ge(sV.sem, VN(NH - 1))
                ab = attn_sb[:, c * R:(c + 1) * R]
                sq = ring2(sq_sb, c, R)
                st, sp = (c == 0), (c == NCH - 1)
                tensor.matmul(ss2[0:1, 0:512], ones_cb[:, 0:1], ab[:, 0:512],
                              start=st, stop=sp)
                tensor.matmul(ss2[0:1, 512:1024], ones_cb[:, 0:1], ab[:, 512:1024],
                              start=st, stop=sp)
                tensor.matmul(qq2[0:1, 0:512], ones_cb[:, 0:1], sq[:, 0:512],
                              start=st, stop=sp)
                ins = tensor.matmul(qq2[0:1, 512:1024], ones_cb[:, 0:1],
                                    sq[:, 512:1024], start=st, stop=sp)
                sT.inc(ins)
                assert sT.n == M[f"st2_{c}"]
        block.tensor(t5)

        def v10(vector):
            vector.wait_ge(sT.sem, M[f"st2_{NCH - 1}"])
            rows_part1(vector, ss2, qq2, "rows2")
        block.vector(v10)

        def a6(scalar):
            scalar.wait_ge(sV.sem, M["rows2"])
            sS.inc(scalar.sqrt(row(ROW_RSQ), row(ROW_RSQ)))
            M["sqrt2"] = sS.n
        block.scalar(a6)

        m2b = ctx_ps
        a2b = tail_ps

        def t6(tensor):
            tensor.wait_ge(sS.sem, M["sqrt2"])
            tensor.matmul(m2b[:, 0:512], ones_rf[0:1, :], rowh(ROW_M, 0),
                          start=True, stop=True)
            tensor.matmul(m2b[:, 512:1024], ones_rf[0:1, :], rowh(ROW_M, 1),
                          start=True, stop=True)
            tensor.matmul(a2b[:, 0:512], ones_rf[0:1, :], rowh(ROW_RSQ, 0),
                          start=True, stop=True)
            sT.inc(tensor.matmul(a2b[:, 512:1024], ones_rf[0:1, :],
                                 rowh(ROW_RSQ, 1), start=True, stop=True))
            M["bc2"] = sT.n
        block.tensor(t6)

        def v12(vector):
            vector.wait_ge(sT.sem, M["bc2"])
            for c in range(NCH):
                ab = attn_sb[:, c * R:(c + 1) * R]
                xs = xT_sb[:, c * R:(c + 1) * R]
                vector.tensor_tensor(ln_tmp[:, :], ab, m2b[:, :], AL.subtract)
                vector.tensor_tensor(ln_tmp[:, :], ln_tmp[:, :], a2b[:, :], AL.mult)
                sV.inc(vector.tensor_tensor(xs, xs, ln_tmp[:, :], AL.add))
                M[f"xu2_{c}"] = sV.n
            M["x2done"] = sV.n
        block.vector(v12)

        if debug:
            def g_dbg1(gpsimd):
                gpsimd.wait_ge(sWS.sem, M["wsg_loaded"])
                gpsimd.dma_start(out=dbg["wsg"][:, :], in_=wsg_sb[:, :]).then_inc(sG.sem, 16)
                sG.n += 16
                gpsimd.wait_ge(sV.sem, M["thr_copy"])
                gpsimd.dma_start(out=dbg["thr"][:, :], in_=thr_sb[:, :]).then_inc(sG.sem, 16)
                sG.n += 16
                gpsimd.wait_ge(sV.sem, VN(NH - 1))
                gpsimd.dma_start(out=dbg["ctxn"][:, :], in_=ctxn_sb[:, :]).then_inc(sG.sem, 16)
                sG.n += 16
                gpsimd.wait_ge(sV.sem, EV2(NCH - 1))
                gpsimd.dma_start(out=dbg["attn"][:, :], in_=attn_sb[:, :]).then_inc(sG.sem, 16)
                sG.n += 16
                gpsimd.wait_ge(sP[0].sem, sP[0].n)
                gpsimd.wait_ge(sP[1].sem, sP[1].n)
                gpsimd.dma_start(out=dbg["qwloc"][:, :], in_=qw_loc.ap()[:, :]).then_inc(sG.sem, 16)
                sG.n += 16
                gpsimd.wait_ge(sV.sem, M["x2done"])
                gpsimd.dma_start(out=dbg["x1"][:, :], in_=xT_sb[:, :]).then_inc(sG.sem, 16)
                sG.n += 16
            block.gpsimd(g_dbg1)

        es_attp.close()
        es_p2b.close()
        es_p2.close()
        es_ln3p = ExitStack()
        ss3 = es_ln3p.enter_context(nc.psum_tensor("ss3", [1, R], F32))
        qq3 = es_ln3p.enter_context(nc.psum_tensor("qq3", [1, R], F32))

        T7B = sT.n
        for c in range(NCH):
            M[f"st3_{c}"] = T7B + c + 1

        def a7(scalar):
            for c in range(NCH):
                scalar.wait_ge(sV.sem, M[f"xu2_{c}"])
                if c >= 2:
                    scalar.wait_ge(sT.sem, M[f"st3_{c - 2}"])
                sS.inc(scalar.activation(out=ring2(sq_sb, c, R),
                                         in_=xT_sb[:, c * R:(c + 1) * R],
                                         func=AF.Square))
                M[f"sq3_{c}"] = sS.n
        block.scalar(a7)

        def v13(vector):
            for c in range(NCH):
                if c >= 2:
                    vector.wait_ge(sT.sem, M[f"st3_{c - 2}"])
                sV.inc(vector.tensor_copy(ring2(xb_sb, c, R),
                                          xT_sb[:, c * R:(c + 1) * R]))
                M[f"cast3_{c}"] = sV.n
        block.vector(v13)

        def t7(tensor):
            assert sT.n == T7B
            for c in range(NCH):
                tensor.wait_ge(sV.sem, M[f"cast3_{c}"])
                tensor.wait_ge(sS.sem, M[f"sq3_{c}"])
                xb = ring2(xb_sb, c, R)
                sq = ring2(sq_sb, c, R)
                st, sp = (c == 0), (c == NCH - 1)
                tensor.matmul(ss3[0:1, 0:512], ones_cb[:, 0:1], xb[:, 0:512],
                              start=st, stop=sp)
                tensor.matmul(ss3[0:1, 512:1024], ones_cb[:, 0:1], xb[:, 512:1024],
                              start=st, stop=sp)
                tensor.matmul(qq3[0:1, 0:512], ones_cb[:, 0:1], sq[:, 0:512],
                              start=st, stop=sp)
                ins = tensor.matmul(qq3[0:1, 512:1024], ones_cb[:, 0:1],
                                    sq[:, 512:1024], start=st, stop=sp)
                sT.inc(ins)
                assert sT.n == M[f"st3_{c}"]
        block.tensor(t7)

        def v14(vector):
            vector.wait_ge(sT.sem, M[f"st3_{NCH - 1}"])
            rows_part1(vector, ss3, qq3, "rows3")
        block.vector(v14)

        def a8(scalar):
            scalar.wait_ge(sV.sem, M["rows3"])
            sS.inc(scalar.sqrt(row(ROW_RSQ3), row(ROW_RSQ)))
            M["sqrt3"] = sS.n
        block.scalar(a8)

        def v14b(vector):
            vector.wait_ge(sS.sem, M["sqrt3"])
            sV.inc(vector.scalar_tensor_tensor(row(ROW_B3), row(ROW_M), NEG1,
                                               row(ROW_RSQ3), AL.mult, AL.mult))
            M["rows3b"] = sV.n
        block.vector(v14b)

        es_ln3p.close()

        qw12 = qwgv[:, 0:QW1].rearrange("r (u p h) -> r u p h", u=2, p=128)
        qw22 = qwgv[:, QW1:2 * QW1].rearrange("r (u p h) -> r u p h", u=2, p=128)
        qw3v = qwgv[:, 2 * QW1:3 * QW1].rearrange("r (hc p o) -> r hc p o",
                                                  hc=HUC, p=128)
        qw12p = qw12.transpose([2, 0, 1, 3])
        qw22p = qw22.transpose([2, 0, 1, 3])
        es_qu.close()
        XUF = {}
        for rh in range(2):
            es_rh = ExitStack()
            hid_sb = es_rh.enter_context(
                nc.sbuf_tensor(f"hid_{rh}", [128, HUC * 512], BF16))
            q3sl = es_rh.enter_context(
                nc.sbuf_tensor(f"q3sl_{rh}", [128, 2 * HUC * 128], BF16))
            es_rha = ExitStack()
            lnx3_sb = es_rha.enter_context(
                nc.sbuf_tensor(f"lnx3_{rh}", [128, NCH * 512], BF16))
            sg_sb = es_rha.enter_context(
                nc.sbuf_tensor(f"sg_{rh}", [128, 2 * 512], BF16))
            xe_sb = es_rha.enter_context(
                nc.sbuf_tensor(f"xe_{rh}", [128, 2 * 512], BF16))
            q1sl = es_rha.enter_context(
                nc.sbuf_tensor(f"q1sl_{rh}", [128, 2 * NCH * 128], BF16))
            q2sl = es_rha.enter_context(
                nc.sbuf_tensor(f"q2sl_{rh}", [128, 2 * NCH * 128], BF16))
            es_rhp = ExitStack()
            a3b = es_rhp.enter_context(nc.psum_tensor(f"a3b_{rh}", [128, 512], F32))
            b3b = es_rhp.enter_context(nc.psum_tensor(f"b3b_{rh}", [128, 512], F32))
            px1 = [es_rhp.enter_context(nc.psum_tensor(f"px1{i}_{rh}", [128, 512], F32))
                   for i in range(2)]
            px2 = [es_rhp.enter_context(nc.psum_tensor(f"px2{i}_{rh}", [128, 512], F32))
                   for i in range(2)]

            def t8(tensor, rh=rh, a3b=a3b, b3b=b3b):
                tensor.wait_ge(sV.sem, M["rows3b"])
                if rh == 1:
                    tensor.wait_ge(sV.sem, XUF[0])
                tensor.matmul(a3b[:, :], ones_rf[0:1, :], rowh(ROW_RSQ3, rh),
                              start=True, stop=True)
                sT.inc(tensor.matmul(b3b[:, :], ones_rf[0:1, :], rowh(ROW_B3, rh),
                                     start=True, stop=True))
                M[f"bc3_{rh}"] = sT.n
            block.tensor(t8)

            def v15(vector, rh=rh, lnx3_sb=lnx3_sb, a3b=a3b, b3b=b3b):
                vector.wait_ge(sT.sem, M[f"bc3_{rh}"])
                for c in range(NCH):
                    xs = xT_sb[:, c * R + rh * 512: c * R + (rh + 1) * 512]
                    vector.tensor_tensor(ln_tmp[:, 0:512], xs, a3b[:, :], AL.mult)
                    sV.inc(vector.tensor_tensor(
                        lnx3_sb[:, c * 512:(c + 1) * 512], ln_tmp[:, 0:512],
                        b3b[:, :], AL.add))
                    M[f"lnx3_{rh}_{c}"] = sV.n
            block.vector(v15)

            T9B = sT.n
            S9B = sS.n
            V9B = sV.n

            def TX2(hc):
                return T9B + 2 * hc + 2

            def SIG(hc):
                return S9B + hc + 1

            def VH(hc):
                return V9B + hc + 1

            def s6(sync, rh=rh, q1sl=q1sl, q2sl=q2sl):
                for hc in range(HUC):
                    if rh == 0 and hc == 0:
                        sync.wait_ge(sC.sem, M["ag_qw"])
                        sync.wait_ge(sV.sem, M["x2done"])
                    if rh == 1 and hc == 0:
                        sync.wait_ge(sT.sem, M["w3_last_0"])
                    if hc >= 2:
                        sync.wait_ge(sT.sem, TX2(hc - 2))
                    fl = sF[hc % 2]
                    for (sl, qv2) in ((q1sl, qw12p), (q2sl, qw22p)):
                        for u in range(2):
                            sync.dma_start(
                                out=ring2(sl, hc, NCH * 128).rearrange(
                                    "p (r u h) -> p r u h", r=8, u=2
                                )[:, :, u, :],
                                in_=qv2[:, :, u, hc * 128:(hc + 1) * 128],
                            ).then_inc(fl.sem, 16)
                            fl.n += 16
                    M[f"q12_{rh}_{hc}"] = (hc % 2, fl.n)
            block.sync(s6)

            def t9(tensor, rh=rh, q1sl=q1sl, q2sl=q2sl, lnx3_sb=lnx3_sb,
                   px1=px1, px2=px2):
                assert sT.n == T9B
                for hc in range(HUC):
                    _s, _c = M[f"q12_{rh}_{hc}"]
                    tensor.wait_ge(sF[_s].sem, _c)
                    if hc == 0:
                        tensor.wait_ge(sV.sem, M[f"lnx3_{rh}_{NCH - 1}"])
                    if hc >= 2:
                        tensor.wait_ge(sS.sem, SIG(hc - 2))
                        tensor.wait_ge(sV.sem, VH(hc - 2))
                    for (sl, ps) in ((q1sl, px1[hc % 2]), (q2sl, px2[hc % 2])):
                        for dc in range(NCH):
                            lhsT = ring2(sl, hc, NCH * 128)[:, dc * 128:(dc + 1) * 128]
                            ins = tensor.matmul(
                                ps[:, :], lhsT,
                                lnx3_sb[:, dc * 512:(dc + 1) * 512],
                                start=(dc == 0), stop=(dc == NCH - 1))
                        sT.inc(ins)
                    assert sT.n == TX2(hc)
            block.tensor(t9)

            def a9(scalar, rh=rh, sg_sb=sg_sb, px1=px1):
                assert sS.n == S9B
                for hc in range(HUC):
                    scalar.wait_ge(sT.sem, TX2(hc) - 1)
                    sS.inc(scalar.activation(out=ring2(sg_sb, hc, 512),
                                             in_=px1[hc % 2][:, :], func=AF.Sigmoid))
                    assert sS.n == SIG(hc)
            block.scalar(a9)

            def v16(vector, rh=rh, sg_sb=sg_sb, xe_sb=xe_sb, hid_sb=hid_sb,
                    px2=px2):
                assert sV.n == V9B
                for hc in range(HUC):
                    vector.wait_ge(sT.sem, TX2(hc))
                    vector.wait_ge(sS.sem, SIG(hc))
                    vector.tensor_copy(ring2(xe_sb, hc, 512), px2[hc % 2][:, :])
                    sV.inc(vector.tensor_tensor(
                        hid_sb[:, hc * 512:(hc + 1) * 512], ring2(sg_sb, hc, 512),
                        ring2(xe_sb, hc, 512), AL.mult))
                    assert sV.n == VH(hc)
            block.vector(v16)

            es_rhp.close()
            es_rha.close()
            es_w3s = ExitStack()
            y_sb = es_w3s.enter_context(
                nc.sbuf_tensor(f"y_sb_{rh}", [128, NCH * 512], BF16))
            es_w3p = ExitStack()
            py = [es_w3p.enter_context(nc.psum_tensor(f"py{i}_{rh}", [128, 512], F32))
                  for i in range(2)]
            ss4 = es_w3p.enter_context(nc.psum_tensor(f"ss4_{rh}", [1, 512], F32))
            qq4 = es_w3p.enter_context(nc.psum_tensor(f"qq4_{rh}", [1, 512], F32))

            T10B = sT.n

            def TY(oc):
                return T10B + oc + 1

            V17B = sV.n

            def EV4(oc):
                return V17B + oc + 1

            def s7(sync, rh=rh, q3sl=q3sl):
                for oc in range(NCH):
                    if oc == 0 and rh == 1:
                        sync.wait_ge(sT.sem, M["w3_last_0"])
                    if oc >= 2:
                        sync.wait_ge(sT.sem, TY(oc - 2))
                    src = qw3v[oc // 2:oc // 2 + 1, :, :,
                               (oc % 2) * 128:(oc % 2) * 128 + 128]
                    q3l = sQ3[oc % 2]
                    for hh in range(2):
                        sync.dma_start(
                            out=ring2(q3sl, oc, HUC * 128)[
                                :, hh * 22 * 128:(hh + 1) * 22 * 128].rearrange(
                                "p (hc o) -> p hc o", hc=22),
                            in_=src[:, hh * 22:(hh + 1) * 22].squeeze(
                                0).transpose([1, 0, 2]),
                        ).then_inc(q3l.sem, 16)
                        q3l.n += 16
                    M[f"q3_{rh}_{oc}"] = (oc % 2, q3l.n)
            block.sync(s7)

            def t10(tensor, rh=rh, q3sl=q3sl, hid_sb=hid_sb, py=py):
                assert sT.n == T10B
                for oc in range(NCH):
                    _s, _c = M[f"q3_{rh}_{oc}"]
                    tensor.wait_ge(sQ3[_s].sem, _c)
                    if oc == 0:
                        tensor.wait_ge(sV.sem, VH(HUC - 1))
                    if oc >= 2:
                        tensor.wait_ge(sV.sem, EV4(oc - 2))
                    for hc in range(HUC):
                        lhsT = ring2(q3sl, oc, HUC * 128)[:, hc * 128:(hc + 1) * 128]
                        ins = tensor.matmul(py[oc % 2][:, :], lhsT,
                                            hid_sb[:, hc * 512:(hc + 1) * 512],
                                            start=(hc == 0), stop=(hc == HUC - 1))
                    sT.inc(ins)
                    assert sT.n == TY(oc)
                M[f"w3_last_{rh}"] = sT.n
            block.tensor(t10)

            if debug and rh == 0:
                def g_dbg2(gpsimd, hid_sb=hid_sb):
                    gpsimd.wait_ge(sV.sem, VH(HUC - 1))
                    gpsimd.dma_start(out=dbg["hid"][:, :], in_=hid_sb[:, :]).then_inc(sG.sem, 16)
                    sG.n += 16
                block.gpsimd(g_dbg2)

            def v17(vector, rh=rh, y_sb=y_sb, py=py):
                assert sV.n == V17B
                for oc in range(NCH):
                    vector.wait_ge(sT.sem, TY(oc))
                    sV.inc(vector.tensor_copy(y_sb[:, oc * 512:(oc + 1) * 512],
                                              py[oc % 2][:, :]))
                    assert sV.n == EV4(oc)
            block.vector(v17)

            if debug and rh == 0:
                def g_dbg2b(gpsimd, q3sl=q3sl, c0=32, c1=32):
                    gpsimd.wait_ge(sQ3[0].sem, c0)
                    gpsimd.wait_ge(sQ3[1].sem, c1)
                    gpsimd.dma_start(out=dbg["q3sl"][:, :], in_=q3sl[:, :]).then_inc(sG.sem, 16)
                    sG.n += 16
                block.gpsimd(g_dbg2b)

                def g_dbg3(gpsimd, y_sb=y_sb):
                    gpsimd.wait_ge(sV.sem, EV4(NCH - 1))
                    gpsimd.dma_start(out=dbg["y"][:, :], in_=y_sb[:, :]).then_inc(sG.sem, 16)
                    sG.n += 16
                block.gpsimd(g_dbg3)

            T11B = sT.n
            for c in range(NCH):
                M[f"st4_{rh}_{c}"] = T11B + c + 1

            def a10(scalar, rh=rh, y_sb=y_sb):
                for c in range(NCH):
                    scalar.wait_ge(sV.sem, EV4(c))
                    if c >= 2:
                        scalar.wait_ge(sT.sem, M[f"st4_{rh}_{c - 2}"])
                    sS.inc(scalar.activation(out=ring2(sq_sb, c, R)[:, 0:512],
                                             in_=y_sb[:, c * 512:(c + 1) * 512],
                                             func=AF.Square))
                    M[f"sq4_{rh}_{c}"] = sS.n
            block.scalar(a10)

            def t11(tensor, rh=rh, y_sb=y_sb, ss4=ss4, qq4=qq4):
                assert sT.n == T11B
                for c in range(NCH):
                    tensor.wait_ge(sV.sem, EV4(c))
                    tensor.wait_ge(sS.sem, M[f"sq4_{rh}_{c}"])
                    st, sp = (c == 0), (c == NCH - 1)
                    tensor.matmul(ss4[0:1, :], ones_cb[:, 0:1],
                                  y_sb[:, c * 512:(c + 1) * 512],
                                  start=st, stop=sp)
                    ins = tensor.matmul(qq4[0:1, :], ones_cb[:, 0:1],
                                        ring2(sq_sb, c, R)[:, 0:512],
                                        start=st, stop=sp)
                    sT.inc(ins)
                    assert sT.n == M[f"st4_{rh}_{c}"]
            block.tensor(t11)

            def v18(vector, rh=rh, ss4=ss4, qq4=qq4):
                vector.wait_ge(sT.sem, M[f"st4_{rh}_{NCH - 1}"])
                rows_part1_w(vector, ss4, qq4, f"rows4_{rh}", 512)
            block.vector(v18)

            def a11(scalar, rh=rh):
                scalar.wait_ge(sV.sem, M[f"rows4_{rh}"])
                sS.inc(scalar.sqrt(rows[0:1, ROW_RSQ * R:ROW_RSQ * R + 512],
                                   rows[0:1, ROW_RSQ * R:ROW_RSQ * R + 512]))
                M[f"sqrt4_{rh}"] = sS.n
            block.scalar(a11)

            es_bc4 = ExitStack()
            m4b = es_bc4.enter_context(nc.psum_tensor(f"m4b_{rh}", [128, 512], F32))
            a4b = es_bc4.enter_context(nc.psum_tensor(f"a4b_{rh}", [128, 512], F32))

            def t12(tensor, rh=rh, m4b=m4b, a4b=a4b):
                tensor.wait_ge(sS.sem, M[f"sqrt4_{rh}"])
                tensor.matmul(m4b[:, :], ones_rf[0:1, :],
                              rows[0:1, ROW_M * R:ROW_M * R + 512],
                              start=True, stop=True)
                sT.inc(tensor.matmul(a4b[:, :], ones_rf[0:1, :],
                                     rows[0:1, ROW_RSQ * R:ROW_RSQ * R + 512],
                                     start=True, stop=True))
                M[f"bc4_{rh}"] = sT.n
            block.tensor(t12)

            def v19(vector, rh=rh, y_sb=y_sb, m4b=m4b, a4b=a4b):
                vector.wait_ge(sT.sem, M[f"bc4_{rh}"])
                for c in range(NCH):
                    ys = y_sb[:, c * 512:(c + 1) * 512]
                    xs = xT_sb[:, c * R + rh * 512: c * R + (rh + 1) * 512]
                    vector.tensor_tensor(ln_tmp[:, 0:512], ys, m4b[:, :],
                                         AL.subtract)
                    vector.tensor_tensor(ln_tmp[:, 0:512], ln_tmp[:, 0:512],
                                         a4b[:, :], AL.mult)
                    sV.inc(vector.tensor_tensor(xs, xs, ln_tmp[:, 0:512], AL.add))
                XUF[rh] = sV.n
            block.vector(v19)

            es_bc4.close()
            es_w3p.close()
            es_w3s.close()
            es_rh.close()

        def g5(gpsimd):
            gpsimd.wait_ge(sV.sem, XUF[1])
            gpsimd.dma_start(
                out=yT.rearrange("(c p) r -> p c r", p=128),
                in_=xT_sb[:, :].rearrange("p (c r) -> p c r", c=NCH),
            ).then_inc(sG.sem, 16)
            sG.n += 16
            gpsimd.wait_ge(sG.sem, sG.n)
        block.gpsimd(g5)

        es_qu.close()

    return nc


_NC_CACHE = {}
_LAST_RES = None


def _get_nc():
    if "nc" not in _NC_CACHE:
        _NC_CACHE["nc"] = build_program()
    return _NC_CACHE["nc"]


def kernel(L0_attn_query, L0_attn_key, L0_attn_value, L0_attn_Wqkv,
           L0_attn_Wproj, L0_attn_bproj, L0_ffn_w1, L0_ffn_w2, L0_ffn_w3,
           **_unused):
    x = np.asarray(L0_attn_query)
    Wqkv = np.asarray(L0_attn_Wqkv)
    Wproj = np.asarray(L0_attn_Wproj)
    bproj = np.asarray(L0_attn_bproj)
    w1 = np.asarray(L0_ffn_w1)
    w2 = np.asarray(L0_ffn_w2)
    w3 = np.asarray(L0_ffn_w3)

    Wq3 = Wqkv.reshape(NH, 3, 128, H)
    WK = np.ascontiguousarray(Wq3[:, 1].reshape(NH * 128, H).T)
    WQ = np.ascontiguousarray(Wq3[:, 0].reshape(NH * 128, H).T)
    wkqT = np.ascontiguousarray(np.concatenate([WK, WQ], axis=1))
    wvT = np.ascontiguousarray(Wq3[:, 2].reshape(NH * 128, H).T)
    wpT = np.ascontiguousarray(Wproj.T)
    w1T = np.ascontiguousarray(w1.T)
    w2T = np.ascontiguousarray(w2.T)
    w3T = np.ascontiguousarray(w3.T)

    in_maps = []
    for c in range(NCORES):
        b, half = c // 2, c % 2
        s0 = half * R
        ctl = np.zeros((1, 128), dtype=np.uint32)
        ctl[0, 0] = half
        in_maps.append({
            "xT": np.ascontiguousarray(x[b, s0:s0 + R, :].T),
            "wkqT": wkqT,
            "wvT": wvT,
            "wpT": wpT,
            "bpj": np.ascontiguousarray(bproj.reshape(1, H)),
            "w1s": np.ascontiguousarray(w1T[c * 256:(c + 1) * 256, :]),
            "w2s": np.ascontiguousarray(w2T[c * 256:(c + 1) * 256, :]),
            "w3s": np.ascontiguousarray(w3T[:, c * 256:(c + 1) * 256]),
            "ctl": ctl,
        })

    nc = _get_nc()
    res = run_bass_kernel_spmd(nc, in_maps, core_ids=list(range(NCORES)))
    global _LAST_RES
    _LAST_RES = res
    out = np.empty((4, SQ, H), dtype=np.float32)
    for c in range(NCORES):
        b, half = c // 2, c % 2
        out[b, half * R:(half + 1) * R, :] = res.results[c]["yT"].T
    return out


# version 07 (final)
# speedup vs baseline: 1.2985x; 1.0484x over previous
import os
os.environ.setdefault("TRN_ALLOW_ICI_ALL_TO_ALL", "1")

import numpy as np
from contextlib import ExitStack

import concourse.bass as bass
import concourse.mybir as mybir
from concourse.bass import ds
from concourse.bass_utils import run_bass_kernel_spmd

F32 = mybir.dt.float32
BF16 = mybir.dt.bfloat16
U32 = mybir.dt.uint32
AF = mybir.ActivationFunctionType
AL = mybir.AluOpType
AX = mybir.AxisListType.X

NCORES = 8
RG = [list(range(NCORES))]

H = 2048
NCH = 16
R = 1024
NH = 16
SQ = 2048
HU = 5632
HUC = 44
SEC = 128 * R
NE = HU * H
QW1 = (H // 8) * HU
QWALL = 3 * QW1
PIECE = 512
NPIECE = (H // 8) * HU // 128 // PIECE
NP3 = 3 * NPIECE
QRING = 4

f32 = np.float32
SM_SCALE = f32(f32(1.0) / np.sqrt(f32(128.0)))
SM_BIAS = f32(-12.0)
INV_H = float(f32(1.0 / H))
LN_EPS = float(f32(1e-5))
NEG1 = -1.0
THR_SCL = float(f32(0.5 / NE))
THR_OFF = float(f32(0.5e-8))


class Ctr:
    def __init__(self, nc, name):
        self.sem = nc.alloc_semaphore(name)
        self.n = 0

    def inc(self, instr, k=1):
        self.n += k
        instr.then_inc(self.sem, k)
        return self.n


def build_program(debug=False):
    nc = bass.Bass("TRN2", target_bir_lowering=False)
    dbg = {}
    if debug:
        dbg["lnx"] = nc.dram_tensor("d_lnx", [128, NCH * R], BF16, kind="ExternalOutput").ap()
        dbg["kvloc"] = nc.dram_tensor("d_kvloc", [2 * NH, SEC], BF16, kind="ExternalOutput").ap()
        dbg["qloc"] = nc.dram_tensor("d_qloc", [NH, SEC], BF16, kind="ExternalOutput").ap()
        dbg["ctxn"] = nc.dram_tensor("d_ctxn", [128, NH * R], BF16, kind="ExternalOutput").ap()
        dbg["attn"] = nc.dram_tensor("d_attn", [128, NCH * R], BF16, kind="ExternalOutput").ap()
        dbg["thr"] = nc.dram_tensor("d_thr", [128, 8], F32, kind="ExternalOutput").ap()
        dbg["qwloc"] = nc.dram_tensor("d_qwloc", [1, QWALL], BF16, kind="ExternalOutput").ap()
        dbg["x1"] = nc.dram_tensor("d_x1", [128, NCH * R], F32, kind="ExternalOutput").ap()
        dbg["hid"] = nc.dram_tensor("d_hid", [128, HUC * 512], BF16, kind="ExternalOutput").ap()
        dbg["y"] = nc.dram_tensor("d_y", [128, NCH * 512], BF16, kind="ExternalOutput").ap()
        dbg["parts"] = nc.dram_tensor("d_parts", [128, NP3 + 2], F32, kind="ExternalOutput").ap()
        dbg["q3sl"] = nc.dram_tensor("d_q3sl", [128, 2 * HUC * 128], BF16, kind="ExternalOutput").ap()
        dbg["p3"] = nc.dram_tensor("d_p3", [128, 4], F32, kind="ExternalOutput").ap()
        dbg["ws"] = nc.dram_tensor("d_ws", [1, 16], F32, kind="ExternalOutput").ap()
        dbg["wsg"] = nc.dram_tensor("d_wsg", [1, 128], F32, kind="ExternalOutput").ap()

    xT = nc.dram_tensor("xT", [H, R], F32, kind="ExternalInput").ap()
    wkqT = nc.dram_tensor("wkqT", [H, 2 * H], F32, kind="ExternalInput").ap()
    wvT = nc.dram_tensor("wvT", [H, H], F32, kind="ExternalInput").ap()
    wpTt = nc.dram_tensor("wpT", [H, H], F32, kind="ExternalInput").ap()
    bpj = nc.dram_tensor("bpj", [1, H], F32, kind="ExternalInput").ap()
    w1s = nc.dram_tensor("w1s", [H // 8, HU], F32, kind="ExternalInput").ap()
    w2s = nc.dram_tensor("w2s", [H // 8, HU], F32, kind="ExternalInput").ap()
    w3s = nc.dram_tensor("w3s", [HU, H // 8], F32, kind="ExternalInput").ap()
    ctl = nc.dram_tensor("ctl", [1, 128], U32, kind="ExternalInput").ap()
    yT = nc.dram_tensor("yT", [H, R], F32, kind="ExternalOutput").ap()

    kv_sendK = nc.dram_tensor("kv_sendK", [NCORES, 4, 128, R], BF16)
    kv_sendV = nc.dram_tensor("kv_sendV", [NCORES, 4, 128, R], BF16)
    q_send = nc.dram_tensor("q_send", [NCORES, 4, 128, R], BF16)
    ws_loc = nc.dram_tensor("ws_loc", [1, 16], F32)
    qw_loc = nc.dram_tensor("qw_loc", [1, QWALL], BF16)
    kv_stK = nc.dram_tensor("kv_stK", [NCORES, 4, 128, R], BF16)
    kv_stV = nc.dram_tensor("kv_stV", [NCORES, 4, 128, R], BF16)
    q_stA = nc.dram_tensor("q_stA", [NCORES, 4, 128, R], BF16)
    q_st = nc.dram_tensor("q_st", [4, 4, 128, R], BF16)
    ws_g = nc.dram_tensor("ws_g", [NCORES, 16], F32)
    qw_g = nc.dram_tensor("qw_g", [NCORES, QWALL], BF16)

    xT_sb = nc.alloc_sbuf_tensor("xT_sb", [128, NCH * R], F32)
    rows = nc.alloc_sbuf_tensor("rows", [1, 5 * R], F32)
    ROW_M, ROW_RSQ, ROW_B, ROW_T, ROW_RSQ3 = 0, 1, 2, 3, 4
    ROW_B3 = ROW_B
    xb_sb = nc.alloc_sbuf_tensor("xb_sb", [128, 2 * R], BF16)
    sq_sb = nc.alloc_sbuf_tensor("sq_sb", [128, 2 * R], BF16)
    ln_tmp = nc.alloc_sbuf_tensor("ln_tmp", [128, R], F32)
    ones_cb = nc.alloc_sbuf_tensor("ones_cb", [128, 16], BF16)
    ones_cf = nc.alloc_sbuf_tensor("ones_cf", [128, 16], F32)
    ones_rf = nc.alloc_sbuf_tensor("ones_rf", [1, 128], F32)
    ones_rb = nc.alloc_sbuf_tensor("ones_rb", [1, R], BF16)
    bprojB = nc.alloc_sbuf_tensor("bprojB", [1, H], BF16)
    ctl_sb = nc.alloc_sbuf_tensor("ctl_sb", [1, 128], U32)
    thr_sb = nc.alloc_sbuf_tensor("thr_sb", [128, 8], F32)
    thrr_sb = nc.alloc_sbuf_tensor("thrr_sb", [1, 16], F32)
    wsg_sb = nc.alloc_sbuf_tensor("wsg_sb", [1, 128], F32)
    parts_sb = nc.alloc_sbuf_tensor("parts_sb", [128, NP3 + 2], F32)
    p3_sb = nc.alloc_sbuf_tensor("p3_sb", [128, 4], F32)
    ws_sb = nc.alloc_sbuf_tensor("ws_sb", [1, 16], F32)

    def row(i):
        return rows[0:1, i * R:(i + 1) * R]

    def rowh(i, h, w=512):
        return rows[0:1, i * R + h * w: i * R + (h + 1) * w]

    es_qu = ExitStack()
    wq_sb = es_qu.enter_context(nc.sbuf_tensor("wq_sb", [128, QRING * PIECE], F32))
    qa_sb = es_qu.enter_context(nc.sbuf_tensor("qa_sb", [128, 2 * PIECE], BF16))
    qb_sb = es_qu.enter_context(nc.sbuf_tensor("qb_sb", [128, 2 * PIECE], BF16))
    qp_sb = es_qu.enter_context(nc.sbuf_tensor("qp_sb", [128, 2 * PIECE], BF16))

    def wqsl(j):
        return wq_sb[:, (j % QRING) * PIECE:(j % QRING + 1) * PIECE]

    def ring2(buf, j, w):
        return buf[:, (j % 2) * w:(j % 2 + 1) * w]

    def ring4(buf, j, w):
        return buf[:, (j % 4) * w:(j % 4 + 1) * w]

    sX = Ctr(nc, "sX")
    sWS = Ctr(nc, "sWS")
    sQ = [Ctr(nc, f"sQ{i}") for i in range(QRING)]
    sP = [Ctr(nc, f"sP{i}") for i in range(2)]
    sW2 = [Ctr(nc, f"sW2{i}") for i in range(2)]
    sE = [Ctr(nc, f"sE{i}") for i in range(2)]
    sVW = [Ctr(nc, f"sVW{i}") for i in range(2)]
    sGL = [Ctr(nc, f"sGL{i}") for i in range(2)]
    sST = Ctr(nc, "sST")
    sF = [Ctr(nc, f"sF{i}") for i in range(2)]
    sQ3 = [Ctr(nc, f"sQ3{i}") for i in range(2)]
    sG = Ctr(nc, "sG")
    sC = Ctr(nc, "sC")
    sT = Ctr(nc, "sT")
    sV = Ctr(nc, "sV")
    sS = Ctr(nc, "sS")
    M = {}

    def slab_sem(s):
        return sW2[s % 2], 16 * (s // 2 + 1)

    w1v = w1s.rearrange("(c p) h -> c p h", p=128)
    w2v = w2s.rearrange("(c p) h -> c p h", p=128)
    w3v = w3s.rearrange("(c p) o -> c p o", p=128)
    WSRC = [w1v, w2v, w3v]

    def wpiece(j):
        m, pj = divmod(j, NPIECE)
        if m < 2:
            c, h0 = divmod(pj * PIECE, HU)
            return WSRC[m][c:c + 1, :, h0:h0 + PIECE].squeeze(0)
        cb = 2 * pj
        return w3v[cb:cb + 2, :, :].transpose([1, 0, 2])

    def wq_dst(j):
        if j // NPIECE < 2:
            return wqsl(j)
        return wqsl(j).rearrange("p (c o) -> p c o", c=2)

    kvKv = kv_sendK.ap()
    kvVv = kv_sendV.ap().rearrange("(c j) h p s -> c j h p s", j=2)
    qsv = q_send.ap()
    qwgv = qw_g.ap()

    es_p1 = ExitStack()
    lnx_sb = es_p1.enter_context(nc.sbuf_tensor("lnx_sb", [128, NCH * R], BF16))
    wsl_sb = es_p1.enter_context(nc.sbuf_tensor("wsl_sb", [128, 2 * NCH * 512], BF16))
    evq_sb = es_p1.enter_context(nc.sbuf_tensor("evq_sb", [128, 2 * R], BF16))
    evv_sb = es_p1.enter_context(nc.sbuf_tensor("evv_sb", [128, 2 * 512], BF16))

    def wsl(par):
        return wsl_sb[:, par * NCH * 512:(par + 1) * NCH * 512]

    es_ln1p = ExitStack()
    ss_ps = es_ln1p.enter_context(nc.psum_tensor("ss_ps", [1, R], F32))
    qq_ps = es_ln1p.enter_context(nc.psum_tensor("qq_ps", [1, R], F32))
    a_ps = es_ln1p.enter_context(nc.psum_tensor("a_ps", [128, 512], F32))
    b_ps = es_ln1p.enter_context(nc.psum_tensor("b_ps", [128, 512], F32))

    with nc.Block() as block:
        def s1(sync):
            sync.dma_start(out=ctl_sb[:, :], in_=ctl[:, :]).then_inc(sX.sem, 16)
            sX.n += 16
            sync.dma_start(
                out=xT_sb[:, :].rearrange("p (c r) -> p c r", c=NCH),
                in_=xT.rearrange("(c p) r -> p c r", p=128),
            ).then_inc(sX.sem, 16)
            sX.n += 16
            M["x_loaded"] = sX.n
        block.sync(s1)

        M["consts"] = 1
        for c in range(NCH):
            M[f"cast_{c}"] = 1 + 5 * c + 1
            for k in range(4):
                M[f"absred_{4 * c + k}"] = 1 + 5 * c + 2 + k
        M["rows1"] = 82
        M["rowsb"] = 83
        for h in range(2):
            for c in range(NCH):
                M[f"lnx_{h}_{c}"] = 83 + h * NCH + c + 1
        M["absred_64"] = 116
        M["absred_65"] = 117
        M["p3"] = 118

        def s2(sync):
            for j in range(NP3):
                if j >= QRING:
                    sync.wait_ge(sV.sem, M[f"absred_{j - QRING}"])
                sl = sQ[j % QRING]
                sync.dma_start(out=wq_dst(j), in_=wpiece(j)).then_inc(sl.sem, 16)
                sl.n += 16
                M[f"wq1_{j}"] = (j % QRING, sl.n)
        block.sync(s2)

        def v1(vector):
            vector.memset(ones_cb[:, :], 1.0)
            vector.memset(ones_cf[:, :], 1.0)
            vector.memset(ones_rf[:, :], 1.0)
            vector.memset(ones_rb[:, :], 1.0)
            vector.memset(p3_sb[:, :], 0.0)
            vector.memset(ws_sb[:, :], 0.0)
            sV.inc(vector.memset(parts_sb[:, :], 0.0))
            assert sV.n == M["consts"]
        block.vector(v1)

        def a1(scalar):
            scalar.wait_ge(sX.sem, M["x_loaded"])
            for c in range(NCH):
                if c >= 2:
                    scalar.wait_ge(sT.sem, c - 1)
                sS.inc(scalar.activation(
                    out=ring2(sq_sb, c, R), in_=xT_sb[:, c * R:(c + 1) * R],
                    func=AF.Square))
                M[f"sq1_{c}"] = sS.n
        block.scalar(a1)

        def v2(vector):
            vector.wait_ge(sX.sem, M["x_loaded"])
            for c in range(NCH):
                if c >= 2:
                    vector.wait_ge(sT.sem, c - 1)
                sV.inc(vector.tensor_copy(ring2(xb_sb, c, R),
                                          xT_sb[:, c * R:(c + 1) * R]))
                assert sV.n == M[f"cast_{c}"]
                for k in range(4):
                    j = 4 * c + k
                    slot, cnt = M[f"wq1_{j}"]
                    vector.wait_ge(sQ[slot].sem, cnt)
                    sV.inc(vector.tensor_reduce(
                        parts_sb[:, j:j + 1], wqsl(j), axis=AX, op=AL.add,
                        apply_absolute_value=True))
                    assert sV.n == M[f"absred_{j}"]
        block.vector(v2)

        def t1(tensor):
            assert sT.n == 0
            for c in range(NCH):
                tensor.wait_ge(sV.sem, M[f"cast_{c}"])
                tensor.wait_ge(sS.sem, M[f"sq1_{c}"])
                xb = ring2(xb_sb, c, R)
                sq = ring2(sq_sb, c, R)
                st, sp = (c == 0), (c == NCH - 1)
                tensor.matmul(ss_ps[0:1, 0:512], ones_cb[:, 0:1], xb[:, 0:512],
                              start=st, stop=sp)
                tensor.matmul(ss_ps[0:1, 512:1024], ones_cb[:, 0:1],
                              xb[:, 512:1024], start=st, stop=sp)
                tensor.matmul(qq_ps[0:1, 0:512], ones_cb[:, 0:1], sq[:, 0:512],
                              start=st, stop=sp)
                ins = tensor.matmul(qq_ps[0:1, 512:1024], ones_cb[:, 0:1],
                                    sq[:, 512:1024], start=st, stop=sp)
                sT.inc(ins)
            M["ln1_stats"] = sT.n
        block.tensor(t1)

        def rows_part1(vector, ssp, qqp, mark):
            vector.tensor_scalar(row(ROW_M), ssp[0:1, :], INV_H, None, AL.mult)
            vector.tensor_tensor(row(ROW_T), row(ROW_M), row(ROW_M), AL.mult)
            vector.scalar_tensor_tensor(row(ROW_T), qqp[0:1, :], INV_H,
                                        row(ROW_T), AL.mult, AL.subtract)
            vector.tensor_scalar(row(ROW_T), row(ROW_T), LN_EPS, None, AL.add)
            sV.inc(vector.reciprocal(row(ROW_RSQ), row(ROW_T)))
            M[mark] = sV.n

        def rows_part1_w(vector, ssp, qqp, mark, w):
            vector.tensor_scalar(rows[0:1, ROW_M * R:ROW_M * R + w],
                                 ssp[0:1, :], INV_H, None, AL.mult)
            rm = rows[0:1, ROW_M * R:ROW_M * R + w]
            rt = rows[0:1, ROW_T * R:ROW_T * R + w]
            vector.tensor_tensor(rt, rm, rm, AL.mult)
            vector.scalar_tensor_tensor(rt, qqp[0:1, :], INV_H, rt,
                                        AL.mult, AL.subtract)
            vector.tensor_scalar(rt, rt, LN_EPS, None, AL.add)
            sV.inc(vector.reciprocal(
                rows[0:1, ROW_RSQ * R:ROW_RSQ * R + w], rt))
            M[mark] = sV.n

        def v3(vector):
            vector.wait_ge(sT.sem, M["ln1_stats"])
            rows_part1(vector, ss_ps, qq_ps, "rows1")
            assert sV.n == M["rows1"]
        block.vector(v3)

        def a2(scalar):
            scalar.wait_ge(sV.sem, M["rows1"])
            sS.inc(scalar.sqrt(row(ROW_RSQ), row(ROW_RSQ)))
            M["sqrt1"] = sS.n
        block.scalar(a2)

        def v4(vector):
            vector.wait_ge(sS.sem, M["sqrt1"])
            sV.inc(vector.scalar_tensor_tensor(row(ROW_B), row(ROW_M), NEG1,
                                               row(ROW_RSQ), AL.mult, AL.mult))
            assert sV.n == M["rowsb"]
        block.vector(v4)

        for h in range(2):
            def t2(tensor, h=h):
                if h == 0:
                    tensor.wait_ge(sV.sem, M["rowsb"])
                else:
                    tensor.wait_ge(sV.sem, M[f"lnx_0_{NCH - 1}"])
                tensor.matmul(a_ps[:, :], ones_rf[0:1, :],
                              rowh(ROW_RSQ, h), start=True, stop=True)
                ins = tensor.matmul(b_ps[:, :], ones_rf[0:1, :],
                                    rowh(ROW_B, h), start=True, stop=True)
                sT.inc(ins)
                M[f"ln1_bc_{h}"] = sT.n
            block.tensor(t2)

            def v5a(vector, h=h):
                vector.wait_ge(sT.sem, M[f"ln1_bc_{h}"])
                for c in range(NCH):
                    xs = xT_sb[:, c * R + h * 512: c * R + (h + 1) * 512]
                    vector.tensor_tensor(ln_tmp[:, 0:512], xs, a_ps[:, :], AL.mult)
                    sV.inc(vector.tensor_tensor(
                        lnx_sb[:, c * R + h * 512: c * R + (h + 1) * 512],
                        ln_tmp[:, 0:512], b_ps[:, :], AL.add))
                    assert sV.n == M[f"lnx_{h}_{c}"]
            block.vector(v5a)

        def v5b(vector):
            for j in (64, 65):
                slot, cnt = M[f"wq1_{j}"]
                vector.wait_ge(sQ[slot].sem, cnt)
                sV.inc(vector.tensor_reduce(
                    parts_sb[:, j:j + 1], wqsl(j), axis=AX, op=AL.add,
                    apply_absolute_value=True))
                assert sV.n == M[f"absred_{j}"]
            for m in range(3):
                vector.tensor_reduce(p3_sb[:, m:m + 1],
                                     parts_sb[:, m * NPIECE:(m + 1) * NPIECE],
                                     axis=AX, op=AL.add)
            sV.inc(vector.engine_nop())
            assert sV.n == M["p3"]
        block.vector(v5b)

        es_ln1p.close()
        es_qkvp = ExitStack()
        pqk = [es_qkvp.enter_context(nc.psum_tensor(f"pqk{i}", [128, R], F32))
               for i in range(2)]
        pv = [es_qkvp.enter_context(nc.psum_tensor(f"pv{i}", [128, 512], F32))
              for i in range(2)]
        pr_ps = es_qkvp.enter_context(nc.psum_tensor("pr_ps", [1, 16], F32))

        EVB = M["p3"]
        M["ws_evac"] = EVB + 1

        def evK(oc):
            return EVB + 1 + oc + 1

        def evV(idx):
            return EVB + 17 + idx + 1

        def evQ(oc):
            return EVB + 49 + oc + 1

        _e = [sE[0].n, sE[1].n]
        _v = [sVW[0].n, sVW[1].n]
        WRK, WRV, WRQ = {}, {}, {}
        for _oc in range(NH):
            _e[_oc % 2] += 32
            WRK[_oc] = (_oc % 2, _e[_oc % 2])
        for _idx in range(32):
            _v[_idx % 2] += 32
            WRV[_idx] = (_idx % 2, _v[_idx % 2])
        for _oc in range(NH):
            _e[_oc % 2] += 32
            WRQ[_oc] = (_oc % 2, _e[_oc % 2])

        T_QKV = {}

        def t3(tensor):
            tensor.wait_ge(sV.sem, M["p3"])
            sT.inc(tensor.matmul(pr_ps[0:1, 0:4], ones_cf[:, 0:1], p3_sb[:, 0:4],
                                 start=True, stop=True))
            M["pmm"] = sT.n
            for oc in range(NH):
                s = oc // 4
                _sm, _c = slab_sem(s)
                tensor.wait_ge(_sm.sem, _c)
                if oc == 0:
                    tensor.wait_ge(sV.sem, M[f"lnx_1_{NCH - 1}"])
                if oc >= 2:
                    tensor.wait_ge(sV.sem, evK(oc - 2))
                tgt = pqk[oc % 2]
                for hc in range(NCH):
                    lhsT = wsl(s % 2)[:, hc * 512 + (oc % 4) * 128:
                                      hc * 512 + (oc % 4) * 128 + 128]
                    st, sp = (hc == 0), (hc == NCH - 1)
                    tensor.matmul(tgt[:, 0:512], lhsT,
                                  lnx_sb[:, hc * R: hc * R + 512],
                                  start=st, stop=sp)
                    ins = tensor.matmul(tgt[:, 512:1024], lhsT,
                                        lnx_sb[:, hc * R + 512: hc * R + 1024],
                                        start=st, stop=sp)
                sT.inc(ins)
                T_QKV[("K", oc)] = sT.n
            for vs in range(4):
                s = 4 + vs
                for rsub in range(8):
                    idx = vs * 8 + rsub
                    _sm, _c = slab_sem(s)
                    tensor.wait_ge(_sm.sem, _c)
                    if idx >= 2:
                        tensor.wait_ge(sV.sem, evV(idx - 2))
                    tgt = pv[idx % 2]
                    for hc in range(NCH):
                        lhsT = lnx_sb[:, hc * R + rsub * 128:
                                      hc * R + rsub * 128 + 128]
                        rhs = wsl(s % 2)[:, hc * 512:(hc + 1) * 512]
                        ins = tensor.matmul(tgt[:, :], lhsT, rhs,
                                            start=(hc == 0), stop=(hc == NCH - 1))
                    sT.inc(ins)
                    T_QKV[("V", idx)] = sT.n
            for oc in range(NH):
                s = 8 + oc // 4
                _sm, _c = slab_sem(s)
                tensor.wait_ge(_sm.sem, _c)
                if oc >= 2:
                    tensor.wait_ge(sV.sem, evQ(oc - 2))
                tgt = pqk[oc % 2]
                for hc in range(NCH):
                    lhsT = wsl(s % 2)[:, hc * 512 + (oc % 4) * 128:
                                      hc * 512 + (oc % 4) * 128 + 128]
                    st, sp = (hc == 0), (hc == NCH - 1)
                    tensor.matmul(tgt[:, 0:512], lhsT,
                                  lnx_sb[:, hc * R: hc * R + 512],
                                  start=st, stop=sp)
                    ins = tensor.matmul(tgt[:, 512:1024], lhsT,
                                        lnx_sb[:, hc * R + 512: hc * R + 1024],
                                        start=st, stop=sp)
                sT.inc(ins)
                T_QKV[("Q", oc)] = sT.n
        block.tensor(t3)

        def v6(vector):
            assert sV.n == EVB
            vector.wait_ge(sT.sem, M["pmm"])
            sV.inc(vector.tensor_copy(ws_sb[0:1, 0:4], pr_ps[0:1, 0:4]))
            assert sV.n == M["ws_evac"]
            for oc in range(NH):
                vector.wait_ge(sT.sem, T_QKV[("K", oc)])
                if oc >= 2:
                    _s, _c = WRK[oc - 2]
                    vector.wait_ge(sE[_s].sem, _c)
                sV.inc(vector.tensor_copy(ring2(evq_sb, oc, R), pqk[oc % 2][:, :]))
                assert sV.n == evK(oc)
            for idx in range(32):
                vector.wait_ge(sT.sem, T_QKV[("V", idx)])
                if idx >= 2:
                    _s, _c = WRV[idx - 2]
                    vector.wait_ge(sVW[_s].sem, _c)
                sV.inc(vector.tensor_copy(ring2(evv_sb, idx, 512),
                                          pv[idx % 2][:, :]))
                assert sV.n == evV(idx)
            for oc in range(NH):
                vector.wait_ge(sT.sem, T_QKV[("Q", oc)])
                if oc >= 2:
                    _s, _c = WRQ[oc - 2]
                    vector.wait_ge(sE[_s].sem, _c)
                sV.inc(vector.tensor_copy(ring2(evq_sb, oc, R), pqk[oc % 2][:, :]))
                assert sV.n == evQ(oc)
        block.vector(v6)

        def s3(sync):
            sync.wait_ge(sV.sem, M["ws_evac"])
            sync.dma_start(out=ws_loc.ap()[:, :],
                           in_=ws_sb[:, :]).then_inc(sWS.sem, 16)
            sWS.n += 16
            M["ws_store"] = sWS.n
            for oc in range(NH):
                sync.wait_ge(sV.sem, evK(oc))
                cl, hic = oc % 4, oc // 4
                _s, _c = WRK[oc]
                for j in range(2):
                    sync.dma_start(
                        out=kvKv[2 * cl + j, hic, :, :],
                        in_=ring2(evq_sb, oc, R),
                    ).then_inc(sE[_s].sem, 16)
                    sE[_s].n += 16
                assert sE[_s].n == _c
            for idx in range(32):
                vs, rsub = divmod(idx, 8)
                sync.wait_ge(sV.sem, evV(idx))
                _s, _c = WRV[idx]
                for j in range(2):
                    sync.dma_start(
                        out=kvVv[:, j, vs, :, rsub * 128:(rsub + 1) * 128
                                 ].transpose([1, 0, 2]),
                        in_=ring2(evv_sb, idx, 512).rearrange(
                            "p (n d) -> p n d", n=4),
                    ).then_inc(sVW[_s].sem, 16)
                    sVW[_s].n += 16
                assert sVW[_s].n == _c
            for oc in range(NH):
                sync.wait_ge(sV.sem, evQ(oc))
                cl, hic = oc % 4, oc // 4
                _s, _c = WRQ[oc]
                for j in range(2):
                    sync.dma_start(
                        out=qsv[2 * cl + j, hic, :, :],
                        in_=ring2(evq_sb, oc, R),
                    ).then_inc(sE[_s].sem, 16)
                    sE[_s].n += 16
                assert sE[_s].n == _c
        block.sync(s3)

        def g1(gpsimd):
            slab_last = {}
            for s in range(4):
                slab_last[s] = T_QKV[("K", s * 4 + 3)]
                slab_last[4 + s] = T_QKV[("V", s * 8 + 7)]
                slab_last[8 + s] = T_QKV[("Q", s * 4 + 3)]
            for s in range(12):
                if s >= 2:
                    gpsimd.wait_ge(sT.sem, slab_last[s - 2])
                if s < 4:
                    src = wkqT[:, s * 512:(s + 1) * 512]
                elif s < 8:
                    src = wvT[:, (s - 4) * 512:(s - 3) * 512]
                else:
                    src = wkqT[:, 2048 + (s - 8) * 512: 2048 + (s - 7) * 512]
                _sm, _c = slab_sem(s)
                gpsimd.dma_start(
                    out=wsl(s % 2).rearrange("p (c o) -> p c o", c=NCH),
                    in_=src.rearrange("(c p) o -> p c o", p=128),
                ).then_inc(_sm.sem, 16)
                _sm.n += 16
                assert _sm.n == _c
                if s == 4:
                    gpsimd.wait_ge(sWS.sem, M["ws_store"])
                    gpsimd.collective_compute(
                        "AllGather", AL.bypass, replica_groups=RG,
                        ins=[ws_loc.ap().opt()], outs=[ws_g.ap().opt()],
                    ).then_inc(sC.sem, 1)
                    sC.n += 1
                    M["ag_ws"] = sC.n
                if s == 7:
                    gpsimd.wait_ge(sE[0].sem, WRK[14][1])
                    gpsimd.wait_ge(sE[1].sem, WRK[15][1])
                    gpsimd.collective_compute(
                        "AllToAll", AL.bypass, replica_groups=RG,
                        ins=[kv_sendK.ap().opt()], outs=[kv_stK.ap().opt()],
                    ).then_inc(sC.sem, 1)
                    sC.n += 1
                    M["a2a_k"] = sC.n
            _sm, _c = slab_sem(12)
            gpsimd.dma_start(out=bprojB[:, :], in_=bpj[:, :]).then_inc(_sm.sem, 16)
            _sm.n += 16
            assert _sm.n == _c
            gpsimd.wait_ge(sVW[0].sem, WRV[30][1])
            gpsimd.wait_ge(sVW[1].sem, WRV[31][1])
            gpsimd.collective_compute(
                "AllToAll", AL.bypass, replica_groups=RG,
                ins=[kv_sendV.ap().opt()], outs=[kv_stV.ap().opt()],
            ).then_inc(sC.sem, 1)
            sC.n += 1
            M["a2a_v"] = sC.n
            gpsimd.wait_ge(sE[0].sem, WRQ[14][1])
            gpsimd.wait_ge(sE[1].sem, WRQ[15][1])
            gpsimd.collective_compute(
                "AllToAll", AL.bypass, replica_groups=RG,
                ins=[q_send.ap().opt()], outs=[q_stA.ap().opt()],
            ).then_inc(sC.sem, 1)
            sC.n += 1
            M["a2a_q"] = sC.n
        block.gpsimd(g1)

        def g2(gpsimd):
            if debug:
                gpsimd.dma_start(out=dbg["parts"][:, :], in_=parts_sb[:, :]).then_inc(sG.sem, 16)
                sG.n += 16
                gpsimd.dma_start(out=dbg["p3"][:, :], in_=p3_sb[:, :]).then_inc(sG.sem, 16)
                sG.n += 16
                gpsimd.dma_start(out=dbg["ws"][:, :], in_=ws_sb[:, :]).then_inc(sG.sem, 16)
                sG.n += 16
                gpsimd.dma_start(out=dbg["lnx"][:, :], in_=lnx_sb[:, :]).then_inc(sG.sem, 16)
                sG.n += 16
                kvdbg = dbg["kvloc"].rearrange("(c h k) m -> k c h m", c=4, h=4)
                gpsimd.dma_start(
                    out=kvdbg[0, :, :, :],
                    in_=kv_sendK.ap().rearrange(
                        "(c j) h p s -> c j h (p s)", j=2)[:, 0],
                ).then_inc(sG.sem, 16)
                sG.n += 16
                gpsimd.dma_start(
                    out=kvdbg[1, :, :, :],
                    in_=kv_sendV.ap().rearrange(
                        "(c j) h p s -> c j h (p s)", j=2)[:, 0],
                ).then_inc(sG.sem, 16)
                sG.n += 16
                gpsimd.dma_start(
                    out=dbg["qloc"].rearrange("(c h) m -> c h m", c=4),
                    in_=q_send.ap().rearrange(
                        "(c j) h p s -> c j h (p s)", j=2)[:, 0],
                ).then_inc(sG.sem, 16)
                sG.n += 16
        if debug:
            block.gpsimd(g2)

        def s4(sync):
            sync.wait_ge(sC.sem, M["ag_ws"])
            sync.dma_start(out=wsg_sb[:, :],
                           in_=ws_g.ap().flatten().unsqueeze(0),
                           ).then_inc(sWS.sem, 16)
            sWS.n += 16
            M["wsg_loaded"] = sWS.n
        block.sync(s4)

        def v7(vector):
            vector.wait_ge(sWS.sem, M["wsg_loaded"])
            vector.tensor_reduce(
                ws_sb[0:1, :],
                wsg_sb[0:1, :].rearrange("a (r j) -> a j r", r=NCORES),
                axis=AX, op=AL.add)
            vector.tensor_copy(thrr_sb[0:1, 8:16], wsg_sb[0:1, 0:8])
            vector.tensor_copy(thrr_sb[0:1, 8:16], wsg_sb[0:1, 8:16])
            vector.tensor_scalar(thrr_sb[0:1, 0:4], ws_sb[0:1, 0:4],
                                 THR_SCL, THR_OFF, AL.mult, AL.add)
            sV.inc(vector.tensor_scalar(thrr_sb[0:1, 4:8], ws_sb[0:1, 0:4],
                                        -THR_SCL, -THR_OFF, AL.mult, AL.add))
            M["thr_rows"] = sV.n
        block.vector(v7)

        es_qkvp.close()
        es_attp = ExitStack()
        sc_ps = [es_attp.enter_context(nc.psum_tensor(f"sc{i}", [128, R], F32))
                 for i in range(2)]
        ctx_ps = es_attp.enter_context(nc.psum_tensor("ctx_ps", [128, R], F32))
        tail_ps = es_attp.enter_context(nc.psum_tensor("tail_ps", [128, R], F32))

        es_p1.close()
        es_p2 = ExitStack()
        ctxn_sb = es_p2.enter_context(nc.sbuf_tensor("ctxn_sb", [128, NH * R], BF16))
        wpsl_sb = es_p2.enter_context(nc.sbuf_tensor("wpsl_sb", [128, 2 * NCH * 256], BF16))
        es_att = ExitStack()
        kt_sb = es_att.enter_context(nc.sbuf_tensor("kt_sb", [128, 2 * SQ], BF16))
        vv_sb = es_att.enter_context(nc.sbuf_tensor("vv_sb", [128, 2 * SQ], BF16))
        qq_sb = es_att.enter_context(nc.sbuf_tensor("qq_sb", [128, 2 * R], BF16))
        pt_sb = es_att.enter_context(nc.sbuf_tensor("pt_sb", [128, 2 * R], BF16))
        run_sb = es_att.enter_context(nc.sbuf_tensor("run_sb", [128, R], BF16))
        rs_sb = es_att.enter_context(nc.sbuf_tensor("rs_sb", [128, R], F32))

        M["thr_bc"] = sT.n + 1

        def v8a(vector):
            vector.wait_ge(sT.sem, M["thr_bc"])
            sV.inc(vector.tensor_copy(thr_sb[:, :], tail_ps[:, 0:8]))
            M["thr_copy"] = sV.n
            vector.tensor_copy(thrr_sb[0:1, 8:16], wsg_sb[0:1, 0:8])
            vector.tensor_copy(thrr_sb[0:1, 8:16], wsg_sb[0:1, 8:16])
        block.vector(v8a)

        VQ0 = sV.n

        def VQ2(j):
            return VQ0 + 2 * j + 1

        def VQ3(j):
            return VQ0 + 2 * j + 2

        SQ2, LDQ, STQ = {}, {}, {}

        def STQF(j):
            return j % 2, 16 * (j // 2 + 1) + (sP[j % 2].n)

        _sp_base = [sP[0].n, sP[1].n]

        def qdst(j):
            m, pj = divmod(j, NPIECE)
            if m < 2:
                c, h0 = divmod(pj * PIECE, HU)
                blk = qw_loc.ap()[0:1, m * QW1:(m + 1) * QW1].squeeze(0).rearrange(
                    "(c p h) -> c p h", c=2, p=128)
                return blk[c:c + 1, :, h0:h0 + PIECE].squeeze(0), None
            cb = pj * 2
            base = 2 * QW1 + cb * 128 * 256
            return (qw_loc.ap()[0:1, base:base + 2 * 128 * 256].squeeze(
                0).rearrange("(c p o) -> c p o", c=2,
                             p=128).transpose([1, 0, 2]), "w3")

        def gq(scalar):
            scalar.wait_ge(sV.sem, M["absred_65"])
            for j in range(min(QRING, NP3)):
                sl = sQ[j % QRING]
                scalar.dma_start(out=wq_dst(j), in_=wpiece(j)).then_inc(sl.sem, 16)
                sl.n += 16
                LDQ[j] = (j % QRING, sl.n)
            for j in range(NP3):
                if j + QRING < NP3:
                    scalar.wait_ge(sV.sem, VQ2(j))
                    sl = sQ[(j + QRING) % QRING]
                    scalar.dma_start(out=wq_dst(j + QRING),
                                     in_=wpiece(j + QRING)).then_inc(sl.sem, 16)
                    sl.n += 16
                    LDQ[j + QRING] = ((j + QRING) % QRING, sl.n)
                scalar.wait_ge(sV.sem, VQ3(j))
                dst, kind = qdst(j)
                srcp = (ring2(qp_sb, j, PIECE) if kind is None else
                        ring2(qp_sb, j, PIECE).rearrange("p (c o) -> p c o", c=2))
                pl = sP[j % 2]
                scalar.dma_start(out=dst, in_=srcp).then_inc(pl.sem, 16)
                pl.n += 16
                _sl, _cn = STQF(j)
                STQ[j] = (_sl, pl.n)
                assert pl.n == _sp_base[_sl] + 16 * (j // 2 + 1)
        block.scalar(gq)

        def vq(vector):
            assert sV.n == VQ0
            for j in range(NP3):
                m = j // NPIECE
                _s, _c = LDQ[j]
                vector.wait_ge(sQ[_s].sem, _c)
                vector.tensor_scalar(ring2(qa_sb, j, PIECE), wqsl(j),
                                     thr_sb[:, m:m + 1], None, AL.is_gt)
                sV.inc(vector.tensor_scalar(ring2(qb_sb, j, PIECE), wqsl(j),
                                            thr_sb[:, 4 + m:5 + m], None,
                                            AL.is_lt))
                assert sV.n == VQ2(j)
                if j >= 2:
                    _ps, _pc = STQ[j - 2]
                    vector.wait_ge(sP[_ps].sem, _pc)
                sV.inc(vector.tensor_tensor(ring2(qp_sb, j, PIECE),
                                            ring2(qa_sb, j, PIECE),
                                            ring2(qb_sb, j, PIECE), AL.subtract))
                assert sV.n == VQ3(j)
        block.vector(vq)

        def g2b(gpsimd):
            gpsimd.wait_ge(sP[0].sem, sP[0].n)
            gpsimd.wait_ge(sP[1].sem, sP[1].n)
            gpsimd.collective_compute(
                "AllGather", AL.bypass, replica_groups=RG,
                ins=[qw_loc.ap().opt()], outs=[qw_g.ap().opt()],
            ).then_inc(sC.sem, 1)
            sC.n += 1
            M["ag_qw"] = sC.n
        block.gpsimd(g2b)

        def ld_g(g):
            return sGL[g % 2], 80 * (g // 2 + 1)

        S0 = sS.n

        def ES(p):
            return S0 + p + 1

        V0 = sV.n

        def VS(p):
            g, t = divmod(p, 16)
            return V0 + 18 * g + t + 1

        def VR(g):
            return V0 + 18 * g + 17

        def VN(g):
            return V0 + 18 * g + 18

        TS, TC, TR, TB, TP = {}, {}, {}, {}, {}
        EV2B = V0 + 18 * 16

        def EV2(oc):
            return EV2B + oc + 1

        def t4(tensor):
            tensor.wait_ge(sV.sem, M["thr_rows"])
            tensor.wait_ge(sV.sem, evQ(15))
            sT.inc(tensor.matmul(tail_ps[:, 0:8], ones_rf[0:1, :],
                                 thrr_sb[0:1, 0:8], start=True, stop=True))
            assert M["thr_bc"] == sT.n
            for g in range(NH):
                par = g % 2
                kt = kt_sb[:, par * SQ:(par + 1) * SQ]
                vv = vv_sb[:, par * SQ:(par + 1) * SQ]
                qv = qq_sb[:, par * R:(par + 1) * R]
                for t in range(16):
                    p = 16 * g + t
                    if t == 0:
                        _sm, _c = ld_g(g)
                        tensor.wait_ge(_sm.sem, _c)
                    if p == 0:
                        tensor.wait_ge(sV.sem, M["thr_copy"])
                    if p >= 2:
                        tensor.wait_ge(sS.sem, ES(p - 2))
                    sc = sc_ps[t % 2]
                    lhsT = kt[:, t * 128:(t + 1) * 128]
                    tensor.matmul(sc[:, 0:512], lhsT, qv[:, 0:512],
                                  start=True, stop=True)
                    sT.inc(tensor.matmul(sc[:, 512:1024], lhsT, qv[:, 512:1024],
                                         start=True, stop=True))
                    TS[p] = sT.n
                    if t >= 1:
                        tp = t - 1
                        pp_ = 16 * g + tp
                        tensor.wait_ge(sS.sem, ES(pp_))
                        if tp == 0:
                            _sm, _c = ld_g(g)
                            tensor.wait_ge(_sm.sem, _c)
                            if g >= 1:
                                tensor.wait_ge(sV.sem, VN(g - 1))
                        pt = ring2(pt_sb, tp, R)
                        lv = vv[:, tp * 128:(tp + 1) * 128]
                        st = (tp == 0)
                        tensor.matmul(ctx_ps[:, 0:512], lv, pt[:, 0:512],
                                      start=st, stop=False)
                        sT.inc(tensor.matmul(ctx_ps[:, 512:1024], lv,
                                             pt[:, 512:1024], start=st, stop=False))
                        TC[pp_] = sT.n
                p15 = 16 * g + 15
                tensor.wait_ge(sS.sem, ES(p15))
                pt = ring2(pt_sb, 15, R)
                lv = vv[:, 15 * 128:16 * 128]
                tensor.matmul(ctx_ps[:, 0:512], lv, pt[:, 0:512],
                              start=False, stop=True)
                sT.inc(tensor.matmul(ctx_ps[:, 512:1024], lv, pt[:, 512:1024],
                                     start=False, stop=True))
                TC[p15] = sT.n
                tensor.wait_ge(sV.sem, VS(p15))
                tensor.matmul(tail_ps[0:1, 0:512], ones_cb[:, 0:1],
                              run_sb[:, 0:512], start=True, stop=True)
                sT.inc(tensor.matmul(tail_ps[0:1, 512:1024], ones_cb[:, 0:1],
                                     run_sb[:, 512:1024], start=True, stop=True))
                TR[g] = sT.n
                tensor.wait_ge(sV.sem, VR(g))
                tensor.matmul(tail_ps[:, 0:512], ones_rf[0:1, :],
                              row(ROW_T)[0:1, 0:512], start=True, stop=True)
                sT.inc(tensor.matmul(tail_ps[:, 512:1024], ones_rf[0:1, :],
                                     row(ROW_T)[0:1, 512:1024],
                                     start=True, stop=True))
                TB[g] = sT.n
            for oc in range(NCH):
                _sm, _c = slab_sem(13 + oc // 2)
                tensor.wait_ge(_sm.sem, _c)
                if oc == 0:
                    tensor.wait_ge(sV.sem, VN(NH - 1))
                    tensor.wait_ge(sS.sem, ES(255))
                if oc >= 2:
                    tensor.wait_ge(sV.sem, EV2(oc - 2))
                tgt = sc_ps[oc % 2]
                for np_ in range(NH):
                    lhsT = wpsl_sb[:, ((oc // 2) % 2) * NCH * 256 +
                                   np_ * 256 + (oc % 2) * 128:
                                   ((oc // 2) % 2) * NCH * 256 +
                                   np_ * 256 + (oc % 2) * 128 + 128]
                    st = (np_ == 0)
                    tensor.matmul(tgt[:, 0:512], lhsT,
                                  ctxn_sb[:, np_ * R: np_ * R + 512],
                                  start=st, stop=False)
                    tensor.matmul(tgt[:, 512:1024], lhsT,
                                  ctxn_sb[:, np_ * R + 512: np_ * R + 1024],
                                  start=st, stop=False)
                bl = bprojB[0:1, oc * 128:(oc + 1) * 128]
                tensor.matmul(tgt[:, 0:512], bl, ones_rb[0:1, 0:512],
                              start=False, stop=True)
                sT.inc(tensor.matmul(tgt[:, 512:1024], bl, ones_rb[0:1, 512:1024],
                                     start=False, stop=True))
                TP[oc] = sT.n
        block.tensor(t4)

        def a3(scalar):
            assert sS.n == S0
            for g in range(NH):
                for t in range(16):
                    p = 16 * g + t
                    scalar.wait_ge(sT.sem, TS[p])
                    if p >= 2:
                        scalar.wait_ge(sT.sem, TC[p - 2])
                        scalar.wait_ge(sV.sem, VS(p - 2))
                    sS.inc(scalar.activation(
                        out=ring2(pt_sb, t, R), in_=sc_ps[t % 2][:, :],
                        func=AF.Exp, bias=SM_BIAS, scale=SM_SCALE))
                    assert sS.n == ES(p)
        block.scalar(a3)

        def v8(vector):
            assert sV.n == V0
            for g in range(NH):
                for t in range(16):
                    p = 16 * g + t
                    vector.wait_ge(sS.sem, ES(p))
                    if t == 0:
                        sV.inc(vector.tensor_copy(run_sb[:, :], ring2(pt_sb, t, R)))
                    else:
                        sV.inc(vector.tensor_tensor(run_sb[:, :], run_sb[:, :],
                                                    ring2(pt_sb, t, R), AL.add))
                    assert sV.n == VS(p)
                vector.wait_ge(sT.sem, TR[g])
                sV.inc(vector.reciprocal(row(ROW_T), tail_ps[0:1, :]))
                assert sV.n == VR(g)
                vector.wait_ge(sT.sem, TB[g])
                vector.tensor_copy(rs_sb[:, :], tail_ps[:, :])
                sV.inc(vector.tensor_tensor(
                    ctxn_sb[:, g * R:(g + 1) * R], ctx_ps[:, :], rs_sb[:, :],
                    AL.mult))
                assert sV.n == VN(g)
        block.vector(v8)

        def s_att(sync):
            reg = sync.alloc_register("gidx")
            sync.wait_ge(sX.sem, M["x_loaded"])
            sync.wait_ge(sC.sem, M["a2a_q"])
            sync.reg_load(reg, ctl_sb[0:1, 0:1])
            hsel = sync.snap(reg, min_val=0, max_val=1)
            q2A = q_stA.ap().rearrange("g h p s -> g (h p s)").rearrange(
                "(a j) m -> j a m", j=2)
            sync.dma_start(out=q_st.ap().rearrange("b h p s -> b (h p s)"),
                           in_=q2A[ds(hsel, 1), :, :].squeeze(0),
                           ).then_inc(sST.sem, 16)
            sST.n += 16
            sync.wait_ge(sST.sem, sST.n)
            kstK = kv_stK.ap()
            kstV = kv_stV.ap()
            qst = q_st.ap()
            for g in range(NH):
                par = g % 2
                a, b = g // 4, g % 4
                if g >= 2:
                    sync.wait_ge(sT.sem, TC[16 * (g - 2) + 15])
                sl = sGL[par]
                for t in range(2):
                    sync.dma_start(
                        out=kt_sb[:, par * SQ + t * R: par * SQ + (t + 1) * R],
                        in_=kstK[2 * a + t, b, :, :],
                    ).then_inc(sl.sem, 16)
                    sl.n += 16
                    sync.dma_start(
                        out=vv_sb[:, par * SQ + t * R: par * SQ + (t + 1) * R],
                        in_=kstV[2 * a + t, b, :, :],
                    ).then_inc(sl.sem, 16)
                    sl.n += 16
                sync.dma_start(
                    out=qq_sb[:, par * R:(par + 1) * R],
                    in_=qst[a, b, :, :],
                ).then_inc(sl.sem, 16)
                sl.n += 16
                assert (sl.n) == ld_g(g)[1]
        block.sync(s_att)

        def g3(gpsimd):
            for k in range(8):
                s = 13 + k
                if k >= 2:
                    gpsimd.wait_ge(sT.sem, TP[2 * (k - 2) + 1])
                _sm, _c = slab_sem(s)
                gpsimd.dma_start(
                    out=wpsl_sb[:, (k % 2) * NCH * 256:(k % 2 + 1) * NCH * 256
                                ].rearrange("p (c o) -> p c o", c=NCH),
                    in_=wpTt[:, k * 256:(k + 1) * 256].rearrange(
                        "(c p) o -> p c o", p=128),
                ).then_inc(_sm.sem, 16)
                _sm.n += 16
                assert _sm.n == _c
        block.gpsimd(g3)

        es_att.close()
        es_p2b = ExitStack()
        attn_sb = es_p2b.enter_context(nc.sbuf_tensor("attn_sb", [128, NCH * R], BF16))

        def v9(vector):
            assert sV.n == EV2B
            for oc in range(NCH):
                vector.wait_ge(sT.sem, TP[oc])
                sV.inc(vector.tensor_copy(attn_sb[:, oc * R:(oc + 1) * R],
                                          sc_ps[oc % 2][:, :]))
                assert sV.n == EV2(oc)
        block.vector(v9)

        ss2 = ctx_ps
        qq2 = tail_ps

        def a5(scalar):
            for c in range(NCH):
                scalar.wait_ge(sV.sem, EV2(c))
                if c >= 2:
                    scalar.wait_ge(sT.sem, M.get(f"st2_{c - 2}", 0))
                sS.inc(scalar.activation(out=ring2(sq_sb, c, R),
                                         in_=attn_sb[:, c * R:(c + 1) * R],
                                         func=AF.Square))
                M[f"sq2_{c}"] = sS.n
        T5B = sT.n
        for c in range(NCH):
            M[f"st2_{c}"] = T5B + c + 1
        block.scalar(a5)

        def t5(tensor):
            assert sT.n == T5B
            for c in range(NCH):
                tensor.wait_ge(sV.sem, EV2(c))
                tensor.wait_ge(sS.sem, M[f"sq2_{c}"])
                if c == 0:
                    tensor.wait_ge(sV.sem, VN(NH - 1))
                ab = attn_sb[:, c * R:(c + 1) * R]
                sq = ring2(sq_sb, c, R)
                st, sp = (c == 0), (c == NCH - 1)
                tensor.matmul(ss2[0:1, 0:512], ones_cb[:, 0:1], ab[:, 0:512],
                              start=st, stop=sp)
                tensor.matmul(ss2[0:1, 512:1024], ones_cb[:, 0:1], ab[:, 512:1024],
                              start=st, stop=sp)
                tensor.matmul(qq2[0:1, 0:512], ones_cb[:, 0:1], sq[:, 0:512],
                              start=st, stop=sp)
                ins = tensor.matmul(qq2[0:1, 512:1024], ones_cb[:, 0:1],
                                    sq[:, 512:1024], start=st, stop=sp)
                sT.inc(ins)
                assert sT.n == M[f"st2_{c}"]
        block.tensor(t5)

        def v10(vector):
            vector.wait_ge(sT.sem, M[f"st2_{NCH - 1}"])
            rows_part1(vector, ss2, qq2, "rows2")
        block.vector(v10)

        def a6(scalar):
            scalar.wait_ge(sV.sem, M["rows2"])
            sS.inc(scalar.sqrt(row(ROW_RSQ), row(ROW_RSQ)))
            M["sqrt2"] = sS.n
        block.scalar(a6)

        m2b = ctx_ps
        a2b = tail_ps

        def t6(tensor):
            tensor.wait_ge(sS.sem, M["sqrt2"])
            tensor.matmul(m2b[:, 0:512], ones_rf[0:1, :], rowh(ROW_M, 0),
                          start=True, stop=True)
            tensor.matmul(m2b[:, 512:1024], ones_rf[0:1, :], rowh(ROW_M, 1),
                          start=True, stop=True)
            tensor.matmul(a2b[:, 0:512], ones_rf[0:1, :], rowh(ROW_RSQ, 0),
                          start=True, stop=True)
            sT.inc(tensor.matmul(a2b[:, 512:1024], ones_rf[0:1, :],
                                 rowh(ROW_RSQ, 1), start=True, stop=True))
            M["bc2"] = sT.n
        block.tensor(t6)

        def v12(vector):
            vector.wait_ge(sT.sem, M["bc2"])
            for c in range(NCH):
                ab = attn_sb[:, c * R:(c + 1) * R]
                xs = xT_sb[:, c * R:(c + 1) * R]
                vector.tensor_tensor(ln_tmp[:, :], ab, m2b[:, :], AL.subtract)
                vector.tensor_tensor(ln_tmp[:, :], ln_tmp[:, :], a2b[:, :], AL.mult)
                sV.inc(vector.tensor_tensor(xs, xs, ln_tmp[:, :], AL.add))
                M[f"xu2_{c}"] = sV.n
            M["x2done"] = sV.n
        block.vector(v12)

        if debug:
            def g_dbg1(gpsimd):
                gpsimd.wait_ge(sWS.sem, M["wsg_loaded"])
                gpsimd.dma_start(out=dbg["wsg"][:, :], in_=wsg_sb[:, :]).then_inc(sG.sem, 16)
                sG.n += 16
                gpsimd.wait_ge(sV.sem, M["thr_copy"])
                gpsimd.dma_start(out=dbg["thr"][:, :], in_=thr_sb[:, :]).then_inc(sG.sem, 16)
                sG.n += 16
                gpsimd.wait_ge(sV.sem, VN(NH - 1))
                gpsimd.dma_start(out=dbg["ctxn"][:, :], in_=ctxn_sb[:, :]).then_inc(sG.sem, 16)
                sG.n += 16
                gpsimd.wait_ge(sV.sem, EV2(NCH - 1))
                gpsimd.dma_start(out=dbg["attn"][:, :], in_=attn_sb[:, :]).then_inc(sG.sem, 16)
                sG.n += 16
                gpsimd.wait_ge(sP[0].sem, sP[0].n)
                gpsimd.wait_ge(sP[1].sem, sP[1].n)
                gpsimd.dma_start(out=dbg["qwloc"][:, :], in_=qw_loc.ap()[:, :]).then_inc(sG.sem, 16)
                sG.n += 16
                gpsimd.wait_ge(sV.sem, M["x2done"])
                gpsimd.dma_start(out=dbg["x1"][:, :], in_=xT_sb[:, :]).then_inc(sG.sem, 16)
                sG.n += 16
            block.gpsimd(g_dbg1)

        es_attp.close()
        es_p2b.close()
        es_p2.close()
        es_ln3p = ExitStack()
        ss3 = es_ln3p.enter_context(nc.psum_tensor("ss3", [1, R], F32))
        qq3 = es_ln3p.enter_context(nc.psum_tensor("qq3", [1, R], F32))

        T7B = sT.n
        for c in range(NCH):
            M[f"st3_{c}"] = T7B + c + 1

        def a7(scalar):
            for c in range(NCH):
                scalar.wait_ge(sV.sem, M[f"xu2_{c}"])
                if c >= 2:
                    scalar.wait_ge(sT.sem, M[f"st3_{c - 2}"])
                sS.inc(scalar.activation(out=ring2(sq_sb, c, R),
                                         in_=xT_sb[:, c * R:(c + 1) * R],
                                         func=AF.Square))
                M[f"sq3_{c}"] = sS.n
        block.scalar(a7)

        def v13(vector):
            for c in range(NCH):
                if c >= 2:
                    vector.wait_ge(sT.sem, M[f"st3_{c - 2}"])
                sV.inc(vector.tensor_copy(ring2(xb_sb, c, R),
                                          xT_sb[:, c * R:(c + 1) * R]))
                M[f"cast3_{c}"] = sV.n
        block.vector(v13)

        def t7(tensor):
            assert sT.n == T7B
            for c in range(NCH):
                tensor.wait_ge(sV.sem, M[f"cast3_{c}"])
                tensor.wait_ge(sS.sem, M[f"sq3_{c}"])
                xb = ring2(xb_sb, c, R)
                sq = ring2(sq_sb, c, R)
                st, sp = (c == 0), (c == NCH - 1)
                tensor.matmul(ss3[0:1, 0:512], ones_cb[:, 0:1], xb[:, 0:512],
                              start=st, stop=sp)
                tensor.matmul(ss3[0:1, 512:1024], ones_cb[:, 0:1], xb[:, 512:1024],
                              start=st, stop=sp)
                tensor.matmul(qq3[0:1, 0:512], ones_cb[:, 0:1], sq[:, 0:512],
                              start=st, stop=sp)
                ins = tensor.matmul(qq3[0:1, 512:1024], ones_cb[:, 0:1],
                                    sq[:, 512:1024], start=st, stop=sp)
                sT.inc(ins)
                assert sT.n == M[f"st3_{c}"]
        block.tensor(t7)

        def v14(vector):
            vector.wait_ge(sT.sem, M[f"st3_{NCH - 1}"])
            rows_part1(vector, ss3, qq3, "rows3")
        block.vector(v14)

        def a8(scalar):
            scalar.wait_ge(sV.sem, M["rows3"])
            sS.inc(scalar.sqrt(row(ROW_RSQ3), row(ROW_RSQ)))
            M["sqrt3"] = sS.n
        block.scalar(a8)

        def v14b(vector):
            vector.wait_ge(sS.sem, M["sqrt3"])
            sV.inc(vector.scalar_tensor_tensor(row(ROW_B3), row(ROW_M), NEG1,
                                               row(ROW_RSQ3), AL.mult, AL.mult))
            M["rows3b"] = sV.n
        block.vector(v14b)

        es_ln3p.close()

        qw12 = qwgv[:, 0:QW1].rearrange("r (u p h) -> r u p h", u=2, p=128)
        qw22 = qwgv[:, QW1:2 * QW1].rearrange("r (u p h) -> r u p h", u=2, p=128)
        qw3v = qwgv[:, 2 * QW1:3 * QW1].rearrange("r (hc p o) -> r hc p o",
                                                  hc=HUC, p=128)
        qw12p = qw12.transpose([2, 0, 1, 3])
        qw22p = qw22.transpose([2, 0, 1, 3])
        es_qu.close()
        XUF = {}
        for rh in range(2):
            es_rh = ExitStack()
            hid_sb = es_rh.enter_context(
                nc.sbuf_tensor(f"hid_{rh}", [128, HUC * 512], BF16))
            q3sl = es_rh.enter_context(
                nc.sbuf_tensor(f"q3sl_{rh}", [128, 2 * HUC * 128], BF16))
            es_rha = ExitStack()
            lnx3_sb = es_rha.enter_context(
                nc.sbuf_tensor(f"lnx3_{rh}", [128, NCH * 512], BF16))
            sg_sb = es_rha.enter_context(
                nc.sbuf_tensor(f"sg_{rh}", [128, 2 * 512], BF16))
            xe_sb = es_rha.enter_context(
                nc.sbuf_tensor(f"xe_{rh}", [128, 2 * 512], BF16))
            q1sl = es_rha.enter_context(
                nc.sbuf_tensor(f"q1sl_{rh}", [128, 2 * NCH * 128], BF16))
            q2sl = es_rha.enter_context(
                nc.sbuf_tensor(f"q2sl_{rh}", [128, 2 * NCH * 128], BF16))
            es_rhp = ExitStack()
            a3b = es_rhp.enter_context(nc.psum_tensor(f"a3b_{rh}", [128, 512], F32))
            b3b = es_rhp.enter_context(nc.psum_tensor(f"b3b_{rh}", [128, 512], F32))
            px1 = [es_rhp.enter_context(nc.psum_tensor(f"px1{i}_{rh}", [128, 512], F32))
                   for i in range(2)]
            px2 = [es_rhp.enter_context(nc.psum_tensor(f"px2{i}_{rh}", [128, 512], F32))
                   for i in range(2)]

            def t8(tensor, rh=rh, a3b=a3b, b3b=b3b):
                tensor.wait_ge(sV.sem, M["rows3b"])
                if rh == 1:
                    tensor.wait_ge(sV.sem, XUF[0])
                tensor.matmul(a3b[:, :], ones_rf[0:1, :], rowh(ROW_RSQ3, rh),
                              start=True, stop=True)
                sT.inc(tensor.matmul(b3b[:, :], ones_rf[0:1, :], rowh(ROW_B3, rh),
                                     start=True, stop=True))
                M[f"bc3_{rh}"] = sT.n
            block.tensor(t8)

            def v15(vector, rh=rh, lnx3_sb=lnx3_sb, a3b=a3b, b3b=b3b):
                vector.wait_ge(sT.sem, M[f"bc3_{rh}"])
                for c in range(NCH):
                    xs = xT_sb[:, c * R + rh * 512: c * R + (rh + 1) * 512]
                    vector.tensor_tensor(ln_tmp[:, 0:512], xs, a3b[:, :], AL.mult)
                    sV.inc(vector.tensor_tensor(
                        lnx3_sb[:, c * 512:(c + 1) * 512], ln_tmp[:, 0:512],
                        b3b[:, :], AL.add))
                    M[f"lnx3_{rh}_{c}"] = sV.n
            block.vector(v15)

            T9B = sT.n
            S9B = sS.n
            V9B = sV.n

            def TX2(hc):
                return T9B + 2 * hc + 2

            def SIG(hc):
                return S9B + hc + 1

            def VH(hc):
                return V9B + hc + 1

            def s6(sync, rh=rh, q1sl=q1sl, q2sl=q2sl):
                for hc in range(HUC):
                    if rh == 0 and hc == 0:
                        sync.wait_ge(sC.sem, M["ag_qw"])
                        sync.wait_ge(sV.sem, M["x2done"])
                    if rh == 1 and hc == 0:
                        sync.wait_ge(sT.sem, M["w3_last_0"])
                    if hc >= 2:
                        sync.wait_ge(sT.sem, TX2(hc - 2))
                    fl = sF[hc % 2]
                    for (sl, qv2) in ((q1sl, qw12p), (q2sl, qw22p)):
                        for u in range(2):
                            sync.dma_start(
                                out=ring2(sl, hc, NCH * 128).rearrange(
                                    "p (r u h) -> p r u h", r=8, u=2
                                )[:, :, u, :],
                                in_=qv2[:, :, u, hc * 128:(hc + 1) * 128],
                            ).then_inc(fl.sem, 16)
                            fl.n += 16
                    M[f"q12_{rh}_{hc}"] = (hc % 2, fl.n)
            block.sync(s6)

            def t9(tensor, rh=rh, q1sl=q1sl, q2sl=q2sl, lnx3_sb=lnx3_sb,
                   px1=px1, px2=px2):
                assert sT.n == T9B
                for hc in range(HUC):
                    _s, _c = M[f"q12_{rh}_{hc}"]
                    tensor.wait_ge(sF[_s].sem, _c)
                    if hc == 0:
                        tensor.wait_ge(sV.sem, M[f"lnx3_{rh}_{NCH - 1}"])
                    if hc >= 2:
                        tensor.wait_ge(sS.sem, SIG(hc - 2))
                        tensor.wait_ge(sV.sem, VH(hc - 2))
                    for (sl, ps) in ((q1sl, px1[hc % 2]), (q2sl, px2[hc % 2])):
                        for dc in range(NCH):
                            lhsT = ring2(sl, hc, NCH * 128)[:, dc * 128:(dc + 1) * 128]
                            ins = tensor.matmul(
                                ps[:, :], lhsT,
                                lnx3_sb[:, dc * 512:(dc + 1) * 512],
                                start=(dc == 0), stop=(dc == NCH - 1))
                        sT.inc(ins)
                    assert sT.n == TX2(hc)
            block.tensor(t9)

            def a9(scalar, rh=rh, sg_sb=sg_sb, px1=px1):
                assert sS.n == S9B
                for hc in range(HUC):
                    scalar.wait_ge(sT.sem, TX2(hc) - 1)
                    sS.inc(scalar.activation(out=ring2(sg_sb, hc, 512),
                                             in_=px1[hc % 2][:, :], func=AF.Sigmoid))
                    assert sS.n == SIG(hc)
            block.scalar(a9)

            def v16(vector, rh=rh, sg_sb=sg_sb, xe_sb=xe_sb, hid_sb=hid_sb,
                    px2=px2):
                assert sV.n == V9B
                for hc in range(HUC):
                    vector.wait_ge(sT.sem, TX2(hc))
                    vector.wait_ge(sS.sem, SIG(hc))
                    vector.tensor_copy(ring2(xe_sb, hc, 512), px2[hc % 2][:, :])
                    sV.inc(vector.tensor_tensor(
                        hid_sb[:, hc * 512:(hc + 1) * 512], ring2(sg_sb, hc, 512),
                        ring2(xe_sb, hc, 512), AL.mult))
                    assert sV.n == VH(hc)
            block.vector(v16)

            es_rhp.close()
            es_rha.close()
            es_w3s = ExitStack()
            y_sb = es_w3s.enter_context(
                nc.sbuf_tensor(f"y_sb_{rh}", [128, NCH * 512], BF16))
            es_w3p = ExitStack()
            py = [es_w3p.enter_context(nc.psum_tensor(f"py{i}_{rh}", [128, 512], F32))
                  for i in range(2)]
            ss4 = es_w3p.enter_context(nc.psum_tensor(f"ss4_{rh}", [1, 512], F32))
            qq4 = es_w3p.enter_context(nc.psum_tensor(f"qq4_{rh}", [1, 512], F32))

            T10B = sT.n

            def TY(oc):
                return T10B + oc + 1

            V17B = sV.n

            def EV4(oc):
                return V17B + oc + 1

            def s7(sync, rh=rh, q3sl=q3sl):
                for oc in range(NCH):
                    if oc == 0 and rh == 1:
                        sync.wait_ge(sT.sem, M["w3_last_0"])
                    if oc >= 2:
                        sync.wait_ge(sT.sem, TY(oc - 2))
                    src = qw3v[oc // 2:oc // 2 + 1, :, :,
                               (oc % 2) * 128:(oc % 2) * 128 + 128]
                    q3l = sQ3[oc % 2]
                    for hh in range(2):
                        sync.dma_start(
                            out=ring2(q3sl, oc, HUC * 128)[
                                :, hh * 22 * 128:(hh + 1) * 22 * 128].rearrange(
                                "p (hc o) -> p hc o", hc=22),
                            in_=src[:, hh * 22:(hh + 1) * 22].squeeze(
                                0).transpose([1, 0, 2]),
                        ).then_inc(q3l.sem, 16)
                        q3l.n += 16
                    M[f"q3_{rh}_{oc}"] = (oc % 2, q3l.n)
            block.sync(s7)

            def t10(tensor, rh=rh, q3sl=q3sl, hid_sb=hid_sb, py=py):
                assert sT.n == T10B
                for oc in range(NCH):
                    _s, _c = M[f"q3_{rh}_{oc}"]
                    tensor.wait_ge(sQ3[_s].sem, _c)
                    if oc == 0:
                        tensor.wait_ge(sV.sem, VH(HUC - 1))
                    if oc >= 2:
                        tensor.wait_ge(sV.sem, EV4(oc - 2))
                    for hc in range(HUC):
                        lhsT = ring2(q3sl, oc, HUC * 128)[:, hc * 128:(hc + 1) * 128]
                        ins = tensor.matmul(py[oc % 2][:, :], lhsT,
                                            hid_sb[:, hc * 512:(hc + 1) * 512],
                                            start=(hc == 0), stop=(hc == HUC - 1))
                    sT.inc(ins)
                    assert sT.n == TY(oc)
                M[f"w3_last_{rh}"] = sT.n
            block.tensor(t10)

            if debug and rh == 0:
                def g_dbg2(gpsimd, hid_sb=hid_sb):
                    gpsimd.wait_ge(sV.sem, VH(HUC - 1))
                    gpsimd.dma_start(out=dbg["hid"][:, :], in_=hid_sb[:, :]).then_inc(sG.sem, 16)
                    sG.n += 16
                block.gpsimd(g_dbg2)

            def v17(vector, rh=rh, y_sb=y_sb, py=py):
                assert sV.n == V17B
                for oc in range(NCH):
                    vector.wait_ge(sT.sem, TY(oc))
                    sV.inc(vector.tensor_copy(y_sb[:, oc * 512:(oc + 1) * 512],
                                              py[oc % 2][:, :]))
                    assert sV.n == EV4(oc)
            block.vector(v17)

            if debug and rh == 0:
                def g_dbg2b(gpsimd, q3sl=q3sl, c0=32, c1=32):
                    gpsimd.wait_ge(sQ3[0].sem, c0)
                    gpsimd.wait_ge(sQ3[1].sem, c1)
                    gpsimd.dma_start(out=dbg["q3sl"][:, :], in_=q3sl[:, :]).then_inc(sG.sem, 16)
                    sG.n += 16
                block.gpsimd(g_dbg2b)

                def g_dbg3(gpsimd, y_sb=y_sb):
                    gpsimd.wait_ge(sV.sem, EV4(NCH - 1))
                    gpsimd.dma_start(out=dbg["y"][:, :], in_=y_sb[:, :]).then_inc(sG.sem, 16)
                    sG.n += 16
                block.gpsimd(g_dbg3)

            T11B = sT.n
            for c in range(NCH):
                M[f"st4_{rh}_{c}"] = T11B + c + 1

            def a10(scalar, rh=rh, y_sb=y_sb):
                for c in range(NCH):
                    scalar.wait_ge(sV.sem, EV4(c))
                    if c >= 2:
                        scalar.wait_ge(sT.sem, M[f"st4_{rh}_{c - 2}"])
                    sS.inc(scalar.activation(out=ring2(sq_sb, c, R)[:, 0:512],
                                             in_=y_sb[:, c * 512:(c + 1) * 512],
                                             func=AF.Square))
                    M[f"sq4_{rh}_{c}"] = sS.n
            block.scalar(a10)

            def t11(tensor, rh=rh, y_sb=y_sb, ss4=ss4, qq4=qq4):
                assert sT.n == T11B
                for c in range(NCH):
                    tensor.wait_ge(sV.sem, EV4(c))
                    tensor.wait_ge(sS.sem, M[f"sq4_{rh}_{c}"])
                    st, sp = (c == 0), (c == NCH - 1)
                    tensor.matmul(ss4[0:1, :], ones_cb[:, 0:1],
                                  y_sb[:, c * 512:(c + 1) * 512],
                                  start=st, stop=sp)
                    ins = tensor.matmul(qq4[0:1, :], ones_cb[:, 0:1],
                                        ring2(sq_sb, c, R)[:, 0:512],
                                        start=st, stop=sp)
                    sT.inc(ins)
                    assert sT.n == M[f"st4_{rh}_{c}"]
            block.tensor(t11)

            def v18(vector, rh=rh, ss4=ss4, qq4=qq4):
                vector.wait_ge(sT.sem, M[f"st4_{rh}_{NCH - 1}"])
                rows_part1_w(vector, ss4, qq4, f"rows4_{rh}", 512)
            block.vector(v18)

            def a11(scalar, rh=rh):
                scalar.wait_ge(sV.sem, M[f"rows4_{rh}"])
                sS.inc(scalar.sqrt(rows[0:1, ROW_RSQ * R:ROW_RSQ * R + 512],
                                   rows[0:1, ROW_RSQ * R:ROW_RSQ * R + 512]))
                M[f"sqrt4_{rh}"] = sS.n
            block.scalar(a11)

            es_bc4 = ExitStack()
            m4b = es_bc4.enter_context(nc.psum_tensor(f"m4b_{rh}", [128, 512], F32))
            a4b = es_bc4.enter_context(nc.psum_tensor(f"a4b_{rh}", [128, 512], F32))

            def t12(tensor, rh=rh, m4b=m4b, a4b=a4b):
                tensor.wait_ge(sS.sem, M[f"sqrt4_{rh}"])
                tensor.matmul(m4b[:, :], ones_rf[0:1, :],
                              rows[0:1, ROW_M * R:ROW_M * R + 512],
                              start=True, stop=True)
                sT.inc(tensor.matmul(a4b[:, :], ones_rf[0:1, :],
                                     rows[0:1, ROW_RSQ * R:ROW_RSQ * R + 512],
                                     start=True, stop=True))
                M[f"bc4_{rh}"] = sT.n
            block.tensor(t12)

            def v19(vector, rh=rh, y_sb=y_sb, m4b=m4b, a4b=a4b):
                vector.wait_ge(sT.sem, M[f"bc4_{rh}"])
                for c in range(NCH):
                    ys = y_sb[:, c * 512:(c + 1) * 512]
                    xs = xT_sb[:, c * R + rh * 512: c * R + (rh + 1) * 512]
                    vector.tensor_tensor(ln_tmp[:, 0:512], ys, m4b[:, :],
                                         AL.subtract)
                    vector.tensor_tensor(ln_tmp[:, 0:512], ln_tmp[:, 0:512],
                                         a4b[:, :], AL.mult)
                    sV.inc(vector.tensor_tensor(xs, xs, ln_tmp[:, 0:512], AL.add))
                XUF[rh] = sV.n
            block.vector(v19)

            es_bc4.close()
            es_w3p.close()
            es_w3s.close()
            es_rh.close()

        def g5(gpsimd):
            gpsimd.wait_ge(sV.sem, XUF[1])
            gpsimd.dma_start(
                out=yT.rearrange("(c p) r -> p c r", p=128),
                in_=xT_sb[:, :].rearrange("p (c r) -> p c r", c=NCH),
            ).then_inc(sG.sem, 16)
            sG.n += 16
            gpsimd.wait_ge(sG.sem, sG.n)
        block.gpsimd(g5)

        es_qu.close()

    return nc


_NC_CACHE = {}
_LAST_RES = None


def _get_nc():
    if "nc" not in _NC_CACHE:
        _NC_CACHE["nc"] = build_program()
    return _NC_CACHE["nc"]


def kernel(L0_attn_query, L0_attn_key, L0_attn_value, L0_attn_Wqkv,
           L0_attn_Wproj, L0_attn_bproj, L0_ffn_w1, L0_ffn_w2, L0_ffn_w3,
           **_unused):
    x = np.asarray(L0_attn_query)
    Wqkv = np.asarray(L0_attn_Wqkv)
    Wproj = np.asarray(L0_attn_Wproj)
    bproj = np.asarray(L0_attn_bproj)
    w1 = np.asarray(L0_ffn_w1)
    w2 = np.asarray(L0_ffn_w2)
    w3 = np.asarray(L0_ffn_w3)

    Wq3 = Wqkv.reshape(NH, 3, 128, H)
    WK = np.ascontiguousarray(Wq3[:, 1].reshape(NH * 128, H).T)
    WQ = np.ascontiguousarray(Wq3[:, 0].reshape(NH * 128, H).T)
    wkqT = np.ascontiguousarray(np.concatenate([WK, WQ], axis=1))
    wvT = np.ascontiguousarray(Wq3[:, 2].reshape(NH * 128, H).T)
    wpT = np.ascontiguousarray(Wproj.T)
    w1T = np.ascontiguousarray(w1.T)
    w2T = np.ascontiguousarray(w2.T)
    w3T = np.ascontiguousarray(w3.T)

    in_maps = []
    for c in range(NCORES):
        b, half = c // 2, c % 2
        s0 = half * R
        ctl = np.zeros((1, 128), dtype=np.uint32)
        ctl[0, 0] = half
        in_maps.append({
            "xT": np.ascontiguousarray(x[b, s0:s0 + R, :].T),
            "wkqT": wkqT,
            "wvT": wvT,
            "wpT": wpT,
            "bpj": np.ascontiguousarray(bproj.reshape(1, H)),
            "w1s": np.ascontiguousarray(w1T[c * 256:(c + 1) * 256, :]),
            "w2s": np.ascontiguousarray(w2T[c * 256:(c + 1) * 256, :]),
            "w3s": np.ascontiguousarray(w3T[:, c * 256:(c + 1) * 256]),
            "ctl": ctl,
        })

    nc = _get_nc()
    res = run_bass_kernel_spmd(nc, in_maps, core_ids=list(range(NCORES)))
    global _LAST_RES
    _LAST_RES = res
    out = np.empty((4, SQ, H), dtype=np.float32)
    for c in range(NCORES):
        b, half = c // 2, c % 2
        out[b, half * R:(half + 1) * R, :] = res.results[c]["yT"].T
    return out
